# version 05
# baseline (speedup 1.0000x reference)
import math
import os
from contextlib import ExitStack

import numpy as np
import ml_dtypes

import concourse.bass as bass
import concourse.mybir as mybir
from concourse.bass_utils import run_bass_kernel_spmd

F32 = mybir.dt.float32
BF16 = mybir.dt.bfloat16
AF = mybir.ActivationFunctionType
ALU = mybir.AluOpType

NCORES = 8
D = 1024
T = 4096
HALO = 4
OWN = T // 2
L = HALO + OWN
NH = 8
HD = 64
LAMBDA_INIT = 0.8 - 0.6 * math.exp(-0.3 * (4 - 1))
EPS = 1e-5
ATT_BIAS = -12.0
PAIRS = [[0, 1], [2, 3], [4, 5], [6, 7]]
ENGS = ("sync", "gpsimd", "vector", "scalar", "tensor")
BF = ml_dtypes.bfloat16


def chunks_of(total, step=512):
    out, pos = [], 0
    r = total % step
    if r:
        out.append((0, r))
        pos = r
    while pos < total:
        out.append((pos, step))
        pos += step
    return out


GRID_L = [(0, 512), (512, 512), (1024, 512), (1536, 512), (1540, 512)]
GRID_OWN = [(0, 512), (512, 512), (1024, 512), (1536, 512)]


class Sem:
    def __init__(self, handle):
        self.h = handle
        self.n = 0

    def bump(self, k=1):
        self.n += k
        return self.n


class Prog:

    def __init__(self, nc, ctx):
        self.nc = nc
        self.ctx = ctx
        self.ops = {e: [] for e in ENGS}
        self._semid = 0

    def sem(self, name=None):
        self._semid += 1
        nm = f"{name}_{self._semid}" if name else f"s{self._semid}"
        h = self.ctx.enter_context(self.nc.semaphore(nm))
        return Sem(h)

    def raw(self, eng, fn):
        self.ops[eng].append(fn)

    def op(self, eng, build, wait=(), inc=()):
        waits = [(s.h, v) for s, v in wait]
        incs = []
        for s, k in inc:
            s.bump(k)
            incs.append((s.h, k))

        def f(e):
            for h, v in waits:
                e.wait_ge(h, v)
            ins = build(e)
            if incs:
                assert ins is not None
                for h, k in incs:
                    ins.then_inc(h, k)
        self.ops[eng].append(f)

    def play(self):
        with self.nc.Block() as block:
            @block.sync
            def _(e):
                for f in self.ops["sync"]:
                    f(e)

            @block.gpsimd
            def _(e):
                for f in self.ops["gpsimd"]:
                    f(e)

            @block.vector
            def _(e):
                for f in self.ops["vector"]:
                    f(e)

            @block.scalar
            def _(e):
                for f in self.ops["scalar"]:
                    f(e)

            @block.tensor
            def _(e):
                for f in self.ops["tensor"]:
                    f(e)


def dma_tiles(P, eng, sem, dram, tiles, n, wait=()):
    dv = dram.rearrange("(a p) t -> a p t", p=128)
    waits = [(s.h, v) for s, v in wait]

    def f(e):
        for h, v in waits:
            e.wait_ge(h, v)
        for i in range(n):
            e.dma_start(out=tiles[i][:, :], in_=dv[i]).then_inc(sem.h, 16)
    sem.bump(16 * n)
    P.raw(eng, f)


class Common:
    def __init__(self, nc, ctx, P):
        sb = lambda nm, s, d: ctx.enter_context(nc.sbuf_tensor(nm, s, d))
        self.ones128f = sb("ones128f", [128, 1], F32)
        self.ones128b = sb("ones128b", [128, 1], BF16)
        self.ones1f = sb("ones1f", [1, 128], F32)
        self.consts = sb("constsA", [128, 12], F32)
        self.s_init = P.sem("s_init")
        cvals = [EPS, -0.5, ATT_BIAS, 0.125, LAMBDA_INIT, 0.0, math.pi / 2, 0.5, math.log(1.0 - LAMBDA_INIT)]
        self.cidx = {v: i for i, v in enumerate(cvals)}

        def v_init(v):
            v.memset(self.ones128f[:, :], 1.0)
            v.memset(self.ones128b[:, :], 1.0)
            v.memset(self.ones1f[:, :], 1.0)
            ins = None
            for val, i in self.cidx.items():
                ins = v.memset(self.consts[:, i:i + 1], val)
            return ins
        P.op("vector", v_init, inc=[(self.s_init, 1)])

    def c(self, val):
        i = self.cidx[val]
        return self.consts[:, i:i + 1]


def emit_ln(P, C, x_tiles, out_tiles, chs, scr, pre_wait, nft=8, sems=None):
    if sems is None:
        sems = tuple(P.sem() for _ in range(6))
    s_sq, s_st, s_row, s_rs, s_bc, s_app = sems
    sq, st_ps, bc_ps = scr["sq"], scr["st_ps"], scr["bc_ps"]
    r_mu, r_mu2, r_var, r_rstd = scr["rows"]
    inv_d = 1.0 / (128 * nft)
    app_vals = []
    for k, (off, w) in enumerate(chs):
        pw = list(pre_wait) if k == 0 else []

        def b_sq(s, off=off, w=w):
            ins = None
            for i in range(nft):
                ins = s.activation(sq[i][:, :w], x_tiles[i][:, off:off + w],
                                   AF.Square)
            return ins
        gate = [(s_st, s_st.n), (s_app, s_app.n)]
        P.op("scalar", b_sq, wait=pw + gate, inc=[(s_sq, 1)])

        def b_st(t, off=off, w=w):
            ins = None
            for i in range(nft):
                ins = t.matmul(st_ps[0:1, :w], C.ones128f[:, :],
                               x_tiles[i][:, off:off + w], start=(i == 0),
                               stop=(i == nft - 1), tile_position=(0, 0))
            return ins
        P.op("tensor", b_st, wait=pw + [(s_row, s_row.n)])

        def b_st2(t, w=w):
            ins = None
            for i in range(nft):
                ins = t.matmul(st_ps[32:33, :w], C.ones128b[:, :], sq[i][:, :w],
                               start=(i == 0), stop=(i == nft - 1),
                               tile_position=(0, 32))
            return ins
        P.op("tensor", b_st2, wait=[(s_sq, s_sq.n)], inc=[(s_st, 1)])

        def b_row(v, w=w):
            v.tensor_scalar_mul(r_mu[:, :w], st_ps[0:1, :w], inv_d)
            v.tensor_mul(r_mu2[:, :w], r_mu[:, :w], r_mu[:, :w])
            return v.scalar_tensor_tensor(r_var[:, :w], st_ps[32:33, :w], inv_d,
                                          r_mu2[:, :w], ALU.mult, ALU.subtract)
        P.op("vector", b_row, wait=[(s_st, s_st.n), (s_bc, s_bc.n)],
             inc=[(s_row, 1)])

        def b_rs(s, w=w):
            s.activation(r_rstd[:, :w], r_var[:, :w], AF.Ln,
                         bias=C.c(EPS)[0:1, :])
            return s.activation(r_rstd[:, :w], r_rstd[:, :w], AF.Exp,
                                scale=C.c(-0.5)[0:1, :])
        P.op("scalar", b_rs, wait=[(s_row, s_row.n)], inc=[(s_rs, 1)])

        def b_bc(t, w=w):
            t.matmul(bc_ps[0][:, :w], C.ones1f[:, :], r_mu[:, :w], start=True,
                     stop=True)
            return t.matmul(bc_ps[1][:, :w], C.ones1f[:, :], r_rstd[:, :w],
                            start=True, stop=True)
        P.op("tensor", b_bc, wait=[(s_rs, s_rs.n), (s_app, s_app.n)],
             inc=[(s_bc, 1)])

        def b_app(v, off=off, w=w):
            ins = None
            for i in range(nft):
                v.tensor_sub(sq[i][:, :w], x_tiles[i][:, off:off + w],
                             bc_ps[0][:, :w])
                ins = v.tensor_mul(out_tiles[i][:, off:off + w], sq[i][:, :w],
                                   bc_ps[1][:, :w])
            return ins
        P.op("vector", b_app, wait=[(s_bc, s_bc.n)], inc=[(s_app, 1)])
        app_vals.append(s_app.n)
    return s_app, app_vals



def build_B():
    nc = bass.Bass("TRN2", target_bir_lowering=False)
    ynf = nc.dram_tensor("ynf", [2048, L], BF16, kind="ExternalInput")
    cw = nc.dram_tensor("cw", [2048, D], F32, kind="ExternalInput")
    x_in = nc.dram_tensor("x", [D, L], F32, kind="ExternalInput")
    wr = nc.dram_tensor("wr", [D, D], F32, kind="ExternalInput")
    wk = nc.dram_tensor("wk", [D, D], F32, kind="ExternalInput")
    wv = nc.dram_tensor("wv", [D, D], F32, kind="ExternalInput")
    mur = nc.dram_tensor("mur", [128, 8], F32, kind="ExternalInput")
    muk = nc.dram_tensor("muk", [128, 8], F32, kind="ExternalInput")
    xp_out = nc.dram_tensor("xp", [D, L], F32, kind="ExternalOutput")

    with ExitStack() as ctx:
        P = Prog(nc, ctx)
        C = Common(nc, ctx, P)
        sb = lambda nm, s, d: ctx.enter_context(nc.sbuf_tensor(nm, s, d))
        psum = lambda nm, s: ctx.enter_context(nc.psum_tensor(nm, s, F32))

        x_sb = [sb(f"x{i}", [128, L], F32) for i in range(8)]
        sq = [sb(f"sq{i}", [128, 512], BF16) for i in range(8)]
        rows = [sb(f"row{i}", [1, 512], F32) for i in range(4)]
        st_ps = psum("st_ps", [64, 512])
        bc_ps = [psum("bc0", [128, 512]), psum("bc1", [128, 512])]
        mm_ps = [psum(f"mm{i}", [128, 512]) for i in range(4)]

        with ExitStack() as c1:
            sb1 = lambda nm, s, d: c1.enter_context(nc.sbuf_tensor(nm, s, d))
            ynf_sb = [sb1(f"ynf{i}", [128, L], BF16) for i in range(16)]
            cw_sb = [sb1(f"cwt{i}", [128, D], BF16) for i in range(16)]
            xin_st = [sb1(f"xis{i}", [128, 512], F32) for i in range(8)]
            s_ld1 = P.sem("s_ld1")
            s_lx = P.sem("s_lx")
            s_cw = P.sem("s_cw")
            s_res = P.sem("s_res")
            dma_tiles(P, "gpsimd", s_ld1, ynf, ynf_sb, 16)
            dma_tiles(P, "gpsimd", s_ld1, cw, cw_sb, 16)

            xiv = x_in.rearrange("(a p) t -> a p t", p=128)
            nblk = 0
            for k, (off, w) in enumerate(GRID_L):
                def b_lx(sy, off=off, w=w, k=k, resn=s_res.n):
                    if k > 0:
                        sy.wait_ge(s_res.h, resn)
                    for i in range(8):
                        sy.dma_start(out=xin_st[i][:, :w],
                                     in_=xiv[i][:, off:off + w]
                                     ).then_inc(s_lx.h, 16)
                s_lx.bump(16 * 8)
                P.raw("sync", b_lx)
                for dm in range(8):
                    bank = mm_ps[nblk % 4]

                    def b_cw(t, dm=dm, off=off, w=w, bank=bank):
                        ins = None
                        for d2 in range(16):
                            ins = t.matmul(bank[:, :w],
                                           cw_sb[d2][:, dm * 128:(dm + 1) * 128],
                                           ynf_sb[d2][:, off:off + w],
                                           start=(d2 == 0), stop=(d2 == 15))
                        return ins
                    wt = [(s_ld1, s_ld1.n)] if nblk == 0 else []
                    if nblk >= 4:
                        wt.append((s_res, nblk - 3))
                    P.op("tensor", b_cw, wait=wt, inc=[(s_cw, 1)])

                    def b_res(v, dm=dm, off=off, w=w, bank=bank):
                        return v.tensor_add(x_sb[dm][:, off:off + w],
                                            xin_st[dm][:, :w],
                                            bank[:, :w])
                    P.op("vector", b_res, wait=[(s_cw, s_cw.n), (s_lx, s_lx.n)],
                         inc=[(s_res, 1)])
                    nblk += 1
            res_done = (s_res, s_res.n)
            cw_done = (s_cw, s_cw.n)
            P.op("gpsimd", lambda g: None, wait=[cw_done, res_done])

        emit_ln2_ffn0(nc, ctx, P, C, dict(x_sb=x_sb, sq=sq, rows=rows,
                                          st_ps=st_ps, bc_ps=bc_ps,
                                          mm_ps=mm_ps),
                      dict(wr=wr, wk=wk, wv=wv, mur=mur, muk=muk,
                           xp_out=xp_out), res_done)
        P.play()
    return nc


def emit_ln2_ffn0(nc, ctx, P, C, scr, io, res_done):
    x_sb, sq, rows = scr["x_sb"], scr["sq"], scr["rows"]
    st_ps, bc_ps, mm_ps = scr["st_ps"], scr["bc_ps"], scr["mm_ps"]
    wr, wk, wv, mur, muk = io["wr"], io["wk"], io["wv"], io["mur"], io["muk"]
    xp_out = io["xp_out"]
    if True:
        with ExitStack() as c2:
            sb2 = lambda nm, s, d: c2.enter_context(nc.sbuf_tensor('F_' + nm, s, d))
            wr_sb = [sb2(f"wrt{i}", [128, D], BF16) for i in range(8)]
            wk_sb = [sb2(f"wkt{i}", [128, D], BF16) for i in range(8)]
            wv_sb = [sb2(f"wvt{i}", [128, D], BF16) for i in range(8)]
            mu_sb = sb2("mu", [128, 16], F32)
            lny = [sb2(f"lny{i}", [128, L], BF16) for i in range(8)]
            xm_sb = [sb2(f"xm{i}", [128, 512], BF16) for i in range(8)]
            xr_sb = xm_sb
            xk_sb = xm_sb
            r_sb = [sb2(f"r{i}", [128, 512], BF16) for i in range(8)]
            ka_sb = [sb2(f"ka{i}", [128, 512], BF16) for i in range(8)]
            tmp_sb = sb2("tmpv", [128, 512], F32)
            xp_st = [sb2(f"xps{i}", [128, 512], F32) for i in range(8)]
            s_ld2 = P.sem("s_ld2")
            dma_tiles(P, "gpsimd", s_ld2, wr, wr_sb, 8)
            dma_tiles(P, "gpsimd", s_ld2, wk, wk_sb, 8)
            dma_tiles(P, "gpsimd", s_ld2, wv, wv_sb, 8)

            def b_mu(g):
                g.dma_start(out=mu_sb[:, 0:8], in_=mur[:, :]).then_inc(s_ld2.h, 16)
                g.dma_start(out=mu_sb[:, 8:16], in_=muk[:, :]).then_inc(s_ld2.h, 16)
            s_ld2.bump(32)
            P.raw("gpsimd", b_mu)

            pre = [res_done, (C.s_init, 1)]
            s_app, app_vals = emit_ln(P, C, x_sb, lny, GRID_L,
                                      dict(sq=sq, st_ps=st_ps, bc_ps=bc_ps,
                                           rows=rows), pre)

            s_mix = P.sem("s_mix")
            s_r = P.sem("s_r")
            s_k = P.sem("s_k")
            s_ka = P.sem("s_ka")
            s_mmf = P.sem("s_mmf")
            s_out = P.sem("s_out")
            s_sto = P.sem("s_sto")

            for k, (off, w) in enumerate(GRID_L):
                def mk_mix(cols):
                    def b_mix(v, off=off, w=w, cols=cols):
                        ins = None
                        for i in range(8):
                            muc = mu_sb[:, cols + i:cols + i + 1]
                            if off == 0:
                                v.tensor_copy(xm_sb[i][:, 0:1], lny[i][:, 0:1])
                                o, wo = 1, w - 1
                            else:
                                o, wo = off, w
                            cur = lny[i][:, o:o + wo]
                            prv = lny[i][:, o - 1:o - 1 + wo]
                            dm_ = xm_sb[i][:, o - off:o - off + wo]
                            v.tensor_sub(dm_, cur, prv)
                            ins = v.scalar_tensor_tensor(dm_, dm_, muc, prv,
                                                         ALU.mult, ALU.add)
                        return ins
                    return b_mix
                P.op("vector", mk_mix(0),
                     wait=[(s_app, app_vals[k]), (s_ld2, s_ld2.n),
                           (s_mmf, s_mmf.n)],
                     inc=[(s_mix, 1)])

                for dt in range(8):
                    bank = mm_ps[dt % 2]

                    def b_rmm(t, dt=dt, w=w, bank=bank):
                        ins = None
                        for ct in range(8):
                            ins = t.matmul(bank[:, :w],
                                           wr_sb[ct][:, dt * 128:(dt + 1) * 128],
                                           xr_sb[ct][:, :w],
                                           start=(ct == 0), stop=(ct == 7))
                        return ins
                    wt = [(s_mix, s_mix.n)] if dt == 0 else []
                    if k == 0 and dt == 0:
                        wt.append(res_done)
                    if s_r.n >= 2:
                        wt.append((s_r, s_r.n - 1))
                    P.op("tensor", b_rmm, wait=wt, inc=[(s_mmf, 1)])

                    def b_rev(s, dt=dt, w=w, bank=bank):
                        return s.activation(r_sb[dt][:, :w], bank[:, :w],
                                            AF.Sigmoid)
                    wt = [(s_mmf, s_mmf.n)]
                    if k > 0:
                        wt.append((s_out, 8 * (k - 1) + dt + 1))
                    P.op("scalar", b_rev, wait=wt, inc=[(s_r, 1)])

                P.op("vector", mk_mix(8), wait=[(s_mmf, s_mmf.n)],
                     inc=[(s_mix, 1)])
                for dt in range(8):
                    bank = mm_ps[2 + dt % 2]

                    def b_kmm(t, dt=dt, w=w, bank=bank):
                        ins = None
                        for ct in range(8):
                            ins = t.matmul(bank[:, :w],
                                           wk_sb[ct][:, dt * 128:(dt + 1) * 128],
                                           xk_sb[ct][:, :w],
                                           start=(ct == 0), stop=(ct == 7))
                        return ins
                    wt = [(s_mix, s_mix.n)] if dt == 0 else []
                    if s_k.n >= 2:
                        wt.append((s_k, s_k.n - 1))
                    P.op("tensor", b_kmm, wait=wt, inc=[(s_mmf, 1)])

                    def b_kev(s, dt=dt, w=w, bank=bank):
                        return s.activation(ka_sb[dt][:, :w], bank[:, :w],
                                            AF.Relu)
                    P.op("scalar", b_kev, wait=[(s_mmf, s_mmf.n)],
                         inc=[(s_k, 1)])

                def b_kasq(v, w=w):
                    ins = None
                    for dt in range(8):
                        ins = v.tensor_mul(ka_sb[dt][:, :w], ka_sb[dt][:, :w],
                                           ka_sb[dt][:, :w])
                    return ins
                P.op("vector", b_kasq, wait=[(s_k, s_k.n)], inc=[(s_ka, 1)])

                for dt in range(8):
                    bank = mm_ps[dt % 2]

                    def b_vmm(t, dt=dt, w=w, bank=bank):
                        ins = None
                        for ct in range(8):
                            ins = t.matmul(bank[:, :w],
                                           wv_sb[ct][:, dt * 128:(dt + 1) * 128],
                                           ka_sb[ct][:, :w],
                                           start=(ct == 0), stop=(ct == 7))
                        return ins
                    wt = [(s_ka, s_ka.n), (s_r, s_r.n)] if dt == 0 else []
                    if dt >= 2:
                        wt.append((s_out, s_out.n - 1))
                    P.op("tensor", b_vmm, wait=wt, inc=[(s_mmf, 1)])

                    def b_vout(v, dt=dt, off=off, w=w, bank=bank):
                        v.tensor_mul(tmp_sb[:, :w], r_sb[dt][:, :w],
                                     bank[:, :w])
                        return v.tensor_add(xp_st[dt][:, :w],
                                            x_sb[dt][:, off:off + w],
                                            tmp_sb[:, :w])
                    wt = [(s_mmf, s_mmf.n)]
                    if k > 0:
                        wt.append((s_sto, 16 * (8 * (k - 1) + dt + 1)))
                    P.op("vector", b_vout, wait=wt, inc=[(s_out, 1)])

                xv = xp_out.rearrange("(a p) t -> a p t", p=128)
                wt_sto = (s_out.h, s_out.n)

                def b_sto(sy, off=off, w=w, wt=wt_sto, xv=xv):
                    sy.wait_ge(*wt)
                    for i in range(8):
                        sy.dma_start(out=xv[i][:, off:off + w],
                                     in_=xp_st[i][:, :w]
                                     ).then_inc(s_sto.h, 16)
                s_sto.bump(16 * 8)
                P.raw("sync", b_sto)

            P.op("sync", lambda e: None, wait=[(s_sto, s_sto.n)])
            return dict(s_sto=s_sto, s_out=s_out, s_mmf=s_mmf, s_app=s_app,
                        s_r=s_r, s_k=s_k, s_ka=s_ka, s_mix=s_mix)



def own_store_range(off, w):
    own_lo = max(0, off - HALO)
    src_lo = own_lo - (off - HALO)
    return src_lo, own_lo, w - src_lo


def declare_C(nc, kind_x="ExternalInput", kind_out="ExternalOutput",
              xname="xc", skip_x=False):
    return dict(
        x_in=None if skip_x else nc.dram_tensor(xname, [D, L], F32,
                                                kind=kind_x),
        win=nc.dram_tensor("win", [D, 4096], F32, kind="ExternalInput"),
        convw=nc.dram_tensor("convw", [128, 64], F32, kind="ExternalInput"),
        convb=nc.dram_tensor("convb", [128, 16], F32, kind="ExternalInput"),
        wB=nc.dram_tensor("wB", [2048, 16], F32, kind="ExternalInput"),
        wC=nc.dram_tensor("wC", [16, 2048], F32, kind="ExternalInput"),
        y0_out=nc.dram_tensor("y0", [2048, OWN], BF16, kind=kind_out),
        sz_out=nc.dram_tensor("sz", [2048, OWN], BF16, kind=kind_out),
        carry_out=nc.dram_tensor("carry", [16, 1], F32, kind=kind_out),
    )


def build_C():
    nc = bass.Bass("TRN2", target_bir_lowering=False)
    io = declare_C(nc, kind_x="ExternalInput", xname="x")
    with ExitStack() as ctx:
        P = Prog(nc, ctx)
        C = Common(nc, ctx, P)
        emit_C(nc, ctx, P, C, io, pre=())
        P.play()
    return nc


def emit_C(nc, ctx, P, C, io, pre=()):
    x_in, win, convw, convb = io["x_in"], io["win"], io["convw"], io["convb"]
    wB, wC = io["wB"], io["wC"]
    y0_out, sz_out, carry_out = io["y0_out"], io["sz_out"], io["carry_out"]
    if True:
        sb = lambda nm, s, d: ctx.enter_context(nc.sbuf_tensor('C_' + nm, s, d))
        psum = lambda nm, s: ctx.enter_context(nc.psum_tensor('C_' + nm, s, F32))

        lnz = [sb(f"lnz{i}", [128, L], BF16) for i in range(8)]
        xp_full = [sb(f"xpf{i}", [128, L], BF16) for i in range(16)]
        s_sto = P.sem("s_sto")

        st_ps = psum("st_ps", [64, 512])
        bc_ps = [psum("bc0", [128, 512]), psum("bc1", [128, 512])]
        with ExitStack() as c1:
            sb1 = lambda nm, s, d: c1.enter_context(nc.sbuf_tensor('C_' + nm, s, d))
            x_sb = [sb1(f"x{i}", [128, L], F32) for i in range(8)]
            sq = [sb1(f"sq{i}", [128, 512], BF16) for i in range(8)]
            rows = [sb1(f"row{i}", [1, 512], F32) for i in range(4)]
            s_ldx = P.sem("s_ldx")
            dma_tiles(P, "sync", s_ldx, x_in, x_sb, 8, wait=pre)
            pre = [(s_ldx, s_ldx.n), (C.s_init, 1)]
            s_app, app_vals = emit_ln(P, C, x_sb, lnz, GRID_L,
                                      dict(sq=sq, st_ps=st_ps, bc_ps=bc_ps,
                                           rows=rows), pre)
            ln_done = (s_app, s_app.n)
            ln_sem, ln_app_vals = s_app, app_vals
            P.op("gpsimd", lambda g: None, wait=[ln_done])

        mm_ps = [psum(f"mm{i}", [128, 512]) for i in range(4)]
        with ExitStack() as c2:
            sb2 = lambda nm, s, d: c2.enter_context(nc.sbuf_tensor('C_' + nm, s, d))
            wi_sb = [sb2(f"wi{i}", [128, 4096], BF16) for i in range(8)]
            sz_st = [sb2(f"szs{i}", [128, 512], BF16) for i in range(16)]
            s_ldw = P.sem("s_ldw")
            dma_tiles(P, "gpsimd", s_ldw, win, wi_sb, 8)

            s_wmm = P.sem("s_wmm")
            s_xp = P.sem("s_xp")
            nblk = 0
            for k, (off, w) in enumerate(GRID_L):
                for dt in range(32):
                    bank = mm_ps[nblk % 4]

                    def b_wmm(t, dt=dt, off=off, w=w, bank=bank):
                        ins = None
                        for ct in range(8):
                            ins = t.matmul(bank[:, :w],
                                           wi_sb[ct][:, dt * 128:(dt + 1) * 128],
                                           lnz[ct][:, off:off + w],
                                           start=(ct == 0), stop=(ct == 7))
                        return ins
                    wt = [(s_ldw, s_ldw.n)] if nblk == 0 else []
                    if dt == 0:
                        wt.append((ln_sem, ln_app_vals[k]))
                    if nblk >= 4:
                        wt.append((s_xp, nblk - 3))
                    P.op("tensor", b_wmm, wait=wt, inc=[(s_wmm, 1)])

                    if dt < 16:
                        def b_xev(v, dt=dt, off=off, w=w, bank=bank):
                            return v.tensor_copy(xp_full[dt][:, off:off + w],
                                                 bank[:, :w])
                        P.op("vector", b_xev, wait=[(s_wmm, s_wmm.n)],
                             inc=[(s_xp, 1)])
                    else:
                        def b_zev(s, dt=dt, w=w, bank=bank):
                            return s.activation(sz_st[dt - 16][:, :w],
                                                bank[:, :w], AF.Silu)
                        wt = [(s_wmm, s_wmm.n)]
                        if k > 0:
                            wt.append((s_sto, 16 * 16 * k))
                        P.op("scalar", b_zev, wait=wt, inc=[(s_xp, 1)])
                    nblk += 1

                src_lo, own_lo, ww = own_store_range(off, w)
                zv = sz_out.rearrange("(a p) t -> a p t", p=128)
                wt_sto = (s_xp.h, s_xp.n)

                def b_szst(sy, src_lo=src_lo, own_lo=own_lo, ww=ww, wt=wt_sto,
                           zv=zv):
                    sy.wait_ge(*wt)
                    for i in range(16):
                        sy.dma_start(out=zv[i][:, own_lo:own_lo + ww],
                                     in_=sz_st[i][:, src_lo:src_lo + ww]
                                     ).then_inc(s_sto.h, 16)
                s_sto.bump(16 * 16)
                P.raw("sync", b_szst)
            wmm_done = (s_wmm, s_wmm.n)
            evac_done = (s_xp, s_xp.n)
            szsto_done = (s_sto, s_sto.n)
            P.op("gpsimd", lambda g: None,
                 wait=[wmm_done, evac_done, szsto_done])

        with ExitStack() as c3:
            sb3 = lambda nm, s, d: c3.enter_context(nc.sbuf_tensor('C_' + nm, s, d))
            cw_sb = sb3("cwv", [128, 64], F32)
            cb_sb = sb3("cbv", [128, 16], F32)
            wb_sb = [sb3(f"wb{i}", [128, 16], BF16) for i in range(16)]
            wc_sb = sb3("wcv", [16, 2048], BF16)
            c09 = sb3("c09", [16, 512], F32)
            zini = sb3("zini", [16, 1], F32)
            xc_st = [sb3(f"xc{i}", [128, 512], BF16) for i in range(16)]
            acc = [sb3(f"acc{i}", [128, 512], F32) for i in range(4)]
            u01 = sb3("u01", [16, 512], F32)
            h_sec = [sb3(f"h{i}", [16, 512], F32) for i in range(4)]
            hb = sb3("hb", [16, 512], BF16)
            hpad = sb3("hpad", [16, 512], F32)
            y0_st = [sb3(f"y0s{i}", [128, 512], BF16) for i in range(16)]
            ups = psum("ups", [16, 512])
            s_ld3 = P.sem("s_ld3")
            dma_tiles(P, "gpsimd", s_ld3, wB, wb_sb, 16)

            def b_ld3(g):
                g.dma_start(out=cw_sb[:, :], in_=convw[:, :]).then_inc(s_ld3.h, 16)
                g.dma_start(out=cb_sb[:, :], in_=convb[:, :]).then_inc(s_ld3.h, 16)
                g.dma_start(out=wc_sb[:, :], in_=wC[:, :]).then_inc(s_ld3.h, 16)
            s_ld3.bump(48)
            P.raw("gpsimd", b_ld3)

            s_c09 = P.sem("s_c09")

            def b_c09(v):
                v.memset(c09[:, :], 0.9)
                return v.memset(zini[:, :], 0.0)
            P.op("vector", b_c09, wait=[evac_done, szsto_done],
                 inc=[(s_c09, 1)])

            s_conv = P.sem("s_conv")
            s_xc = P.sem("s_xc")
            s_u = P.sem("s_u")
            s_scan = P.sem("s_scan")
            s_ymm = P.sem("s_ymm")
            s_yev = P.sem("s_yev")

            for si, (soff, sw) in enumerate(GRID_OWN):
                for pt in range(16):
                    a = acc[pt % 4]

                    def b_conv(v, soff=soff, pt=pt, a=a):
                        v.tensor_scalar(a[:, :], xp_full[pt][:, soff + 1:soff + 513],
                                        cw_sb[:, pt * 4:pt * 4 + 1], None,
                                        ALU.mult)
                        ins = None
                        for j in (1, 2, 3):
                            srcv = xp_full[pt][:, soff + 1 + j:soff + 513 + j]
                            ins = v.scalar_tensor_tensor(
                                a[:, :], srcv, cw_sb[:, pt * 4 + j:pt * 4 + j + 1],
                                a[:, :], ALU.mult, ALU.add)
                        return ins
                    wt = []
                    if si == 0 and pt == 0:
                        wt += [(s_ld3, s_ld3.n), (s_c09, 1)]
                    if s_conv.n >= 4:
                        wt.append((s_xc, s_conv.n - 3))
                    if si > 0 and pt == 0:
                        wt.append((s_ymm, s_ymm.n))
                    P.op("vector", b_conv, wait=wt, inc=[(s_conv, 1)])

                    def b_silu(s, pt=pt, a=a):
                        return s.activation(xc_st[pt][:, :], a[:, :], AF.Silu,
                                            bias=cb_sb[:, pt:pt + 1])
                    wt = [(s_conv, s_conv.n)]
                    if si > 0 and pt == 0:
                        wt.append((s_u, si))
                    P.op("scalar", b_silu, wait=wt, inc=[(s_xc, 1)])

                def b_umm(t, si=si):
                    ins = None
                    for ct in range(16):
                        ins = t.matmul(ups[:, :], wb_sb[ct][:, :],
                                       xc_st[ct][:, :], start=(ct == 0),
                                       stop=(ct == 15))
                    return ins
                P.op("tensor", b_umm, wait=[(s_xc, s_xc.n), (s_scan, si)],
                     inc=[(s_u, 1)])

                def b_scan(v, si=si):
                    v.tensor_scalar_mul(u01[:, :], ups[:, :], 0.1)
                    init = zini[:, 0:1] if si == 0 else h_sec[si - 1][:, 511:512]
                    v.tensor_tensor_scan(h_sec[si][:, :], c09[:, :], u01[:, :],
                                         init, ALU.mult, ALU.add)
                    return v.tensor_copy(hb[:, :], h_sec[si][:, :])
                P.op("vector", b_scan, wait=[(s_u, s_u.n), (s_ymm, s_ymm.n)],
                     inc=[(s_scan, 1)])

                for dt in range(16):
                    bank = mm_ps[dt % 2]

                    def b_ymm(t, dt=dt, bank=bank):
                        return t.matmul(bank[:, :], wc_sb[:, dt * 128:(dt + 1) * 128],
                                        hb[:, :], start=True, stop=True)
                    wt = [(s_scan, s_scan.n)] if dt == 0 else []
                    if s_ymm.n >= 2:
                        wt.append((s_yev, s_ymm.n - 1))
                    P.op("tensor", b_ymm, wait=wt, inc=[(s_ymm, 1)])

                    def b_yev(v, dt=dt, bank=bank):
                        return v.tensor_copy(y0_st[dt][:, :], bank[:, :])
                    wt = [(s_ymm, s_ymm.n)]
                    if si > 0:
                        wt.append((s_sto, 16 * (16 * 5 + 16 * (si - 1) + dt + 1)))
                    P.op("vector", b_yev, wait=wt, inc=[(s_yev, 1)])

                yv = y0_out.rearrange("(a p) t -> a p t", p=128)
                wt_sto = (s_yev.h, s_yev.n)

                def b_yst(sy, soff=soff, wt=wt_sto, yv=yv):
                    sy.wait_ge(*wt)
                    for i in range(16):
                        sy.dma_start(out=yv[i][:, soff:soff + 512],
                                     in_=y0_st[i][:, :]).then_inc(s_sto.h, 16)
                s_sto.bump(16 * 16)
                P.raw("sync", b_yst)

            s_cy = P.sem("s_cy")

            def b_hpad(v):
                return v.tensor_copy(hpad[:, :], h_sec[3][:, :])
            P.op("vector", b_hpad, wait=[(s_scan, s_scan.n)], inc=[(s_cy, 1)])

            def b_cyst(sy):
                sy.wait_ge(s_cy.h, 1)
                sy.dma_start(out=carry_out[:, :], in_=hpad[:, 511:512]
                             ).then_inc(s_sto.h, 16)
            s_sto.bump(16)
            P.raw("sync", b_cyst)

            P.op("sync", lambda e: None, wait=[(s_sto, s_sto.n)])
            return dict(s_sto=s_sto, s_scan=s_scan, s_yev=s_yev, s_ymm=s_ymm,
                        s_u=s_u, s_xc=s_xc, s_conv=s_conv, s_cy=s_cy)



def declare_D(nc, kind_in="ExternalInput", kind_x="ExternalInput",
              xname="xd", carryname="carryin", skip_io=False):
    return dict(
        y0_in=None if skip_io else nc.dram_tensor("y0", [2048, OWN], BF16,
                                                  kind=kind_in),
        sz_in=None if skip_io else nc.dram_tensor("sz", [2048, OWN], BF16,
                                                  kind=kind_in),
        carry_in=None if skip_io else nc.dram_tensor(carryname, [16, 1], F32,
                                                     kind=kind_in),
        x_in=None if skip_io else nc.dram_tensor(xname, [D, OWN], F32,
                                                 kind=kind_x),
        wout=nc.dram_tensor("wout", [2048, D], F32, kind="ExternalInput"),
        wc2=nc.dram_tensor("wc2", [16, 2048], F32, kind="ExternalInput"),
        decay=nc.dram_tensor("decay", [128, OWN], F32, kind="ExternalInput"),
        th=nc.dram_tensor("th", [D, D], F32, kind="ExternalInput"),
        thn=nc.dram_tensor("thn", [D, D], F32, kind="ExternalInput"),
        ph=nc.dram_tensor("ph", [D, D], F32, kind="ExternalInput"),
        phn=nc.dram_tensor("phn", [D, D], F32, kind="ExternalInput"),
        out_d=nc.dram_tensor("out", [D, OWN], F32, kind="ExternalOutput"),
    )


def build_D():
    nc = bass.Bass("TRN2", target_bir_lowering=False)
    io = declare_D(nc, xname="x", carryname="carry")
    with ExitStack() as ctx:
        P = Prog(nc, ctx)
        C = Common(nc, ctx, P)
        emit_D(nc, ctx, P, C, io, pre=())
        P.play()
    return nc


def emit_D(nc, ctx, P, C, io, pre=(), carry_fold=None):
    y0_in, sz_in, carry_in, x_in = (io["y0_in"], io["sz_in"], io["carry_in"],
                                    io["x_in"])
    wout, wc2, decay = io["wout"], io["wc2"], io["decay"]
    th, thn, ph, phn, out_d = io["th"], io["thn"], io["ph"], io["phn"], io["out_d"]
    if True:
        sb = lambda nm, s, d: ctx.enter_context(nc.sbuf_tensor('E_' + nm, s, d))
        psum = lambda nm, s: ctx.enter_context(nc.psum_tensor('E_' + nm, s, F32))

        xsec = [sb(f"x{i}", [128, 512], F32) for i in range(8)]
        dec_sb = sb("dec", [128, OWN], F32)
        ct_sb = [sb(f"ct{i}", [128, D], BF16) for i in range(8)]
        st_sb = [sb(f"st{i}", [128, D], BF16) for i in range(8)]
        wo_sb = [sb(f"wo{i}", [128, D], BF16) for i in range(16)]
        wc2_sb = sb("wc2v", [16, 2048], BF16)
        cb_sb = sb("cb", [16, 1], BF16)
        cpad = sb("cpad", [16, 128], BF16)
        cwc_sb = sb("cwc", [128, 16], F32)
        y0_st = [sb(f"y0s{i}", [128, 512], BF16) for i in range(16)]
        sz_st = [sb(f"szs{i}", [128, 512], BF16) for i in range(16)]
        yb_st = y0_st
        xo_st = [sb(f"xo{i}", [128, 512], F32) for i in range(8)]
        lnw = [sb(f"lnw{i}", [128, 512], BF16) for i in range(8)]
        sq = [sb(f"sq{i}", [128, 512], BF16) for i in range(8)]
        rows = [sb(f"row{i}", [1, 512], F32) for i in range(4)]
        re2 = sb("re2", [128, 512], F32)
        im2 = sb("im2", [128, 512], F32)
        mag = sb("mag", [128, 512], F32)
        out_st = [sb(f"os{i}", [128, 512], F32) for i in range(8)]
        st_ps = psum("st_ps", [64, 512])
        bc_ps = [psum("bc0", [128, 512]), psum("bc1", [128, 512])]
        mm_ps = [psum(f"mm{i}", [128, 512]) for i in range(4)]
        cwc_ps = psum("cwc_ps", [128, 144])

        s_ldx = P.sem("s_ldx")
        s_ldw = P.sem("s_ldw")
        ln_sems_D = tuple(P.sem() for _ in range(6))
        s_trg = P.sem("s_trg")
        s_sin = P.sem("s_sin")
        s_cwm = P.sem("s_cwm")
        s_cwc = P.sem("s_cwc")
        s_ld = P.sem("s_ld")
        s_cor = P.sem("s_cor")
        s_wmm = P.sem("s_wmm")
        s_xo = P.sem("s_xo")
        s_fmm = P.sem("s_fmm")
        s_sqv = P.sem("s_sqv")
        s_mag = P.sem("s_mag")
        s_fin = P.sem("s_fin")
        s_sto = P.sem("s_sto")

        dma_tiles(P, "gpsimd", s_ldw, wout, wo_sb, 16, wait=pre)
        xv_in = x_in.rearrange("(a p) t -> a p t", p=128)

        def b_ldm(g):
            g.dma_start(out=dec_sb[:, :], in_=decay[:, :]).then_inc(s_ldw.h, 16)
            g.dma_start(out=wc2_sb[:, :], in_=wc2[:, :]).then_inc(s_ldw.h, 16)
            g.dma_start(out=cb_sb[:, :], in_=carry_in[:, :]).then_inc(s_ldw.h, 16)
        s_ldw.bump(48)
        P.raw("gpsimd", b_ldm)

        tctx = ExitStack()
        trg = [tctx.enter_context(nc.sbuf_tensor(f"E_trg{i}", [128, D], F32))
               for i in range(2)]
        tsum = tctx.enter_context(nc.sbuf_tensor("E_tsum", [128, D], F32))
        thv = th.rearrange("(a p) t -> a p t", p=128)
        thnv = thn.rearrange("(a p) t -> a p t", p=128)
        phv = ph.rearrange("(a p) t -> a p t", p=128)
        phnv = phn.rearrange("(a p) t -> a p t", p=128)
        for pt in range(8):
            for which, (av, bv, dst, bias) in enumerate(
                    ((thv, thnv, ct_sb, True), (phv, phnv, st_sb, False))):
                def b_ldt(g, pt=pt, av=av, bv=bv):
                    g.dma_start(out=trg[0][:, :], in_=av[pt]).then_inc(s_trg.h, 16)
                    g.dma_start(out=trg[1][:, :], in_=bv[pt]).then_inc(s_trg.h, 16)
                s_trg.bump(32)
                n_prev = s_sin.n
                def b_gate(g, n_prev=n_prev):
                    if n_prev:
                        g.wait_ge(s_sin.h, n_prev)
                P.raw("gpsimd", b_gate)
                P.raw("gpsimd", b_ldt)

                def b_add(v):
                    return v.tensor_add(tsum[:, :], trg[0][:, :], trg[1][:, :])
                P.op("vector", b_add, wait=[(s_trg, s_trg.n),
                                            (s_sin, s_sin.n)],
                     inc=[(s_trg, 1)])

                if bias:
                    def b_sin(s, pt=pt):
                        return s.activation(tsum[:, :], tsum[:, :], AF.Sin,
                                            bias=C.c(math.pi / 2))
                    P.op("scalar", b_sin, wait=[(s_trg, s_trg.n), (C.s_init, 1)],
                         inc=[(s_trg, 1)])

                    def b_ctd(v, pt=pt, dst=dst):
                        return v.tensor_scalar_add(dst[pt][:, :], tsum[:, :],
                                                   -1.0)
                    P.op("vector", b_ctd, wait=[(s_trg, s_trg.n)],
                         inc=[(s_sin, 1)])
                else:
                    def b_sin(s, pt=pt, dst=dst):
                        return s.activation(dst[pt][:, :], tsum[:, :], AF.Sin)
                    P.op("scalar", b_sin, wait=[(s_trg, s_trg.n), (C.s_init, 1)],
                         inc=[(s_sin, 1)])

        tctx.close()
        trig_done = [(s_sin, s_sin.n), (s_trg, s_trg.n)]

        def b_cini(v):
            if carry_fold is not None:
                carry_fold(v, cb_sb)
            return v.memset(cpad[:, :], 0.0)
        P.op("vector", b_cini, wait=[(s_ldw, s_ldw.n)], inc=[(s_cwm, 1)])

        def b_cwm(t):
            for dt in range(16):
                t.matmul(cwc_ps[:, dt:dt + 1], wc2_sb[:, dt * 128:(dt + 1) * 128],
                         cb_sb[:, :], start=True, stop=True)
            return t.matmul(cwc_ps[:, 16:144], wc2_sb[:, 0:128], cpad[:, :],
                            start=True, stop=True)
        P.op("tensor", b_cwm, wait=[(s_cwm, 1)], inc=[(s_cwm, 1)])

        def b_cwe(v):
            return v.tensor_copy(cwc_sb[:, :], cwc_ps[:, 0:16])
        P.op("vector", b_cwe, wait=[(s_cwm, 2)], inc=[(s_cwc, 1)])

        y0v = y0_in.rearrange("(a p) t -> a p t", p=128)
        szv = sz_in.rearrange("(a p) t -> a p t", p=128)
        ov = out_d.rearrange("(a p) t -> a p t", p=128)

        for si, (soff, sw) in enumerate(GRID_OWN):
            def b_lds(sy, soff=soff, si=si, corn=s_cor.n, xon=s_xo.n,
                      td=list(trig_done)):
                if si == 0:
                    for h, v in td:
                        sy.wait_ge(h.h if isinstance(h, Sem) else h, v)
                if si > 0:
                    sy.wait_ge(s_cor.h, corn)
                    sy.wait_ge(s_xo.h, xon)
                for i in range(16):
                    sy.dma_start(out=y0_st[i][:, :],
                                 in_=y0v[i][:, soff:soff + 512]).then_inc(s_ld.h, 16)
                for i in range(16):
                    sy.dma_start(out=sz_st[i][:, :],
                                 in_=szv[i][:, soff:soff + 512]).then_inc(s_ld.h, 16)
                for i in range(8):
                    sy.dma_start(out=xsec[i][:, :],
                                 in_=xv_in[i][:, soff:soff + 512]).then_inc(s_ld.h, 16)
            s_ld.bump(16 * 40)
            P.raw("sync", b_lds)

            def b_cor(v, soff=soff, si=si):
                ins = None
                for pt in range(16):
                    v.scalar_tensor_tensor(yb_st[pt][:, :],
                                           dec_sb[:, soff:soff + 512],
                                           cwc_sb[:, pt:pt + 1],
                                           y0_st[pt][:, :], ALU.mult, ALU.add)
                    ins = v.tensor_mul(yb_st[pt][:, :], yb_st[pt][:, :],
                                       sz_st[pt][:, :])
                return ins
            wt = [(s_ld, s_ld.n), (s_cwc, 1)]
            if si > 0:
                wt.append((s_wmm, s_wmm.n))
            P.op("vector", b_cor, wait=wt, inc=[(s_cor, 1)])

            for dm in range(8):
                bank = mm_ps[dm % 2]

                def b_wmm(t, dm=dm, bank=bank):
                    ins = None
                    for ct in range(16):
                        ins = t.matmul(bank[:, :],
                                       wo_sb[ct][:, dm * 128:(dm + 1) * 128],
                                       yb_st[ct][:, :],
                                       start=(ct == 0), stop=(ct == 15))
                    return ins
                wt = [(s_cor, s_cor.n)] if dm == 0 else []
                if dm == 0 and si == 0:
                    wt.append((s_ldw, s_ldw.n))
                if s_wmm.n >= 2:
                    wt.append((s_xo, s_wmm.n - 1))
                P.op("tensor", b_wmm, wait=wt, inc=[(s_wmm, 1)])

                def b_xo(v, dm=dm, bank=bank):
                    return v.tensor_add(xo_st[dm][:, :], xsec[dm][:, :],
                                        bank[:, :])
                P.op("vector", b_xo, wait=[(s_wmm, s_wmm.n)], inc=[(s_xo, 1)])

            pre = [(s_xo, s_xo.n), (C.s_init, 1)]
            s_app, _ = emit_ln(P, C, xo_st, lnw, [(0, 512)],
                               dict(sq=sq, st_ps=st_ps, bc_ps=bc_ps,
                                    rows=rows), pre, sems=ln_sems_D)

            for ot in range(8):
                b_re = mm_ps[2]
                b_im = mm_ps[3]

                def b_fmm(t, ot=ot, b_re=b_re, b_im=b_im):
                    for it in range(8):
                        t.matmul(b_re[:, :], ct_sb[it][:, ot * 128:(ot + 1) * 128],
                                 lnw[it][:, :], start=(it == 0), stop=(it == 7))
                    ins = None
                    for it in range(8):
                        ins = t.matmul(b_im[:, :],
                                       st_sb[it][:, ot * 128:(ot + 1) * 128],
                                       lnw[it][:, :], start=(it == 0),
                                       stop=(it == 7))
                    return ins
                wt = [(s_app, s_app.n)] if ot == 0 else []
                if si == 0 and ot == 0:
                    wt.append((s_sin, s_sin.n))
                if s_sqv.n > 0:
                    wt.append((s_sqv, s_sqv.n))
                P.op("tensor", b_fmm, wait=wt, inc=[(s_fmm, 1)])

                def b_sq2(s, b_re=b_re, b_im=b_im):
                    s.activation(re2[:, :], b_re[:, :], AF.Square)
                    return s.activation(im2[:, :], b_im[:, :], AF.Square)
                wt = [(s_fmm, s_fmm.n)]
                if s_sqv.n:
                    wt.append((s_mag, s_mag.n - 1))
                P.op("scalar", b_sq2, wait=wt, inc=[(s_sqv, 1)])

                def b_sum(v):
                    return v.tensor_add(mag[:, :], re2[:, :], im2[:, :])
                wt = [(s_sqv, s_sqv.n)]
                if s_fin.n:
                    wt.append((s_fin, s_fin.n))
                P.op("vector", b_sum, wait=wt, inc=[(s_mag, 1)])

                def b_sqrt(s):
                    s.activation(mag[:, :], mag[:, :], AF.Ln)
                    return s.activation(mag[:, :], mag[:, :], AF.Exp,
                                        scale=C.c(0.5))
                P.op("scalar", b_sqrt, wait=[(s_mag, s_mag.n)],
                     inc=[(s_mag, 1)])

                def b_out(v, ot=ot, si=si):
                    return v.tensor_add(out_st[ot][:, :], xo_st[ot][:, :],
                                        mag[:, :])
                wt = [(s_mag, s_mag.n)]
                if si > 0:
                    wt.append((s_sto, 16 * (8 * (si - 1) + ot + 1)))
                P.op("vector", b_out, wait=wt, inc=[(s_fin, 1)])

            wt_sto = (s_fin.h, s_fin.n)

            def b_sto(sy, soff=soff, wt=wt_sto):
                sy.wait_ge(*wt)
                for i in range(8):
                    sy.dma_start(out=ov[i][:, soff:soff + 512],
                                 in_=out_st[i][:, :]).then_inc(s_sto.h, 16)
            s_sto.bump(16 * 8)
            P.raw("sync", b_sto)

        P.op("sync", lambda e: None, wait=[(s_sto, s_sto.n)])
        return dict(s_sto=s_sto, s_fin=s_fin)



GRID_T = [(i * 512, 512) for i in range(8)]


def declare_A(nc, kind_out="ExternalOutput"):
    return dict(
        x_in=nc.dram_tensor("x", [D, T], F32, kind="ExternalInput"),
        q12w=nc.dram_tensor("q12w", [D, 1024], F32, kind="ExternalInput"),
        k12w=nc.dram_tensor("k12w", [D, 1024], F32, kind="ExternalInput"),
        vw=nc.dram_tensor("vw", [D, 1024], F32, kind="ExternalInput"),
        lamv=nc.dram_tensor("lamv", [1, 2048], F32, kind="ExternalInput"),
        tri=nc.dram_tensor("tri", [128, 128], BF16, kind="ExternalInput"),
        yn_out=nc.dram_tensor("ynorm", [1024, T], BF16, kind=kind_out),
    )


def build_A():
    nc = bass.Bass("TRN2", target_bir_lowering=False)
    io = declare_A(nc)
    with ExitStack() as ctx:
        P = Prog(nc, ctx)
        C = Common(nc, ctx, P)
        emit_A(nc, ctx, P, C, io)
        P.play()
    return nc


def emit_A(nc, ctx, P, C, io):
    x_in, q12w, k12w, vw = io["x_in"], io["q12w"], io["k12w"], io["vw"]
    lamv, tri, yn_out = io["lamv"], io["tri"], io["yn_out"]
    if True:
        sb = lambda nm, s, d: ctx.enter_context(nc.sbuf_tensor('A_' + nm, s, d))
        psum = lambda nm, s: ctx.enter_context(nc.psum_tensor('A_' + nm, s, F32))

        lnx = [sb(f"lnx{i}", [128, T], BF16) for i in range(8)]
        tri_sb = sb("trisb", [128, 128], BF16)
        lam_sb = sb("lamsb", [1, 2048], F32)
        nlam = sb("nlam", [1, 512], F32)
        lr1 = sb("lr1", [1, 512], F32)
        lr2 = sb("lr2", [1, 512], F32)
        s_ldc = P.sem("s_ldc")

        def b_ldc(g):
            g.dma_start(out=tri_sb[:, :], in_=tri[:, :]).then_inc(s_ldc.h, 16)
            g.dma_start(out=lam_sb[:, :], in_=lamv[:, :]).then_inc(s_ldc.h, 16)
        s_ldc.bump(32)
        P.raw("gpsimd", b_ldc)

        s_lam = P.sem("s_lam")

        def b_lam1(v):
            v.tensor_mul(lam_sb[0:1, 0:512], lam_sb[0:1, 0:512],
                         lam_sb[0:1, 512:1024])
            v.tensor_mul(lam_sb[0:1, 1024:1536], lam_sb[0:1, 1024:1536],
                         lam_sb[0:1, 1536:2048])
            p1 = lam_sb[0:1, 0:512].rearrange("p (h d) -> p h d", d=64)
            p2 = lam_sb[0:1, 1024:1536].rearrange("p (h d) -> p h d", d=64)
            v.reduce_sum(lr1[0:1, 0:8].rearrange("p (h d) -> p h d", d=1), p1,
                         axis=mybir.AxisListType.X)
            v.reduce_sum(lr2[0:1, 0:8].rearrange("p (h d) -> p h d", d=1),
                         p2, axis=mybir.AxisListType.X)
            return v.tensor_copy(lam_sb[0:1, 0:512], lam_sb[0:1, 0:512])
        P.op("vector", b_lam1, wait=[(s_ldc, 32)], inc=[(s_lam, 1)])

        def b_lam2(s):
            s.activation(lr1[:, 0:8], lr1[:, 0:8], AF.Exp)
            s.activation(lr2[:, 0:8], lr2[:, 0:8], AF.Exp)
            return s.activation(lam_sb[0:1, 0:512], lam_sb[0:1, 0:512],
                                AF.Identity)
        P.op("scalar", b_lam2, wait=[(s_lam, 1)], inc=[(s_lam, 1)])

        def b_lam3(v):
            v.scalar_tensor_tensor(nlam[:, 0:8], lr1[:, 0:8], -1.0,
                                   lr2[:, 0:8], ALU.mult, ALU.add)
            v.tensor_copy(lam_sb[0:1, 0:512], lam_sb[0:1, 0:512])
            v.tensor_scalar_add(nlam[:, 0:8], nlam[:, 0:8], -LAMBDA_INIT)
            return v.tensor_copy(lam_sb[0:1, 0:512], lam_sb[0:1, 0:512])
        P.op("vector", b_lam3, wait=[(s_lam, 2)], inc=[(s_lam, 1)])

        with ExitStack() as c1:
            sb1 = lambda nm, s, d: c1.enter_context(nc.sbuf_tensor('A_' + nm, s, d))
            ps1 = ExitStack()
            st_ps = ps1.enter_context(nc.psum_tensor("st_ps", [64, 512], F32))
            bc_ps = [ps1.enter_context(nc.psum_tensor(f"bc{i}", [128, 512], F32))
                     for i in range(2)]
            xch = [[sb1(f"xc{b}_{i}", [128, 512], F32) for i in range(8)]
                   for b in range(2)]
            sq = [sb1(f"sq{i}", [128, 512], BF16) for i in range(8)]
            rows = [sb1(f"row{i}", [1, 512], F32) for i in range(4)]
            s_lx = P.sem("s_lx")
            ln_sems = tuple(P.sem() for _ in range(6))
            xv = x_in.rearrange("(a p) t -> a p t", p=128)

            app_sems = []
            for k, (off, w) in enumerate(GRID_T):
                buf = xch[k % 2]

                def b_lx(sy, off=off, k=k, buf=buf, prev=app_sems[-2] if
                         len(app_sems) >= 2 else None):
                    if prev is not None:
                        sy.wait_ge(prev[0].h, prev[1])
                    for i in range(8):
                        sy.dma_start(out=buf[i][:, :],
                                     in_=xv[i][:, off:off + w]
                                     ).then_inc(s_lx.h, 16)
                s_lx.bump(16 * 8)
                P.raw("sync", b_lx)

                pre = [(s_lx, s_lx.n), (C.s_init, 1)]
                if app_sems:
                    pre.append(app_sems[-1])
                s_app, _ = emit_ln(P, C, buf, [OffView(lt, off) for lt in lnx],
                                   [(0, 512)],
                                   dict(sq=sq, st_ps=st_ps, bc_ps=bc_ps,
                                        rows=rows), pre, sems=ln_sems)
                app_sems.append((s_app, s_app.n))
            ln_done = app_sems[-1]
            lnall = [(s, v) for s, v in app_sems]
            P.op("gpsimd", lambda g: None,
                 wait=[(s, v) for s, v in app_sems])
            ps1.close()

        s12 = [psum("s12a", [128, 1024]), psum("s12b", [128, 1024])]
        y1_ps = psum("y1", [128, 512])
        y2_ps = psum("y2", [128, 512])
        sum_ps = psum("sums", [64, 512])
        bc2_ps = psum("bc2", [128, 512])

        qw_sb = [sb(f"qw{i}", [128, 1024], BF16) for i in range(2)]
        kw_sb = [sb(f"kw{i}", [128, 1024], BF16) for i in range(2)]
        vw_sb = [sb(f"vw{i}", [128, 1024], BF16) for i in range(2)]
        q12_sb = sb("q12", [128, T], BF16)
        k12_sb = sb("k12", [128, T], BF16)
        v_sb = sb("vsb", [128, T], BF16)
        e12 = [sb(f"e12_{i}", [128, 1024], BF16) for i in range(2)]
        r2_sb = sb("r2", [1, 512], F32)
        gneg = sb("gneg", [1, 512], F32)
        s1_sb = sb("s1r", [1, 512], F32)
        s1e = sb("s1e", [1, 512], F32)
        gb_sb = sb("gb", [128, 512], F32)
        t2_sb = sb("t2", [128, 512], F32)
        yc_bf = sb("ycb", [128, 512], BF16)
        yc2_bf = sb("yc2b", [128, 512], BF16)
        t3_sb = sb("t3", [128, 512], F32)
        yn_st = sb("ynst", [128, 512], BF16)
        mu_r = sb("amur", [1, 512], F32)
        m2_r = sb("m2r", [1, 512], F32)
        var_r = sb("varr", [1, 512], F32)
        rst_r = sb("rstr", [1, 512], F32)

        s_wld = P.sem("s_wld")
        s_pj = P.sem("s_pj")
        s_pe = P.sem("s_pe")
        s_qk = P.sem("s_qk")
        s_e = P.sem("s_e")
        s_tri = P.sem("s_tri")
        s_v = P.sem("s_v")
        s_ep = P.sem("s_ep")
        s_ev = P.sem("s_ev")
        s_er = P.sem("s_er")
        s_sto = P.sem("s_sto")

        ynv = yn_out.rearrange("(a p) t -> a p t", p=128)
        qv = q12w.rearrange("(a p) t -> a p t", p=128)
        kv = k12w.rearrange("(a p) t -> a p t", p=128)
        vv = vw.rearrange("(a p) t -> a p t", p=128)

        def load_head(h):
            hb = h % 2

            def b_lw(g, h=h, hb=hb, pjn=s_pj.n):
                if h >= 2:
                    g.wait_ge(s_pj.h, pjn)
                for i in range(8):
                    g.dma_start(out=qw_sb[hb][:, i * 128:(i + 1) * 128],
                                in_=qv[i][:, h * 128:(h + 1) * 128]
                                ).then_inc(s_wld.h, 16)
                for i in range(8):
                    g.dma_start(out=kw_sb[hb][:, i * 128:(i + 1) * 128],
                                in_=kv[i][:, h * 128:(h + 1) * 128]
                                ).then_inc(s_wld.h, 16)
                for i in range(8):
                    g.dma_start(out=vw_sb[hb][:, i * 128:(i + 1) * 128],
                                in_=vv[i][:, h * 128:(h + 1) * 128]
                                ).then_inc(s_wld.h, 16)
            s_wld.bump(16 * 24)
            P.raw("gpsimd", b_lw)

        load_head(0)
        load_head(1)

        for h in range(8):
            hb = h % 2
            for which, (wsb, dst) in enumerate(((qw_sb[hb], q12_sb),
                                                (kw_sb[hb], k12_sb))):
                for k, (off, w) in enumerate(GRID_T):
                    bank = y1_ps if (k % 2 == 0) else y2_ps

                    def b_p(t, wsb=wsb, off=off, w=w, bank=bank):
                        ins = None
                        for ct in range(8):
                            ins = t.matmul(bank[:, :w],
                                           wsb[:, ct * 128:(ct + 1) * 128],
                                           lnx[ct][:, off:off + w],
                                           start=(ct == 0), stop=(ct == 7))
                        return ins
                    wt = []
                    if which == 0 and k == 0:
                        wt.append((s_wld, 16 * 24 * (h + 1)))
                        if h == 0:
                            wt += list(lnall) + [(s_lam, 3)]
                        else:
                            wt.append((s_v, s_v.n))
                            wt.append((s_ev, s_ev.n))
                    if s_pj.n >= 2:
                        wt.append((s_pe, s_pj.n - 1))
                    P.op("tensor", b_p, wait=wt, inc=[(s_pj, 1)])

                    def b_pe(v, dst=dst, off=off, w=w, bank=bank):
                        return v.tensor_copy(dst[:, off:off + w], bank[:, :w])
                    P.op("vector", b_pe, wait=[(s_pj, s_pj.n)],
                         inc=[(s_pe, 1)])
            for tc in range(32):
                bank = y1_ps if (tc % 2 == 0) else y2_ps

                def b_vp(t, tc=tc, hb=hb, bank=bank):
                    ins = None
                    for ct in range(8):
                        ins = t.matmul(bank[:, 0:128],
                                       lnx[ct][:, tc * 128:(tc + 1) * 128],
                                       vw_sb[hb][:, ct * 128:(ct + 1) * 128],
                                       start=(ct == 0), stop=(ct == 7))
                    return ins
                wt = []
                if s_pj.n >= 2:
                    wt.append((s_pe, s_pj.n - 1))
                P.op("tensor", b_vp, wait=wt, inc=[(s_pj, 1)])

                def b_vpe(v, tc=tc, bank=bank):
                    return v.tensor_copy(v_sb[:, tc * 128:(tc + 1) * 128],
                                         bank[:, 0:128])
                P.op("vector", b_vpe, wait=[(s_pj, s_pj.n)], inc=[(s_pe, 1)])

            proj_done = (s_pe, s_pe.n)
            load_head(h + 2) if h + 2 < 8 else None

            for qb in range(8):
                nt = 4 * (qb + 1)
                for i in range(nt):
                    b = i % 2
                    joff = max(0, (i - 4 * qb) * 128)
                    def b_qk(t, i=i, qb=qb, b=b, joff=joff):
                        q0 = qb * 512
                        t.matmul(s12[b][0:128, joff:512],
                                 k12_sb[0:64, i * 128:(i + 1) * 128],
                                 q12_sb[0:64, q0 + joff:q0 + 512],
                                 start=True, stop=True, tile_position=(0, 0))
                        return t.matmul(s12[b][0:128, 512 + joff:1024],
                                        k12_sb[64:128, i * 128:(i + 1) * 128],
                                        q12_sb[64:128, q0 + joff:q0 + 512],
                                        start=True, stop=True,
                                        tile_position=(64, 0))
                    wt = []
                    if i == 0 and qb == 0:
                        wt.append(proj_done)
                    if s_qk.n >= 2:
                        wt.append((s_e, s_qk.n - 1))
                    P.op("tensor", b_qk, wait=wt, inc=[(s_qk, 1)])

                    def b_exp(s, b=b):
                        return s.activation(e12[b][:, :], s12[b][:, :], AF.Exp,
                                            bias=C.c(ATT_BIAS),
                                            scale=C.c(0.125))
                    wt = [(s_qk, s_qk.n)]
                    if s_e.n >= 2:
                        wt.append((s_v, s_e.n - 1))
                        wt.append((s_tri, s_tri.n))
                    P.op("scalar", b_exp, wait=wt, inc=[(s_e, 1)])

                    if i >= 4 * qb:
                        def b_tri(v, b=b, joff=joff):
                            if joff > 0:
                                v.memset(e12[b][:, 0:joff], 0.0)
                                v.memset(e12[b][:, 512:512 + joff], 0.0)
                            v.tensor_mul(e12[b][:, joff:joff + 128],
                                         e12[b][:, joff:joff + 128],
                                         tri_sb[:, :])
                            return v.tensor_mul(
                                e12[b][:, 512 + joff:512 + joff + 128],
                                e12[b][:, 512 + joff:512 + joff + 128],
                                tri_sb[:, :])
                        P.op("vector", b_tri, wait=[(s_e, s_e.n)],
                             inc=[(s_tri, 1)])
                        tri_gate = (s_tri, s_tri.n)
                    else:
                        tri_gate = None

                    def b_vs(t, i=i, qb=qb, b=b,
                             st=(i == 0), sp=(i == nt - 1)):
                        t.matmul(y1_ps[:, :],
                                 v_sb[:, i * 128:(i + 1) * 128],
                                 e12[b][:, 0:512], start=st, stop=sp)
                        t.matmul(y2_ps[:, :],
                                 v_sb[:, i * 128:(i + 1) * 128],
                                 e12[b][:, 512:1024], start=st, stop=sp)
                        t.matmul(sum_ps[0:1, :], C.ones128b[:, :],
                                 e12[b][:, 0:512], start=st, stop=sp,
                                 tile_position=(0, 0))
                        return t.matmul(sum_ps[32:33, :],
                                        C.ones128b[:, :],
                                        e12[b][:, 512:1024],
                                        start=st, stop=sp,
                                        tile_position=(0, 32))
                    wt = [tri_gate] if tri_gate else [(s_e, s_e.n)]
                    if i == 0:
                        wt.append((s_ev, s_ev.n))
                        wt.append((s_ep, s_ep.n))
                    P.op("tensor", b_vs, wait=wt, inc=[(s_v, 1)])

                def b_ep1(v, h=h):
                    v.reciprocal(r2_sb[:, :], sum_ps[32:33, :])
                    v.tensor_copy(s1_sb[:, :], sum_ps[0:1, :])
                    v.scalar_tensor_tensor(gneg[:, :], sum_ps[0:1, :],
                                           nlam[0:1, h:h + 1], r2_sb[:, :],
                                           ALU.mult, ALU.mult)
                    return v.scalar_tensor_tensor(s1e[:, :], s1_sb[:, :],
                                                  C.c(EPS)[0:1, :], s1_sb[:, :],
                                                  ALU.mult, ALU.mult)
                P.op("vector", b_ep1, wait=[(s_v, s_v.n)], inc=[(s_ev, 1)])

                def b_gb(t):
                    return t.matmul(bc2_ps[:, :], C.ones1f[:, :], gneg[:, :],
                                    start=True, stop=True)
                P.op("tensor", b_gb, wait=[(s_ev, s_ev.n)], inc=[(s_ep, 1)])

                def b_ep2(v):
                    v.tensor_copy(gb_sb[:, :], bc2_ps[:, :])
                    v.tensor_mul(t2_sb[:, :], gb_sb[:, :], y2_ps[:, :])
                    v.tensor_add(yc_bf[:, :], y1_ps[:, :], t2_sb[:, :])
                    return v.tensor_mul(yc2_bf[:, :], yc_bf[:, :],
                                        yc_bf[:, :])
                P.op("vector", b_ep2, wait=[(s_ep, s_ep.n)], inc=[(s_ev, 1)])

                def b_st2(t):
                    t.matmul(sum_ps[0:1, :], C.ones128b[:, :], yc_bf[:, :],
                             start=True, stop=True, tile_position=(0, 0))
                    return t.matmul(sum_ps[32:33, :], C.ones128b[:, :],
                                    yc2_bf[:, :], start=True, stop=True,
                                    tile_position=(0, 32))
                P.op("tensor", b_st2, wait=[(s_ev, s_ev.n)], inc=[(s_ep, 1)])

                def b_ep3(v):
                    v.tensor_scalar_mul(mu_r[:, :], sum_ps[0:1, :], 1.0 / 128)
                    v.tensor_mul(m2_r[:, :], mu_r[:, :], mu_r[:, :])
                    v.scalar_tensor_tensor(var_r[:, :], sum_ps[32:33, :],
                                           1.0 / 128, m2_r[:, :], ALU.mult,
                                           ALU.subtract)
                    return v.tensor_add(var_r[:, :], var_r[:, :], s1e[:, :])
                P.op("vector", b_ep3, wait=[(s_ep, s_ep.n)], inc=[(s_ev, 1)])

                def b_ep4(s):
                    s.activation(rst_r[:, :], var_r[:, :], AF.Ln)
                    return s.activation(rst_r[:, :], rst_r[:, :], AF.Exp,
                                        scale=C.c(-0.5)[0:1, :],
                                        bias=C.c(math.log(1.0 - LAMBDA_INIT))[0:1, :])
                P.op("scalar", b_ep4, wait=[(s_ev, s_ev.n)], inc=[(s_er, 1)])

                def b_mb(t):
                    return t.matmul(bc2_ps[:, :], C.ones1f[:, :], mu_r[:, :],
                                    start=True, stop=True)
                P.op("tensor", b_mb, wait=[(s_ev, s_ev.n), (s_er, s_er.n)],
                     inc=[(s_ep, 1)])

                def b_ep5(v):
                    return v.tensor_sub(t3_sb[:, :], yc_bf[:, :],
                                        bc2_ps[:, :])
                P.op("vector", b_ep5, wait=[(s_ep, s_ep.n)], inc=[(s_ev, 1)])

                def b_rb(t):
                    return t.matmul(bc2_ps[:, :], C.ones1f[:, :], rst_r[:, :],
                                    start=True, stop=True)
                P.op("tensor", b_rb, wait=[(s_ev, s_ev.n)], inc=[(s_ep, 1)])

                def b_ep6(v):
                    return v.tensor_mul(yn_st[:, :], t3_sb[:, :],
                                        bc2_ps[:, :])
                wt = [(s_ep, s_ep.n)]
                if s_sto.n:
                    wt.append((s_sto, s_sto.n))
                P.op("vector", b_ep6, wait=wt, inc=[(s_ev, 1)])

                def b_yst(sy, h=h, qb=qb, evn=s_ev.n):
                    sy.wait_ge(s_ev.h, evn)
                    sy.dma_start(out=ynv[h][:, qb * 512:(qb + 1) * 512],
                                 in_=yn_st[:, :]).then_inc(s_sto.h, 16)
                s_sto.bump(16)
                P.raw("sync", b_yst)

        P.op("sync", lambda e: None, wait=[(s_sto, s_sto.n)])
        return dict(s_sto=s_sto, s_v=s_v, s_ev=s_ev, s_ep=s_ep, s_er=s_er,
                    s_pe=s_pe, s_pj=s_pj, s_e=s_e, s_tri=s_tri, s_qk=s_qk,
                    lnx=lnx, q12_sb=q12_sb, k12_sb=k12_sb, v_sb=v_sb)


class OffView:

    def __init__(self, t, off):
        self.t = t
        self.off = off

    def __getitem__(self, idx):
        p, f = idx
        assert isinstance(f, slice)
        return self.t[p, f.start + self.off:f.stop + self.off]



class _APView:

    def __init__(self, ap_fn):
        self._fn = ap_fn

    def __getitem__(self, idx):
        return self._fn(idx)


def build_fused():
    FDBG = bool(os.environ.get("FDBG"))
    KI = "ExternalOutput" if FDBG else "Internal"
    nc = bass.Bass("TRN2", target_bir_lowering=False)
    ioA = declare_A(nc, kind_out="Internal")
    cwm = nc.dram_tensor("cwm", [D, D], F32, kind="ExternalInput")
    xw_d = nc.dram_tensor("xw", [D, L], F32, kind="ExternalInput")
    um_d = nc.dram_tensor("umask", [128, 1], F32, kind="ExternalInput")
    wr = nc.dram_tensor("wr", [D, D], F32, kind="ExternalInput")
    wk = nc.dram_tensor("wk", [D, D], F32, kind="ExternalInput")
    wv = nc.dram_tensor("wv", [D, D], F32, kind="ExternalInput")
    mur = nc.dram_tensor("mur", [128, 8], F32, kind="ExternalInput")
    muk = nc.dram_tensor("muk", [128, 8], F32, kind="ExternalInput")
    cwp = nc.dram_tensor("cwp", [2 * D, OWN], F32, kind="Internal")
    rs_out = nc.dram_tensor("rsout", [D, OWN], F32, kind="Internal")
    agh_in = nc.dram_tensor("aghin", [D, HALO], F32, kind="Internal")
    agh_out = nc.dram_tensor("aghout", [2 * D, HALO], F32, kind="Internal")
    cy_out = nc.dram_tensor("cyout", [32, 1], F32, kind="Internal")
    xp_d = nc.dram_tensor("xp", [D, L], F32, kind=KI)
    xod = nc.dram_tensor("xod", [D, OWN], F32, kind="Internal")
    ioC = declare_C(nc, kind_out=KI, skip_x=True)
    ioC["x_in"] = xp_d
    if FDBG:
        cdb = ioC["carry_out"]
        ioC["carry_out"] = nc.dram_tensor("carryint", [16, 1], F32,
                                          kind="Internal")
    ioD = declare_D(nc, skip_io=True)
    ioD["y0_in"] = ioC["y0_out"]
    ioD["sz_in"] = ioC["sz_out"]
    ioD["x_in"] = xod
    ioD["carry_in"] = _APView(lambda idx: cy_out[0:16, 0:1])

    with ExitStack() as ctx:
        P = Prog(nc, ctx)
        C = Common(nc, ctx, P)
        sb = lambda nm, s, d: ctx.enter_context(nc.sbuf_tensor(nm, s, d))
        um_sb = sb("umsb", [128, 1], F32)
        s_um = P.sem("s_um")

        def b_um(sy):
            sy.dma_start(out=um_sb[:, :], in_=um_d[:, :]).then_inc(s_um.h, 16)
        s_um.bump(16)
        P.raw("sync", b_um)

        with ExitStack() as ctxA:
            gA = emit_A(nc, ctxA, P, C, ioA)
        gateA = [(gA[k], gA[k].n) for k in
                 ("s_sto", "s_v", "s_ev", "s_ep", "s_er", "s_pe")]

        s_rs1 = P.sem("s_rs1")
        s_agh = P.sem("s_agh")
        s_agc = P.sem("s_agc")
        with ExitStack() as ctxP:
            sbP = lambda nm, s, d: ctxP.enter_context(nc.sbuf_tensor(nm, s, d))
            psP = lambda nm, s: ctxP.enter_context(nc.psum_tensor(nm, s, F32))
            yn_sb = [sbP(f"pyn{i}", [128, T], BF16) for i in range(8)]
            cwm_sb = [sbP(f"pcw{i}", [128, D], BF16) for i in range(8)]
            o2_st = [sbP(f"po2{i}", [128, 512], F32) for i in range(4)]
            pp_ps = [psP(f"pp{i}", [128, 512]) for i in range(4)]
            s_pl = P.sem("s_pl")
            s_pm = P.sem("s_pm")
            s_pv = P.sem("s_pv")
            s_ps = P.sem("s_ps")
            dma_tiles(P, "gpsimd", s_pl, ioA["yn_out"], yn_sb, 8, wait=gateA)
            dma_tiles(P, "gpsimd", s_pl, cwm, cwm_sb, 8)
            cpv = cwp.rearrange("(a p) t -> a p t", p=128)
            nblk = 0
            for dm in range(8):
                for tc in range(8):
                    bank = pp_ps[nblk % 4]
                    stg = o2_st[nblk % 4]

                    def b_pm(t, dm=dm, tc=tc, bank=bank):
                        ins = None
                        for d2 in range(8):
                            ins = t.matmul(bank[:, :],
                                           cwm_sb[d2][:, dm * 128:(dm + 1) * 128],
                                           yn_sb[d2][:, tc * 512:(tc + 1) * 512],
                                           start=(d2 == 0), stop=(d2 == 7))
                        return ins
                    wt = [(s_pl, s_pl.n)] if nblk == 0 else []
                    if nblk >= 4:
                        wt.append((s_ps, 16 * (nblk - 3)))
                    P.op("tensor", b_pm, wait=wt, inc=[(s_pm, 1)])

                    def b_pv(v, bank=bank, stg=stg):
                        return v.tensor_copy(stg[:, :], bank[:, :])
                    P.op("vector", b_pv, wait=[(s_pm, s_pm.n)], inc=[(s_pv, 1)])

                    blk = tc // 4
                    col = (tc % 4) * 512
                    wtv = (s_pv.h, s_pv.n)

                    def b_pst(sy, dm=dm, blk=blk, col=col, stg=stg, wtv=wtv):
                        sy.wait_ge(*wtv)
                        sy.dma_start(out=cpv[8 * blk + dm][:, col:col + 512],
                                     in_=stg[:, :]).then_inc(s_ps.h, 16)
                    s_ps.bump(16)
                    P.raw("sync", b_pst)
                    nblk += 1

            def b_agh(g):
                g.wait_ge(s_ps.h, s_ps.n)
                g.dma_start(out=agh_in[:, :],
                            in_=cwp[0:D, OWN - HALO:OWN]).then_inc(s_pl.h, 16)
            s_pl.bump(16)
            P.raw("gpsimd", b_agh)

            def b_cc1(g):
                g.wait_ge(s_pl.h, s_pl.n)
                g.collective_compute(
                    "ReduceScatter", ALU.add, replica_groups=PAIRS,
                    ins=[cwp.ap().opt()], outs=[rs_out.ap().opt()],
                ).then_inc(s_rs1.h, 1)
                g.collective_compute(
                    "AllGather", ALU.bypass, replica_groups=PAIRS,
                    ins=[agh_in.ap().opt()], outs=[agh_out.ap().opt()],
                ).then_inc(s_agh.h, 1)
            s_rs1.bump(1)
            s_agh.bump(1)
            P.raw("gpsimd", b_cc1)

        scr = {}
        ctxBspan = ctx.enter_context(ExitStack())
        sbS = lambda nm, s, d: ctxBspan.enter_context(nc.sbuf_tensor(nm, s, d))
        psS = lambda nm, s: ctxBspan.enter_context(nc.psum_tensor(nm, s, F32))
        scr["x_sb"] = [sbS(f"bx{i}", [128, L], F32) for i in range(8)]
        scr["sq"] = [sbS(f"bsq{i}", [128, 512], BF16) for i in range(8)]
        scr["rows"] = [sbS(f"brow{i}", [1, 512], F32) for i in range(4)]
        scr["st_ps"] = psS("bst", [64, 512])
        scr["bc_ps"] = [psS("bbc0", [128, 512]), psS("bbc1", [128, 512])]
        scr["mm_ps"] = [psS(f"bmm{i}", [128, 512]) for i in range(4)]
        x_sb = scr["x_sb"]
        with ExitStack() as ctxR:
            sbR = lambda nm, s, d: ctxR.enter_context(nc.sbuf_tensor(nm, s, d))
            xw_sb = [sbR(f"rxw{i}", [128, L], F32) for i in range(8)]
            rs_st = [sbR(f"rrs{i}", [128, 512], F32) for i in range(8)]
            hsA = sbR("rhsA", [128, 512], F32)
            hsB = sbR("rhsB", [128, 512], F32)
            hsT = sbR("rhsT", [128, 512], F32)
            s_rl = P.sem("s_rl")
            s_rs = P.sem("s_rs")
            s_hz = P.sem("s_hz")
            s_res = P.sem("s_res")
            dma_tiles(P, "sync", s_rl, xw_d, xw_sb, 8,
                      wait=[(s_rs1, 1), (s_agh, 1)])
            ahv = agh_out.rearrange("(a p) t -> a p t", p=128)

            def b_hz(v):
                v.memset(hsA[:, :], 0.0)
                return v.memset(hsB[:, :], 0.0)
            P.op("vector", b_hz, inc=[(s_hz, 1)])
            for i in range(8):
                def b_h(sy, i=i, hzn=s_hz.n, resn=s_res.n):
                    if i == 0:
                        sy.wait_ge(s_hz.h, hzn)
                    else:
                        sy.wait_ge(s_res.h, resn)
                    sy.dma_start(out=hsA[:, 0:HALO],
                                 in_=ahv[i][:, :]).then_inc(s_rl.h, 16)
                    sy.dma_start(out=hsB[:, 0:HALO],
                                 in_=ahv[8 + i][:, :]).then_inc(s_rl.h, 16)
                s_rl.bump(32)
                P.raw("sync", b_h)

                def b_hf(v, i=i):
                    v.tensor_add(hsT[:, :], hsA[:, :], hsB[:, :])
                    v.tensor_add(hsT[:, :], hsT[:, :], xw_sb[i][:, 0:512])
                    return v.tensor_scalar(x_sb[i][:, 0:HALO], hsT[:, 0:HALO],
                                           um_sb[:, 0:1], None, ALU.mult)
                P.op("vector", b_hf, wait=[(s_rl, s_rl.n), (s_um, 16)],
                     inc=[(s_res, 1)])

            rsv = rs_out.rearrange("(a p) t -> a p t", p=128)
            for k in range(4):
                def b_rld(sy, k=k, resn=s_res.n):
                    if k > 0:
                        sy.wait_ge(s_res.h, resn)
                    for i in range(8):
                        sy.dma_start(out=rs_st[i][:, :],
                                     in_=rsv[i][:, k * 512:(k + 1) * 512]
                                     ).then_inc(s_rs.h, 16)
                s_rs.bump(16 * 8)
                P.raw("sync", b_rld)
                for i in range(8):
                    def b_ra(v, i=i, k=k):
                        return v.tensor_add(
                            x_sb[i][:, HALO + k * 512:HALO + (k + 1) * 512],
                            xw_sb[i][:, HALO + k * 512:HALO + (k + 1) * 512],
                            rs_st[i][:, :])
                    P.op("vector", b_ra, wait=[(s_rs, s_rs.n)],
                         inc=[(s_res, 1)])
            res_done = (s_res, s_res.n)
            P.op("gpsimd", lambda g: None, wait=[res_done])

        gB = emit_ln2_ffn0(nc, ctx, P, C, scr,
                           dict(wr=wr, wk=wk, wv=wv, mur=mur, muk=muk,
                                xp_out=xp_d), res_done)
        ctxBspan.close()
        gateB = [(gB[k], gB[k].n) for k in ("s_sto", "s_out", "s_mmf", "s_app",
                                            "s_r", "s_k", "s_ka")]

        s_gx = P.sem("s_gx")

        def b_gx(g):
            for h, v in [(s.h, n) for s, n in gateB]:
                g.wait_ge(h, v)
            g.dma_start(out=xod[:, :], in_=xp_d[0:D, HALO:L]).then_inc(s_gx.h, 16)
        s_gx.bump(16)
        P.raw("gpsimd", b_gx)

        with ExitStack() as ctxC:
            gC = emit_C(nc, ctxC, P, C, ioC, pre=gateB + [(s_gx, 16)])
        gateC = [(gC[k], gC[k].n) for k in ("s_sto", "s_yev", "s_ymm", "s_u",
                                            "s_xc", "s_scan", "s_cy")]

        def b_cc2(g):
            for h, v in [(s.h, n) for s, n in gateC]:
                g.wait_ge(h, v)
            if FDBG:
                g.dma_start(out=cdb[:, :],
                            in_=ioC["carry_out"][:, :]).then_inc(s_agc.h, 16)
            g.collective_compute(
                "AllGather", ALU.bypass, replica_groups=PAIRS,
                ins=[ioC["carry_out"].ap().opt()], outs=[cy_out.ap().opt()],
            ).then_inc(s_agc.h, 1)
        s_agc.bump(1)
        P.raw("gpsimd", b_cc2)

        def carry_fold(v, cb_sb):
            v.tensor_scalar(cb_sb[:, :], cb_sb[:, :], um_sb[0:16, 0:1], None,
                            ALU.mult)
            v.tensor_copy(scr_settle[:, :], scr_settle[:, :])

        scr_settle = sb("settle", [128, 512], F32)

        def b_dsgate(sy):
            for h, v in [(s.h, n) for s, n in gateC]:
                sy.wait_ge(h, v)
        P.raw("sync", b_dsgate)

        with ExitStack() as ctxD:
            gD = emit_D(nc, ctxD, P, C, ioD,
                        pre=gateC + [(s_agc, 1)], carry_fold=carry_fold)
        P.play()
    return nc



_CACHE = {}


def _built(name, fn):
    if name not in _CACHE:
        _CACHE[name] = fn()
    return _CACHE[name]


def _tri_const():
    return np.ascontiguousarray(
        np.tril(np.ones((128, 128), np.float32)).T).astype(BF)


def _decay_const():
    d = (0.9 ** (np.arange(OWN, dtype=np.float64) + 1)).astype(np.float32)
    return np.ascontiguousarray(np.broadcast_to(d, (128, OWN)))


def kernel(**inputs):
    if os.environ.get("K_UNFUSED"):
        return kernel_unfused(**inputs)
    return kernel_fused(**inputs)


def kernel_fused(L0_attn_x, L0_attn_q1_w, L0_attn_q2_w, L0_attn_k1_w,
                 L0_attn_k2_w, L0_attn_v_w, L0_attn_c_w, L0_attn_lambda_q1,
                 L0_attn_lambda_k1, L0_attn_lambda_q2, L0_attn_lambda_k2,
                 L0_ffn_W_r, L0_ffn_W_k, L0_ffn_W_v, L0_ffn_mu_r, L0_ffn_mu_k,
                 L1_attn_w_in, L1_attn_conv_w, L1_attn_conv_b, L1_attn_w_B,
                 L1_attn_w_C, L1_attn_w_out, L1_ffn_theta, L1_ffn_phi,
                 L1_ffn_theta_noise, L1_ffn_phi_noise):
    f32c = lambda a: np.ascontiguousarray(np.asarray(a), dtype=np.float32)
    xT = [f32c(np.asarray(L0_attn_x[b]).T) for b in range(4)]
    q1w, q2w = f32c(L0_attn_q1_w), f32c(L0_attn_q2_w)
    k1w, k2w = f32c(L0_attn_k1_w), f32c(L0_attn_k2_w)
    vw_full = f32c(L0_attn_v_w)
    lq1, lk1 = f32c(L0_attn_lambda_q1), f32c(L0_attn_lambda_k1)
    lq2, lk2 = f32c(L0_attn_lambda_q2), f32c(L0_attn_lambda_k2)
    cw_full = f32c(L0_attn_c_w)
    tri = _tri_const()
    wrT = f32c(np.asarray(L0_ffn_W_r).T)
    wkT = f32c(np.asarray(L0_ffn_W_k).T)
    wvT = f32c(np.asarray(L0_ffn_W_v).T)
    mur = f32c(np.asarray(L0_ffn_mu_r).reshape(8, 128).T)
    muk = f32c(np.asarray(L0_ffn_mu_k).reshape(8, 128).T)
    cwf = np.asarray(L1_attn_conv_w, np.float32).reshape(2048, 4)
    convw = np.zeros((128, 64), np.float32)
    convb = np.zeros((128, 16), np.float32)
    for pt in range(16):
        convw[:, pt * 4:pt * 4 + 4] = cwf[pt * 128:(pt + 1) * 128]
        convb[:, pt] = np.asarray(L1_attn_conv_b,
                                  np.float32)[pt * 128:(pt + 1) * 128]
    win = f32c(L1_attn_w_in)
    wB = f32c(L1_attn_w_B)
    wC = f32c(L1_attn_w_C)
    wout = f32c(L1_attn_w_out)
    decay = _decay_const()
    th, thn = f32c(L1_ffn_theta), f32c(L1_ffn_theta_noise)
    ph, phn = f32c(L1_ffn_phi), f32c(L1_ffn_phi_noise)
    zx = np.zeros((D, HALO), np.float32)

    in_maps = []
    for c in range(NCORES):
        b, u = c // 2, c % 2
        q12 = np.zeros((D, 1024), np.float32)
        k12 = np.zeros((D, 1024), np.float32)
        for hh in range(8):
            g = 8 * u + hh
            q12[:, hh * 128:hh * 128 + 64] = q1w[:, 64 * g:64 * g + 64]
            q12[:, hh * 128 + 64:hh * 128 + 128] = q2w[:, 64 * g:64 * g + 64]
            k12[:, hh * 128:hh * 128 + 64] = k1w[:, 64 * g:64 * g + 64]
            k12[:, hh * 128 + 64:hh * 128 + 128] = k2w[:, 64 * g:64 * g + 64]
        lamv = np.concatenate(
            [lq1[8 * u:8 * u + 8].reshape(-1), lk1[8 * u:8 * u + 8].reshape(-1),
             lq2[8 * u:8 * u + 8].reshape(-1), lk2[8 * u:8 * u + 8].reshape(-1)]
        )[None, :].astype(np.float32)
        if u == 0:
            xw = np.concatenate([zx, xT[b][:, 0:OWN]], axis=1)
        else:
            xw = xT[b][:, OWN - HALO:T]
        in_maps.append(dict(
            x=xT[b], q12w=q12, k12w=k12,
            vw=np.ascontiguousarray(vw_full[:, 1024 * u:1024 * (u + 1)]),
            lamv=lamv, tri=tri,
            cwm=np.ascontiguousarray(cw_full[1024 * u:1024 * (u + 1), :]),
            xw=np.ascontiguousarray(xw),
            umask=np.full((128, 1), float(u), np.float32),
            wr=wrT, wk=wkT, wv=wvT, mur=mur, muk=muk,
            win=win, convw=convw, convb=convb, wB=wB, wC=wC,
            wout=wout, wc2=wC, decay=decay, th=th, thn=thn, ph=ph, phn=phn))
    res = run_bass_kernel_spmd(_built("FUSED", build_fused), in_maps,
                               core_ids=list(range(NCORES)))
    out = np.empty((4, T, D), np.float32)
    for c in range(NCORES):
        b, u = c // 2, c % 2
        out[b, u * OWN:(u + 1) * OWN, :] = \
            np.asarray(res.results[c]["out"], np.float32).T
    return out


def kernel_unfused(L0_attn_x, L0_attn_q1_w, L0_attn_q2_w, L0_attn_k1_w, L0_attn_k2_w,
           L0_attn_v_w, L0_attn_c_w, L0_attn_lambda_q1, L0_attn_lambda_k1,
           L0_attn_lambda_q2, L0_attn_lambda_k2, L0_ffn_W_r, L0_ffn_W_k,
           L0_ffn_W_v, L0_ffn_mu_r, L0_ffn_mu_k, L1_attn_w_in, L1_attn_conv_w,
           L1_attn_conv_b, L1_attn_w_B, L1_attn_w_C, L1_attn_w_out,
           L1_ffn_theta, L1_ffn_phi, L1_ffn_theta_noise, L1_ffn_phi_noise):
    f32c = lambda a: np.ascontiguousarray(np.asarray(a), dtype=np.float32)
    xT = [f32c(np.asarray(L0_attn_x[b]).T) for b in range(4)]
    q1w, q2w = f32c(L0_attn_q1_w), f32c(L0_attn_q2_w)
    k1w, k2w = f32c(L0_attn_k1_w), f32c(L0_attn_k2_w)
    vw_full = f32c(L0_attn_v_w)
    lq1, lk1 = f32c(L0_attn_lambda_q1), f32c(L0_attn_lambda_k1)
    lq2, lk2 = f32c(L0_attn_lambda_q2), f32c(L0_attn_lambda_k2)
    tri = _tri_const()

    in_A = []
    for c in range(NCORES):
        b, u = c // 2, c % 2
        q12 = np.zeros((D, 1024), np.float32)
        k12 = np.zeros((D, 1024), np.float32)
        for hh in range(8):
            g = 8 * u + hh
            q12[:, hh * 128:hh * 128 + 64] = q1w[:, 64 * g:64 * g + 64]
            q12[:, hh * 128 + 64:hh * 128 + 128] = q2w[:, 64 * g:64 * g + 64]
            k12[:, hh * 128:hh * 128 + 64] = k1w[:, 64 * g:64 * g + 64]
            k12[:, hh * 128 + 64:hh * 128 + 128] = k2w[:, 64 * g:64 * g + 64]
        lamv = np.concatenate(
            [lq1[8 * u:8 * u + 8].reshape(-1), lk1[8 * u:8 * u + 8].reshape(-1),
             lq2[8 * u:8 * u + 8].reshape(-1), lk2[8 * u:8 * u + 8].reshape(-1)]
        )[None, :].astype(np.float32)
        in_A.append(dict(x=xT[b], q12w=q12, k12w=k12,
                         vw=np.ascontiguousarray(vw_full[:, 1024 * u:1024 * (u + 1)]),
                         lamv=lamv, tri=tri))
    res_A = run_bass_kernel_spmd(_built("A", build_A), in_A,
                                 core_ids=list(range(NCORES)))
    yn = [np.asarray(res_A.results[c]["ynorm"]) for c in range(NCORES)]

    wrT = f32c(np.asarray(L0_ffn_W_r).T)
    wkT = f32c(np.asarray(L0_ffn_W_k).T)
    wvT = f32c(np.asarray(L0_ffn_W_v).T)
    mur = f32c(np.asarray(L0_ffn_mu_r).reshape(8, 128).T)
    muk = f32c(np.asarray(L0_ffn_mu_k).reshape(8, 128).T)
    cw = f32c(L0_attn_c_w)
    zb = np.zeros((2048, HALO), BF)
    zx = np.zeros((D, HALO), np.float32)
    in_B = []
    for c in range(NCORES):
        b, u = c // 2, c % 2
        ynf = np.concatenate([yn[2 * b], yn[2 * b + 1]], axis=0)
        if u == 0:
            ynw = np.concatenate([zb, ynf[:, 0:OWN]], axis=1)
            xw = np.concatenate([zx, xT[b][:, 0:OWN]], axis=1)
        else:
            ynw = ynf[:, OWN - HALO:T]
            xw = xT[b][:, OWN - HALO:T]
        in_B.append(dict(ynf=np.ascontiguousarray(ynw),
                         x=np.ascontiguousarray(xw), cw=cw, wr=wrT, wk=wkT,
                         wv=wvT, mur=mur, muk=muk))
    res_B = run_bass_kernel_spmd(_built("B", build_B), in_B,
                                 core_ids=list(range(NCORES)))
    xp = [np.asarray(res_B.results[c]["xp"]) for c in range(NCORES)]

    cwf = np.asarray(L1_attn_conv_w, np.float32).reshape(2048, 4)
    convw = np.zeros((128, 64), np.float32)
    convb = np.zeros((128, 16), np.float32)
    for pt in range(16):
        convw[:, pt * 4:pt * 4 + 4] = cwf[pt * 128:(pt + 1) * 128]
        convb[:, pt] = np.asarray(L1_attn_conv_b, np.float32)[pt * 128:(pt + 1) * 128]
    win = f32c(L1_attn_w_in)
    wB = f32c(L1_attn_w_B)
    wC = f32c(L1_attn_w_C)
    in_C = [dict(x=xp[c], win=win, convw=convw, convb=convb, wB=wB, wC=wC)
            for c in range(NCORES)]
    res_C = run_bass_kernel_spmd(_built("C", build_C), in_C,
                                 core_ids=list(range(NCORES)))

    wout = f32c(L1_attn_w_out)
    decay = _decay_const()
    th, thn = f32c(L1_ffn_theta), f32c(L1_ffn_theta_noise)
    ph, phn = f32c(L1_ffn_phi), f32c(L1_ffn_phi_noise)
    zc = np.zeros((16, 1), np.float32)
    in_D = []
    for c in range(NCORES):
        b, u = c // 2, c % 2
        carry = np.asarray(res_C.results[2 * b]["carry"], np.float32) if u == 1 else zc
        in_D.append(dict(y0=np.asarray(res_C.results[c]["y0"]),
                         sz=np.asarray(res_C.results[c]["sz"]),
                         carry=carry,
                         x=np.ascontiguousarray(xp[c][:, HALO:]),
                         wout=wout, wc2=wC, decay=decay, th=th, thn=thn,
                         ph=ph, phn=phn))
    res_D = run_bass_kernel_spmd(_built("D", build_D), in_D,
                                 core_ids=list(range(NCORES)))

    out = np.empty((4, T, D), np.float32)
    for c in range(NCORES):
        b, u = c // 2, c % 2
        out[b, u * OWN:(u + 1) * OWN, :] = \
            np.asarray(res_D.results[c]["out"], np.float32).T
    return out


# version 06
# speedup vs baseline: 1.4107x; 1.4107x over previous
import math
import os
from contextlib import ExitStack

import numpy as np
import ml_dtypes

import concourse.bass as bass
import concourse.mybir as mybir
from concourse.bass_utils import run_bass_kernel_spmd

F32 = mybir.dt.float32
BF16 = mybir.dt.bfloat16
AF = mybir.ActivationFunctionType
ALU = mybir.AluOpType

NCORES = 8
D = 1024
T = 4096
HALO = 4
OWN = T // 2
L = HALO + OWN
NH = 8
HD = 64
LAMBDA_INIT = 0.8 - 0.6 * math.exp(-0.3 * (4 - 1))
EPS = 1e-5
ATT_BIAS = -12.0
PAIRS = [[0, 1], [2, 3], [4, 5], [6, 7]]
ENGS = ("sync", "gpsimd", "vector", "scalar", "tensor")
BF = ml_dtypes.bfloat16


def chunks_of(total, step=512):
    out, pos = [], 0
    r = total % step
    if r:
        out.append((0, r))
        pos = r
    while pos < total:
        out.append((pos, step))
        pos += step
    return out


GRID_L = [(0, 512), (512, 512), (1024, 512), (1536, 512), (1540, 512)]
GRID_OWN = [(0, 512), (512, 512), (1024, 512), (1536, 512)]


class Sem:
    def __init__(self, handle):
        self.h = handle
        self.n = 0

    def bump(self, k=1):
        self.n += k
        return self.n


class Prog:

    def __init__(self, nc, ctx):
        self.nc = nc
        self.ctx = ctx
        self.ops = {e: [] for e in ENGS}
        self._semid = 0

    def sem(self, name=None):
        self._semid += 1
        nm = f"{name}_{self._semid}" if name else f"s{self._semid}"
        h = self.ctx.enter_context(self.nc.semaphore(nm))
        return Sem(h)

    def raw(self, eng, fn):
        self.ops[eng].append(fn)

    def rec(self, eng, build, wait=(), inc=()):
        waits = [(s.h, v) for s, v in wait]
        incs = []
        for s, k in inc:
            s.bump(k)
            incs.append((s.h, k))

        def f(e):
            for h, v in waits:
                e.wait_ge(h, v)
            ins = build(e)
            if incs:
                assert ins is not None
                for h, k in incs:
                    ins.then_inc(h, k)
        return (eng, f)

    def emit_rec(self, rec):
        eng, f = rec
        self.ops[eng].append(f)

    def op(self, eng, build, wait=(), inc=()):
        waits = [(s.h, v) for s, v in wait]
        incs = []
        for s, k in inc:
            s.bump(k)
            incs.append((s.h, k))

        def f(e):
            for h, v in waits:
                e.wait_ge(h, v)
            ins = build(e)
            if incs:
                assert ins is not None
                for h, k in incs:
                    ins.then_inc(h, k)
        self.ops[eng].append(f)

    def play(self):
        with self.nc.Block() as block:
            @block.sync
            def _(e):
                for f in self.ops["sync"]:
                    f(e)

            @block.gpsimd
            def _(e):
                for f in self.ops["gpsimd"]:
                    f(e)

            @block.vector
            def _(e):
                for f in self.ops["vector"]:
                    f(e)

            @block.scalar
            def _(e):
                for f in self.ops["scalar"]:
                    f(e)

            @block.tensor
            def _(e):
                for f in self.ops["tensor"]:
                    f(e)


def dma_tiles(P, eng, sem, dram, tiles, n, wait=()):
    dv = dram.rearrange("(a p) t -> a p t", p=128)
    waits = [(s.h, v) for s, v in wait]

    def f(e):
        for h, v in waits:
            e.wait_ge(h, v)
        for i in range(n):
            e.dma_start(out=tiles[i][:, :], in_=dv[i]).then_inc(sem.h, 16)
    sem.bump(16 * n)
    P.raw(eng, f)


class Common:
    def __init__(self, nc, ctx, P):
        sb = lambda nm, s, d: ctx.enter_context(nc.sbuf_tensor(nm, s, d))
        self.ones128f = sb("ones128f", [128, 1], F32)
        self.ones128b = sb("ones128b", [128, 1], BF16)
        self.ones1f = sb("ones1f", [1, 128], F32)
        self.consts = sb("constsA", [128, 12], F32)
        self.s_init = P.sem("s_init")
        cvals = [EPS, -0.5, ATT_BIAS, 0.125, LAMBDA_INIT, 0.0, math.pi / 2, 0.5, math.log(1.0 - LAMBDA_INIT)]
        self.cidx = {v: i for i, v in enumerate(cvals)}

        def v_init(v):
            v.memset(self.ones128f[:, :], 1.0)
            v.memset(self.ones128b[:, :], 1.0)
            v.memset(self.ones1f[:, :], 1.0)
            ins = None
            for val, i in self.cidx.items():
                ins = v.memset(self.consts[:, i:i + 1], val)
            return ins
        P.op("vector", v_init, inc=[(self.s_init, 1)])

    def c(self, val):
        i = self.cidx[val]
        return self.consts[:, i:i + 1]


def emit_ln(P, C, x_tiles, out_tiles, chs, scr, pre_wait, nft=8, sems=None):
    if sems is None:
        sems = tuple(P.sem() for _ in range(6))
    s_sq, s_st, s_row, s_rs, s_bc, s_app = sems
    sq, st_ps, bc_ps = scr["sq"], scr["st_ps"], scr["bc_ps"]
    r_mu, r_mu2, r_var, r_rstd = scr["rows"]
    inv_d = 1.0 / (128 * nft)
    app_vals = []
    for k, (off, w) in enumerate(chs):
        pw = list(pre_wait) if k == 0 else []

        def b_sq(s, off=off, w=w):
            ins = None
            for i in range(nft):
                ins = s.activation(sq[i][:, :w], x_tiles[i][:, off:off + w],
                                   AF.Square)
            return ins
        gate = [(s_st, s_st.n), (s_app, s_app.n)]
        P.op("scalar", b_sq, wait=pw + gate, inc=[(s_sq, 1)])

        def b_st(t, off=off, w=w):
            ins = None
            for i in range(nft):
                ins = t.matmul(st_ps[0:1, :w], C.ones128f[:, :],
                               x_tiles[i][:, off:off + w], start=(i == 0),
                               stop=(i == nft - 1), tile_position=(0, 0))
            return ins
        P.op("tensor", b_st, wait=pw + [(s_row, s_row.n)])

        def b_st2(t, w=w):
            ins = None
            for i in range(nft):
                ins = t.matmul(st_ps[32:33, :w], C.ones128b[:, :], sq[i][:, :w],
                               start=(i == 0), stop=(i == nft - 1),
                               tile_position=(0, 32))
            return ins
        P.op("tensor", b_st2, wait=[(s_sq, s_sq.n)], inc=[(s_st, 1)])

        def b_row(v, w=w):
            v.tensor_scalar_mul(r_mu[:, :w], st_ps[0:1, :w], inv_d)
            v.tensor_mul(r_mu2[:, :w], r_mu[:, :w], r_mu[:, :w])
            return v.scalar_tensor_tensor(r_var[:, :w], st_ps[32:33, :w], inv_d,
                                          r_mu2[:, :w], ALU.mult, ALU.subtract)
        P.op("vector", b_row, wait=[(s_st, s_st.n), (s_bc, s_bc.n)],
             inc=[(s_row, 1)])

        def b_rs(s, w=w):
            s.activation(r_rstd[:, :w], r_var[:, :w], AF.Ln,
                         bias=C.c(EPS)[0:1, :])
            return s.activation(r_rstd[:, :w], r_rstd[:, :w], AF.Exp,
                                scale=C.c(-0.5)[0:1, :])
        P.op("scalar", b_rs, wait=[(s_row, s_row.n)], inc=[(s_rs, 1)])

        def b_bc(t, w=w):
            t.matmul(bc_ps[0][:, :w], C.ones1f[:, :], r_mu[:, :w], start=True,
                     stop=True)
            return t.matmul(bc_ps[1][:, :w], C.ones1f[:, :], r_rstd[:, :w],
                            start=True, stop=True)
        P.op("tensor", b_bc, wait=[(s_rs, s_rs.n), (s_app, s_app.n)],
             inc=[(s_bc, 1)])

        def b_app(v, off=off, w=w):
            ins = None
            for i in range(nft):
                v.tensor_sub(sq[i][:, :w], x_tiles[i][:, off:off + w],
                             bc_ps[0][:, :w])
                ins = v.tensor_mul(out_tiles[i][:, off:off + w], sq[i][:, :w],
                                   bc_ps[1][:, :w])
            return ins
        P.op("vector", b_app, wait=[(s_bc, s_bc.n)], inc=[(s_app, 1)])
        app_vals.append(s_app.n)
    return s_app, app_vals



def build_B():
    nc = bass.Bass("TRN2", target_bir_lowering=False)
    ynf = nc.dram_tensor("ynf", [2048, L], BF16, kind="ExternalInput")
    cw = nc.dram_tensor("cw", [2048, D], F32, kind="ExternalInput")
    x_in = nc.dram_tensor("x", [D, L], F32, kind="ExternalInput")
    wr = nc.dram_tensor("wr", [D, D], F32, kind="ExternalInput")
    wk = nc.dram_tensor("wk", [D, D], F32, kind="ExternalInput")
    wv = nc.dram_tensor("wv", [D, D], F32, kind="ExternalInput")
    mur = nc.dram_tensor("mur", [128, 8], F32, kind="ExternalInput")
    muk = nc.dram_tensor("muk", [128, 8], F32, kind="ExternalInput")
    xp_out = nc.dram_tensor("xp", [D, L], F32, kind="ExternalOutput")

    with ExitStack() as ctx:
        P = Prog(nc, ctx)
        C = Common(nc, ctx, P)
        sb = lambda nm, s, d: ctx.enter_context(nc.sbuf_tensor(nm, s, d))
        psum = lambda nm, s: ctx.enter_context(nc.psum_tensor(nm, s, F32))

        x_sb = [sb(f"x{i}", [128, L], F32) for i in range(8)]
        sq = [sb(f"sq{i}", [128, 512], BF16) for i in range(8)]
        rows = [sb(f"row{i}", [1, 512], F32) for i in range(4)]
        st_ps = psum("st_ps", [64, 512])
        bc_ps = [psum("bc0", [128, 512]), psum("bc1", [128, 512])]
        mm_ps = [psum(f"mm{i}", [128, 512]) for i in range(4)]

        with ExitStack() as c1:
            sb1 = lambda nm, s, d: c1.enter_context(nc.sbuf_tensor(nm, s, d))
            ynf_sb = [sb1(f"ynf{i}", [128, L], BF16) for i in range(16)]
            cw_sb = [sb1(f"cwt{i}", [128, D], BF16) for i in range(16)]
            xin_st = [sb1(f"xis{i}", [128, 512], F32) for i in range(8)]
            s_ld1 = P.sem("s_ld1")
            s_lx = P.sem("s_lx")
            s_cw = P.sem("s_cw")
            s_res = P.sem("s_res")
            dma_tiles(P, "gpsimd", s_ld1, ynf, ynf_sb, 16)
            dma_tiles(P, "gpsimd", s_ld1, cw, cw_sb, 16)

            xiv = x_in.rearrange("(a p) t -> a p t", p=128)
            nblk = 0
            for k, (off, w) in enumerate(GRID_L):
                def b_lx(sy, off=off, w=w, k=k, resn=s_res.n):
                    if k > 0:
                        sy.wait_ge(s_res.h, resn)
                    for i in range(8):
                        sy.dma_start(out=xin_st[i][:, :w],
                                     in_=xiv[i][:, off:off + w]
                                     ).then_inc(s_lx.h, 16)
                s_lx.bump(16 * 8)
                P.raw("sync", b_lx)
                for dm in range(8):
                    bank = mm_ps[nblk % 4]

                    def b_cw(t, dm=dm, off=off, w=w, bank=bank):
                        ins = None
                        for d2 in range(16):
                            ins = t.matmul(bank[:, :w],
                                           cw_sb[d2][:, dm * 128:(dm + 1) * 128],
                                           ynf_sb[d2][:, off:off + w],
                                           start=(d2 == 0), stop=(d2 == 15))
                        return ins
                    wt = [(s_ld1, s_ld1.n)] if nblk == 0 else []
                    if nblk >= 4:
                        wt.append((s_res, nblk - 3))
                    P.op("tensor", b_cw, wait=wt, inc=[(s_cw, 1)])

                    def b_res(v, dm=dm, off=off, w=w, bank=bank):
                        return v.tensor_add(x_sb[dm][:, off:off + w],
                                            xin_st[dm][:, :w],
                                            bank[:, :w])
                    P.op("vector", b_res, wait=[(s_cw, s_cw.n), (s_lx, s_lx.n)],
                         inc=[(s_res, 1)])
                    nblk += 1
            res_done = (s_res, s_res.n)
            cw_done = (s_cw, s_cw.n)
            P.op("gpsimd", lambda g: None, wait=[cw_done, res_done])

        emit_ln2_ffn0(nc, ctx, P, C, dict(x_sb=x_sb, sq=sq, rows=rows,
                                          st_ps=st_ps, bc_ps=bc_ps,
                                          mm_ps=mm_ps),
                      dict(wr=wr, wk=wk, wv=wv, mur=mur, muk=muk,
                           xp_out=xp_out), res_done)
        P.play()
    return nc


def emit_ln2_ffn0(nc, ctx, P, C, scr, io, res_done):
    x_sb, sq, rows = scr["x_sb"], scr["sq"], scr["rows"]
    st_ps, bc_ps, mm_ps = scr["st_ps"], scr["bc_ps"], scr["mm_ps"]
    wr, wk, wv, mur, muk = io["wr"], io["wk"], io["wv"], io["mur"], io["muk"]
    xp_out = io["xp_out"]
    if True:
        with ExitStack() as c2:
            sb2 = lambda nm, s, d: c2.enter_context(nc.sbuf_tensor('F_' + nm, s, d))
            wr_sb = [sb2(f"wrt{i}", [128, D], BF16) for i in range(8)]
            wk_sb = [sb2(f"wkt{i}", [128, D], BF16) for i in range(8)]
            wv_sb = [sb2(f"wvt{i}", [128, D], BF16) for i in range(8)]
            mu_sb = sb2("mu", [128, 16], F32)
            lny = [sb2(f"lny{i}", [128, L], BF16) for i in range(8)]
            xm_sb = [sb2(f"xm{i}", [128, 512], BF16) for i in range(8)]
            xr_sb = xm_sb
            xk_sb = xm_sb
            r_sb = [sb2(f"r{i}", [128, 512], BF16) for i in range(8)]
            ka_sb = [sb2(f"ka{i}", [128, 512], BF16) for i in range(8)]
            tmp_sb = sb2("tmpv", [128, 512], F32)
            xp_st = [sb2(f"xps{i}", [128, 512], F32) for i in range(8)]
            s_ld2 = P.sem("s_ld2")
            dma_tiles(P, "gpsimd", s_ld2, wr, wr_sb, 8)
            dma_tiles(P, "gpsimd", s_ld2, wk, wk_sb, 8)
            dma_tiles(P, "gpsimd", s_ld2, wv, wv_sb, 8)

            def b_mu(g):
                g.dma_start(out=mu_sb[:, 0:8], in_=mur[:, :]).then_inc(s_ld2.h, 16)
                g.dma_start(out=mu_sb[:, 8:16], in_=muk[:, :]).then_inc(s_ld2.h, 16)
            s_ld2.bump(32)
            P.raw("gpsimd", b_mu)

            pre = [res_done, (C.s_init, 1)]
            s_app, app_vals = emit_ln(P, C, x_sb, lny, GRID_L,
                                      dict(sq=sq, st_ps=st_ps, bc_ps=bc_ps,
                                           rows=rows), pre)

            s_mix = P.sem("s_mix")
            s_r = P.sem("s_r")
            s_k = P.sem("s_k")
            s_ka = P.sem("s_ka")
            s_mmf = P.sem("s_mmf")
            s_out = P.sem("s_out")
            s_sto = P.sem("s_sto")

            for k, (off, w) in enumerate(GRID_L):
                def mk_mix(cols):
                    def b_mix(v, off=off, w=w, cols=cols):
                        ins = None
                        for i in range(8):
                            muc = mu_sb[:, cols + i:cols + i + 1]
                            if off == 0:
                                v.tensor_copy(xm_sb[i][:, 0:1], lny[i][:, 0:1])
                                o, wo = 1, w - 1
                            else:
                                o, wo = off, w
                            cur = lny[i][:, o:o + wo]
                            prv = lny[i][:, o - 1:o - 1 + wo]
                            dm_ = xm_sb[i][:, o - off:o - off + wo]
                            v.tensor_sub(dm_, cur, prv)
                            ins = v.scalar_tensor_tensor(dm_, dm_, muc, prv,
                                                         ALU.mult, ALU.add)
                        return ins
                    return b_mix
                P.op("vector", mk_mix(0),
                     wait=[(s_app, app_vals[k]), (s_ld2, s_ld2.n),
                           (s_mmf, s_mmf.n)],
                     inc=[(s_mix, 1)])

                for dt in range(8):
                    bank = mm_ps[dt % 2]

                    def b_rmm(t, dt=dt, w=w, bank=bank):
                        ins = None
                        for ct in range(8):
                            ins = t.matmul(bank[:, :w],
                                           wr_sb[ct][:, dt * 128:(dt + 1) * 128],
                                           xr_sb[ct][:, :w],
                                           start=(ct == 0), stop=(ct == 7))
                        return ins
                    wt = [(s_mix, s_mix.n)] if dt == 0 else []
                    if k == 0 and dt == 0:
                        wt.append(res_done)
                    if s_r.n >= 2:
                        wt.append((s_r, s_r.n - 1))
                    P.op("tensor", b_rmm, wait=wt, inc=[(s_mmf, 1)])

                    def b_rev(s, dt=dt, w=w, bank=bank):
                        return s.activation(r_sb[dt][:, :w], bank[:, :w],
                                            AF.Sigmoid)
                    wt = [(s_mmf, s_mmf.n)]
                    if k > 0:
                        wt.append((s_out, 8 * (k - 1) + dt + 1))
                    P.op("scalar", b_rev, wait=wt, inc=[(s_r, 1)])

                P.op("vector", mk_mix(8), wait=[(s_mmf, s_mmf.n)],
                     inc=[(s_mix, 1)])
                for dt in range(8):
                    bank = mm_ps[2 + dt % 2]

                    def b_kmm(t, dt=dt, w=w, bank=bank):
                        ins = None
                        for ct in range(8):
                            ins = t.matmul(bank[:, :w],
                                           wk_sb[ct][:, dt * 128:(dt + 1) * 128],
                                           xk_sb[ct][:, :w],
                                           start=(ct == 0), stop=(ct == 7))
                        return ins
                    wt = [(s_mix, s_mix.n)] if dt == 0 else []
                    if s_k.n >= 2:
                        wt.append((s_k, s_k.n - 1))
                    P.op("tensor", b_kmm, wait=wt, inc=[(s_mmf, 1)])

                    def b_kev(s, dt=dt, w=w, bank=bank):
                        return s.activation(ka_sb[dt][:, :w], bank[:, :w],
                                            AF.Relu)
                    P.op("scalar", b_kev, wait=[(s_mmf, s_mmf.n)],
                         inc=[(s_k, 1)])

                def b_kasq(v, w=w):
                    ins = None
                    for dt in range(8):
                        ins = v.tensor_mul(ka_sb[dt][:, :w], ka_sb[dt][:, :w],
                                           ka_sb[dt][:, :w])
                    return ins
                P.op("vector", b_kasq, wait=[(s_k, s_k.n)], inc=[(s_ka, 1)])

                for dt in range(8):
                    bank = mm_ps[dt % 2]

                    def b_vmm(t, dt=dt, w=w, bank=bank):
                        ins = None
                        for ct in range(8):
                            ins = t.matmul(bank[:, :w],
                                           wv_sb[ct][:, dt * 128:(dt + 1) * 128],
                                           ka_sb[ct][:, :w],
                                           start=(ct == 0), stop=(ct == 7))
                        return ins
                    wt = [(s_ka, s_ka.n), (s_r, s_r.n)] if dt == 0 else []
                    if dt >= 2:
                        wt.append((s_out, s_out.n - 1))
                    P.op("tensor", b_vmm, wait=wt, inc=[(s_mmf, 1)])

                    def b_vout(v, dt=dt, off=off, w=w, bank=bank):
                        v.tensor_mul(tmp_sb[:, :w], r_sb[dt][:, :w],
                                     bank[:, :w])
                        return v.tensor_add(xp_st[dt][:, :w],
                                            x_sb[dt][:, off:off + w],
                                            tmp_sb[:, :w])
                    wt = [(s_mmf, s_mmf.n)]
                    if k > 0:
                        wt.append((s_sto, 16 * (8 * (k - 1) + dt + 1)))
                    P.op("vector", b_vout, wait=wt, inc=[(s_out, 1)])

                xv = xp_out.rearrange("(a p) t -> a p t", p=128)
                wt_sto = (s_out.h, s_out.n)

                def b_sto(sy, off=off, w=w, wt=wt_sto, xv=xv):
                    sy.wait_ge(*wt)
                    for i in range(8):
                        sy.dma_start(out=xv[i][:, off:off + w],
                                     in_=xp_st[i][:, :w]
                                     ).then_inc(s_sto.h, 16)
                s_sto.bump(16 * 8)
                P.raw("sync", b_sto)

            P.op("sync", lambda e: None, wait=[(s_sto, s_sto.n)])
            return dict(s_sto=s_sto, s_out=s_out, s_mmf=s_mmf, s_app=s_app,
                        s_r=s_r, s_k=s_k, s_ka=s_ka, s_mix=s_mix)



def own_store_range(off, w):
    own_lo = max(0, off - HALO)
    src_lo = own_lo - (off - HALO)
    return src_lo, own_lo, w - src_lo


def declare_C(nc, kind_x="ExternalInput", kind_out="ExternalOutput",
              xname="xc", skip_x=False):
    return dict(
        x_in=None if skip_x else nc.dram_tensor(xname, [D, L], F32,
                                                kind=kind_x),
        win=nc.dram_tensor("win", [D, 4096], F32, kind="ExternalInput"),
        convw=nc.dram_tensor("convw", [128, 64], F32, kind="ExternalInput"),
        convb=nc.dram_tensor("convb", [128, 16], F32, kind="ExternalInput"),
        wB=nc.dram_tensor("wB", [2048, 16], F32, kind="ExternalInput"),
        wC=nc.dram_tensor("wC", [16, 2048], F32, kind="ExternalInput"),
        y0_out=nc.dram_tensor("y0", [2048, OWN], BF16, kind=kind_out),
        sz_out=nc.dram_tensor("sz", [2048, OWN], BF16, kind=kind_out),
        carry_out=nc.dram_tensor("carry", [16, 1], F32, kind=kind_out),
    )


def build_C():
    nc = bass.Bass("TRN2", target_bir_lowering=False)
    io = declare_C(nc, kind_x="ExternalInput", xname="x")
    with ExitStack() as ctx:
        P = Prog(nc, ctx)
        C = Common(nc, ctx, P)
        emit_C(nc, ctx, P, C, io, pre=())
        P.play()
    return nc


def emit_C(nc, ctx, P, C, io, pre=()):
    x_in, win, convw, convb = io["x_in"], io["win"], io["convw"], io["convb"]
    wB, wC = io["wB"], io["wC"]
    y0_out, sz_out, carry_out = io["y0_out"], io["sz_out"], io["carry_out"]
    if True:
        sb = lambda nm, s, d: ctx.enter_context(nc.sbuf_tensor('C_' + nm, s, d))
        psum = lambda nm, s: ctx.enter_context(nc.psum_tensor('C_' + nm, s, F32))

        lnz = [sb(f"lnz{i}", [128, L], BF16) for i in range(8)]
        xp_full = [sb(f"xpf{i}", [128, L], BF16) for i in range(16)]
        s_sto = P.sem("s_sto")

        st_ps = psum("st_ps", [64, 512])
        bc_ps = [psum("bc0", [128, 512]), psum("bc1", [128, 512])]
        with ExitStack() as c1:
            sb1 = lambda nm, s, d: c1.enter_context(nc.sbuf_tensor('C_' + nm, s, d))
            x_sb = [sb1(f"x{i}", [128, L], F32) for i in range(8)]
            sq = [sb1(f"sq{i}", [128, 512], BF16) for i in range(8)]
            rows = [sb1(f"row{i}", [1, 512], F32) for i in range(4)]
            s_ldx = P.sem("s_ldx")
            dma_tiles(P, "sync", s_ldx, x_in, x_sb, 8, wait=pre)
            pre = [(s_ldx, s_ldx.n), (C.s_init, 1)]
            s_app, app_vals = emit_ln(P, C, x_sb, lnz, GRID_L,
                                      dict(sq=sq, st_ps=st_ps, bc_ps=bc_ps,
                                           rows=rows), pre)
            ln_done = (s_app, s_app.n)
            ln_sem, ln_app_vals = s_app, app_vals
            P.op("gpsimd", lambda g: None, wait=[ln_done])

        mm_ps = [psum(f"mm{i}", [128, 512]) for i in range(4)]
        with ExitStack() as c2:
            sb2 = lambda nm, s, d: c2.enter_context(nc.sbuf_tensor('C_' + nm, s, d))
            wi_sb = [sb2(f"wi{i}", [128, 4096], BF16) for i in range(8)]
            sz_st = [sb2(f"szs{i}", [128, 512], BF16) for i in range(16)]
            s_ldw = P.sem("s_ldw")
            dma_tiles(P, "gpsimd", s_ldw, win, wi_sb, 8)

            s_wmm = P.sem("s_wmm")
            s_xp = P.sem("s_xp")
            nblk = 0
            for k, (off, w) in enumerate(GRID_L):
                for dt in range(32):
                    bank = mm_ps[nblk % 4]

                    def b_wmm(t, dt=dt, off=off, w=w, bank=bank):
                        ins = None
                        for ct in range(8):
                            ins = t.matmul(bank[:, :w],
                                           wi_sb[ct][:, dt * 128:(dt + 1) * 128],
                                           lnz[ct][:, off:off + w],
                                           start=(ct == 0), stop=(ct == 7))
                        return ins
                    wt = [(s_ldw, s_ldw.n)] if nblk == 0 else []
                    if dt == 0:
                        wt.append((ln_sem, ln_app_vals[k]))
                    if nblk >= 4:
                        wt.append((s_xp, nblk - 3))
                    P.op("tensor", b_wmm, wait=wt, inc=[(s_wmm, 1)])

                    if dt < 16:
                        def b_xev(v, dt=dt, off=off, w=w, bank=bank):
                            return v.tensor_copy(xp_full[dt][:, off:off + w],
                                                 bank[:, :w])
                        P.op("vector", b_xev, wait=[(s_wmm, s_wmm.n)],
                             inc=[(s_xp, 1)])
                    else:
                        def b_zev(s, dt=dt, w=w, bank=bank):
                            return s.activation(sz_st[dt - 16][:, :w],
                                                bank[:, :w], AF.Silu)
                        wt = [(s_wmm, s_wmm.n)]
                        if k > 0:
                            wt.append((s_sto, 16 * 16 * k))
                        P.op("scalar", b_zev, wait=wt, inc=[(s_xp, 1)])
                    nblk += 1

                src_lo, own_lo, ww = own_store_range(off, w)
                zv = sz_out.rearrange("(a p) t -> a p t", p=128)
                wt_sto = (s_xp.h, s_xp.n)

                def b_szst(sy, src_lo=src_lo, own_lo=own_lo, ww=ww, wt=wt_sto,
                           zv=zv):
                    sy.wait_ge(*wt)
                    for i in range(16):
                        sy.dma_start(out=zv[i][:, own_lo:own_lo + ww],
                                     in_=sz_st[i][:, src_lo:src_lo + ww]
                                     ).then_inc(s_sto.h, 16)
                s_sto.bump(16 * 16)
                P.raw("sync", b_szst)
            wmm_done = (s_wmm, s_wmm.n)
            evac_done = (s_xp, s_xp.n)
            szsto_done = (s_sto, s_sto.n)
            P.op("gpsimd", lambda g: None,
                 wait=[wmm_done, evac_done, szsto_done])

        with ExitStack() as c3:
            sb3 = lambda nm, s, d: c3.enter_context(nc.sbuf_tensor('C_' + nm, s, d))
            cw_sb = sb3("cwv", [128, 64], F32)
            cb_sb = sb3("cbv", [128, 16], F32)
            wb_sb = [sb3(f"wb{i}", [128, 16], BF16) for i in range(16)]
            wc_sb = sb3("wcv", [16, 2048], BF16)
            c09 = sb3("c09", [16, 512], F32)
            zini = sb3("zini", [16, 1], F32)
            xc_st = [sb3(f"xc{i}", [128, 512], BF16) for i in range(16)]
            acc = [sb3(f"acc{i}", [128, 512], F32) for i in range(4)]
            u01 = sb3("u01", [16, 512], F32)
            h_sec = [sb3(f"h{i}", [16, 512], F32) for i in range(4)]
            hb = sb3("hb", [16, 512], BF16)
            hpad = sb3("hpad", [16, 512], F32)
            y0_st = [sb3(f"y0s{i}", [128, 512], BF16) for i in range(16)]
            ups = psum("ups", [16, 512])
            s_ld3 = P.sem("s_ld3")
            dma_tiles(P, "gpsimd", s_ld3, wB, wb_sb, 16)

            def b_ld3(g):
                g.dma_start(out=cw_sb[:, :], in_=convw[:, :]).then_inc(s_ld3.h, 16)
                g.dma_start(out=cb_sb[:, :], in_=convb[:, :]).then_inc(s_ld3.h, 16)
                g.dma_start(out=wc_sb[:, :], in_=wC[:, :]).then_inc(s_ld3.h, 16)
            s_ld3.bump(48)
            P.raw("gpsimd", b_ld3)

            s_c09 = P.sem("s_c09")

            def b_c09(v):
                v.memset(c09[:, :], 0.9)
                return v.memset(zini[:, :], 0.0)
            P.op("vector", b_c09, wait=[evac_done, szsto_done],
                 inc=[(s_c09, 1)])

            s_conv = P.sem("s_conv")
            s_xc = P.sem("s_xc")
            s_u = P.sem("s_u")
            s_scan = P.sem("s_scan")
            s_ymm = P.sem("s_ymm")
            s_yev = P.sem("s_yev")

            for si, (soff, sw) in enumerate(GRID_OWN):
                for pt in range(16):
                    a = acc[pt % 4]

                    def b_conv(v, soff=soff, pt=pt, a=a):
                        v.tensor_scalar(a[:, :], xp_full[pt][:, soff + 1:soff + 513],
                                        cw_sb[:, pt * 4:pt * 4 + 1], None,
                                        ALU.mult)
                        ins = None
                        for j in (1, 2, 3):
                            srcv = xp_full[pt][:, soff + 1 + j:soff + 513 + j]
                            ins = v.scalar_tensor_tensor(
                                a[:, :], srcv, cw_sb[:, pt * 4 + j:pt * 4 + j + 1],
                                a[:, :], ALU.mult, ALU.add)
                        return ins
                    wt = []
                    if si == 0 and pt == 0:
                        wt += [(s_ld3, s_ld3.n), (s_c09, 1)]
                    if s_conv.n >= 4:
                        wt.append((s_xc, s_conv.n - 3))
                    if si > 0 and pt == 0:
                        wt.append((s_ymm, s_ymm.n))
                    P.op("vector", b_conv, wait=wt, inc=[(s_conv, 1)])

                    def b_silu(s, pt=pt, a=a):
                        return s.activation(xc_st[pt][:, :], a[:, :], AF.Silu,
                                            bias=cb_sb[:, pt:pt + 1])
                    wt = [(s_conv, s_conv.n)]
                    if si > 0 and pt == 0:
                        wt.append((s_u, si))
                    P.op("scalar", b_silu, wait=wt, inc=[(s_xc, 1)])

                def b_umm(t, si=si):
                    ins = None
                    for ct in range(16):
                        ins = t.matmul(ups[:, :], wb_sb[ct][:, :],
                                       xc_st[ct][:, :], start=(ct == 0),
                                       stop=(ct == 15))
                    return ins
                P.op("tensor", b_umm, wait=[(s_xc, s_xc.n), (s_scan, si)],
                     inc=[(s_u, 1)])

                def b_scan(v, si=si):
                    v.tensor_scalar_mul(u01[:, :], ups[:, :], 0.1)
                    init = zini[:, 0:1] if si == 0 else h_sec[si - 1][:, 511:512]
                    v.tensor_tensor_scan(h_sec[si][:, :], c09[:, :], u01[:, :],
                                         init, ALU.mult, ALU.add)
                    return v.tensor_copy(hb[:, :], h_sec[si][:, :])
                P.op("vector", b_scan, wait=[(s_u, s_u.n), (s_ymm, s_ymm.n)],
                     inc=[(s_scan, 1)])

                for dt in range(16):
                    bank = mm_ps[dt % 2]

                    def b_ymm(t, dt=dt, bank=bank):
                        return t.matmul(bank[:, :], wc_sb[:, dt * 128:(dt + 1) * 128],
                                        hb[:, :], start=True, stop=True)
                    wt = [(s_scan, s_scan.n)] if dt == 0 else []
                    if s_ymm.n >= 2:
                        wt.append((s_yev, s_ymm.n - 1))
                    P.op("tensor", b_ymm, wait=wt, inc=[(s_ymm, 1)])

                    def b_yev(v, dt=dt, bank=bank):
                        return v.tensor_copy(y0_st[dt][:, :], bank[:, :])
                    wt = [(s_ymm, s_ymm.n)]
                    if si > 0:
                        wt.append((s_sto, 16 * (16 * 5 + 16 * (si - 1) + dt + 1)))
                    P.op("vector", b_yev, wait=wt, inc=[(s_yev, 1)])

                yv = y0_out.rearrange("(a p) t -> a p t", p=128)
                wt_sto = (s_yev.h, s_yev.n)

                def b_yst(sy, soff=soff, wt=wt_sto, yv=yv):
                    sy.wait_ge(*wt)
                    for i in range(16):
                        sy.dma_start(out=yv[i][:, soff:soff + 512],
                                     in_=y0_st[i][:, :]).then_inc(s_sto.h, 16)
                s_sto.bump(16 * 16)
                P.raw("sync", b_yst)

            s_cy = P.sem("s_cy")

            def b_hpad(v):
                return v.tensor_copy(hpad[:, :], h_sec[3][:, :])
            P.op("vector", b_hpad, wait=[(s_scan, s_scan.n)], inc=[(s_cy, 1)])

            def b_cyst(sy):
                sy.wait_ge(s_cy.h, 1)
                sy.dma_start(out=carry_out[:, :], in_=hpad[:, 511:512]
                             ).then_inc(s_sto.h, 16)
            s_sto.bump(16)
            P.raw("sync", b_cyst)

            P.op("sync", lambda e: None, wait=[(s_sto, s_sto.n)])
            return dict(s_sto=s_sto, s_scan=s_scan, s_yev=s_yev, s_ymm=s_ymm,
                        s_u=s_u, s_xc=s_xc, s_conv=s_conv, s_cy=s_cy)



def declare_D(nc, kind_in="ExternalInput", kind_x="ExternalInput",
              xname="xd", carryname="carryin", skip_io=False):
    return dict(
        y0_in=None if skip_io else nc.dram_tensor("y0", [2048, OWN], BF16,
                                                  kind=kind_in),
        sz_in=None if skip_io else nc.dram_tensor("sz", [2048, OWN], BF16,
                                                  kind=kind_in),
        carry_in=None if skip_io else nc.dram_tensor(carryname, [16, 1], F32,
                                                     kind=kind_in),
        x_in=None if skip_io else nc.dram_tensor(xname, [D, OWN], F32,
                                                 kind=kind_x),
        wout=nc.dram_tensor("wout", [2048, D], F32, kind="ExternalInput"),
        wc2=nc.dram_tensor("wc2", [16, 2048], F32, kind="ExternalInput"),
        decay=nc.dram_tensor("decay", [128, OWN], F32, kind="ExternalInput"),
        th=nc.dram_tensor("th", [D, D], F32, kind="ExternalInput"),
        thn=nc.dram_tensor("thn", [D, D], F32, kind="ExternalInput"),
        ph=nc.dram_tensor("ph", [D, D], F32, kind="ExternalInput"),
        phn=nc.dram_tensor("phn", [D, D], F32, kind="ExternalInput"),
        out_d=nc.dram_tensor("out", [D, OWN], F32, kind="ExternalOutput"),
    )


def build_D():
    nc = bass.Bass("TRN2", target_bir_lowering=False)
    io = declare_D(nc, xname="x", carryname="carry")
    with ExitStack() as ctx:
        P = Prog(nc, ctx)
        C = Common(nc, ctx, P)
        emit_D(nc, ctx, P, C, io, pre=())
        P.play()
    return nc


def emit_D(nc, ctx, P, C, io, pre=(), carry_fold=None):
    y0_in, sz_in, carry_in, x_in = (io["y0_in"], io["sz_in"], io["carry_in"],
                                    io["x_in"])
    wout, wc2, decay = io["wout"], io["wc2"], io["decay"]
    th, thn, ph, phn, out_d = io["th"], io["thn"], io["ph"], io["phn"], io["out_d"]
    if True:
        sb = lambda nm, s, d: ctx.enter_context(nc.sbuf_tensor('E_' + nm, s, d))
        psum = lambda nm, s: ctx.enter_context(nc.psum_tensor('E_' + nm, s, F32))

        xsec = [sb(f"x{i}", [128, 512], F32) for i in range(8)]
        dec_sb = sb("dec", [128, OWN], F32)
        ct_sb = [sb(f"ct{i}", [128, D], BF16) for i in range(8)]
        st_sb = [sb(f"st{i}", [128, D], BF16) for i in range(8)]
        wo_sb = [sb(f"wo{i}", [128, D], BF16) for i in range(16)]
        wc2_sb = sb("wc2v", [16, 2048], BF16)
        cb_sb = sb("cb", [16, 1], BF16)
        cpad = sb("cpad", [16, 128], BF16)
        cwc_sb = sb("cwc", [128, 16], F32)
        y0_st = [sb(f"y0s{i}", [128, 512], BF16) for i in range(16)]
        sz_st = [sb(f"szs{i}", [128, 512], BF16) for i in range(16)]
        yb_st = y0_st
        xo_st = [sb(f"xo{i}", [128, 512], F32) for i in range(8)]
        lnw = [sb(f"lnw{i}", [128, 512], BF16) for i in range(8)]
        sq = [sb(f"sq{i}", [128, 512], BF16) for i in range(8)]
        rows = [sb(f"row{i}", [1, 512], F32) for i in range(4)]
        re2 = sb("re2", [128, 512], F32)
        im2 = sb("im2", [128, 512], F32)
        mag = sb("mag", [128, 512], F32)
        out_st = [sb(f"os{i}", [128, 512], F32) for i in range(8)]
        st_ps = psum("st_ps", [64, 512])
        bc_ps = [psum("bc0", [128, 512]), psum("bc1", [128, 512])]
        mm_ps = [psum(f"mm{i}", [128, 512]) for i in range(4)]
        cwc_ps = psum("cwc_ps", [128, 144])

        s_ldx = P.sem("s_ldx")
        s_ldw = P.sem("s_ldw")
        ln_sems_D = tuple(P.sem() for _ in range(6))
        s_trg = P.sem("s_trg")
        s_sin = P.sem("s_sin")
        s_cwm = P.sem("s_cwm")
        s_cwc = P.sem("s_cwc")
        s_ld = P.sem("s_ld")
        s_cor = P.sem("s_cor")
        s_wmm = P.sem("s_wmm")
        s_xo = P.sem("s_xo")
        s_fmm = P.sem("s_fmm")
        s_sqv = P.sem("s_sqv")
        s_mag = P.sem("s_mag")
        s_fin = P.sem("s_fin")
        s_sto = P.sem("s_sto")

        dma_tiles(P, "gpsimd", s_ldw, wout, wo_sb, 16, wait=pre)
        xv_in = x_in.rearrange("(a p) t -> a p t", p=128)

        def b_ldm(g):
            g.dma_start(out=dec_sb[:, :], in_=decay[:, :]).then_inc(s_ldw.h, 16)
            g.dma_start(out=wc2_sb[:, :], in_=wc2[:, :]).then_inc(s_ldw.h, 16)
            g.dma_start(out=cb_sb[:, :], in_=carry_in[:, :]).then_inc(s_ldw.h, 16)
        s_ldw.bump(48)
        P.raw("gpsimd", b_ldm)

        tctx = ExitStack()
        trg = [tctx.enter_context(nc.sbuf_tensor(f"E_trg{i}", [128, D], F32))
               for i in range(2)]
        tsum = tctx.enter_context(nc.sbuf_tensor("E_tsum", [128, D], F32))
        thv = th.rearrange("(a p) t -> a p t", p=128)
        thnv = thn.rearrange("(a p) t -> a p t", p=128)
        phv = ph.rearrange("(a p) t -> a p t", p=128)
        phnv = phn.rearrange("(a p) t -> a p t", p=128)
        for pt in range(8):
            for which, (av, bv, dst, bias) in enumerate(
                    ((thv, thnv, ct_sb, True), (phv, phnv, st_sb, False))):
                def b_ldt(g, pt=pt, av=av, bv=bv):
                    g.dma_start(out=trg[0][:, :], in_=av[pt]).then_inc(s_trg.h, 16)
                    g.dma_start(out=trg[1][:, :], in_=bv[pt]).then_inc(s_trg.h, 16)
                s_trg.bump(32)
                n_prev = s_sin.n
                def b_gate(g, n_prev=n_prev):
                    if n_prev:
                        g.wait_ge(s_sin.h, n_prev)
                P.raw("gpsimd", b_gate)
                P.raw("gpsimd", b_ldt)

                def b_add(v):
                    return v.tensor_add(tsum[:, :], trg[0][:, :], trg[1][:, :])
                P.op("vector", b_add, wait=[(s_trg, s_trg.n),
                                            (s_sin, s_sin.n)],
                     inc=[(s_trg, 1)])

                if bias:
                    def b_sin(s, pt=pt):
                        return s.activation(tsum[:, :], tsum[:, :], AF.Sin,
                                            bias=C.c(math.pi / 2))
                    P.op("scalar", b_sin, wait=[(s_trg, s_trg.n), (C.s_init, 1)],
                         inc=[(s_trg, 1)])

                    def b_ctd(v, pt=pt, dst=dst):
                        return v.tensor_scalar_add(dst[pt][:, :], tsum[:, :],
                                                   -1.0)
                    P.op("vector", b_ctd, wait=[(s_trg, s_trg.n)],
                         inc=[(s_sin, 1)])
                else:
                    def b_sin(s, pt=pt, dst=dst):
                        return s.activation(dst[pt][:, :], tsum[:, :], AF.Sin)
                    P.op("scalar", b_sin, wait=[(s_trg, s_trg.n), (C.s_init, 1)],
                         inc=[(s_sin, 1)])

        tctx.close()
        trig_done = [(s_sin, s_sin.n), (s_trg, s_trg.n)]

        def b_cini(v):
            if carry_fold is not None:
                carry_fold(v, cb_sb)
            return v.memset(cpad[:, :], 0.0)
        P.op("vector", b_cini, wait=[(s_ldw, s_ldw.n)], inc=[(s_cwm, 1)])

        def b_cwm(t):
            for dt in range(16):
                t.matmul(cwc_ps[:, dt:dt + 1], wc2_sb[:, dt * 128:(dt + 1) * 128],
                         cb_sb[:, :], start=True, stop=True)
            return t.matmul(cwc_ps[:, 16:144], wc2_sb[:, 0:128], cpad[:, :],
                            start=True, stop=True)
        P.op("tensor", b_cwm, wait=[(s_cwm, 1)], inc=[(s_cwm, 1)])

        def b_cwe(v):
            return v.tensor_copy(cwc_sb[:, :], cwc_ps[:, 0:16])
        P.op("vector", b_cwe, wait=[(s_cwm, 2)], inc=[(s_cwc, 1)])

        y0v = y0_in.rearrange("(a p) t -> a p t", p=128)
        szv = sz_in.rearrange("(a p) t -> a p t", p=128)
        ov = out_d.rearrange("(a p) t -> a p t", p=128)

        for si, (soff, sw) in enumerate(GRID_OWN):
            def b_lds(sy, soff=soff, si=si, corn=s_cor.n, xon=s_xo.n,
                      td=list(trig_done)):
                if si == 0:
                    for h, v in td:
                        sy.wait_ge(h.h if isinstance(h, Sem) else h, v)
                if si > 0:
                    sy.wait_ge(s_cor.h, corn)
                    sy.wait_ge(s_xo.h, xon)
                for i in range(16):
                    sy.dma_start(out=y0_st[i][:, :],
                                 in_=y0v[i][:, soff:soff + 512]).then_inc(s_ld.h, 16)
                for i in range(16):
                    sy.dma_start(out=sz_st[i][:, :],
                                 in_=szv[i][:, soff:soff + 512]).then_inc(s_ld.h, 16)
                for i in range(8):
                    sy.dma_start(out=xsec[i][:, :],
                                 in_=xv_in[i][:, soff:soff + 512]).then_inc(s_ld.h, 16)
            s_ld.bump(16 * 40)
            P.raw("sync", b_lds)

            def b_cor(v, soff=soff, si=si):
                ins = None
                for pt in range(16):
                    v.scalar_tensor_tensor(yb_st[pt][:, :],
                                           dec_sb[:, soff:soff + 512],
                                           cwc_sb[:, pt:pt + 1],
                                           y0_st[pt][:, :], ALU.mult, ALU.add)
                    ins = v.tensor_mul(yb_st[pt][:, :], yb_st[pt][:, :],
                                       sz_st[pt][:, :])
                return ins
            wt = [(s_ld, s_ld.n), (s_cwc, 1)]
            if si > 0:
                wt.append((s_wmm, s_wmm.n))
            P.op("vector", b_cor, wait=wt, inc=[(s_cor, 1)])

            for dm in range(8):
                bank = mm_ps[dm % 2]

                def b_wmm(t, dm=dm, bank=bank):
                    ins = None
                    for ct in range(16):
                        ins = t.matmul(bank[:, :],
                                       wo_sb[ct][:, dm * 128:(dm + 1) * 128],
                                       yb_st[ct][:, :],
                                       start=(ct == 0), stop=(ct == 15))
                    return ins
                wt = [(s_cor, s_cor.n)] if dm == 0 else []
                if dm == 0 and si == 0:
                    wt.append((s_ldw, s_ldw.n))
                if s_wmm.n >= 2:
                    wt.append((s_xo, s_wmm.n - 1))
                P.op("tensor", b_wmm, wait=wt, inc=[(s_wmm, 1)])

                def b_xo(v, dm=dm, bank=bank):
                    return v.tensor_add(xo_st[dm][:, :], xsec[dm][:, :],
                                        bank[:, :])
                P.op("vector", b_xo, wait=[(s_wmm, s_wmm.n)], inc=[(s_xo, 1)])

            pre = [(s_xo, s_xo.n), (C.s_init, 1)]
            s_app, _ = emit_ln(P, C, xo_st, lnw, [(0, 512)],
                               dict(sq=sq, st_ps=st_ps, bc_ps=bc_ps,
                                    rows=rows), pre, sems=ln_sems_D)

            for ot in range(8):
                b_re = mm_ps[2]
                b_im = mm_ps[3]

                def b_fmm(t, ot=ot, b_re=b_re, b_im=b_im):
                    for it in range(8):
                        t.matmul(b_re[:, :], ct_sb[it][:, ot * 128:(ot + 1) * 128],
                                 lnw[it][:, :], start=(it == 0), stop=(it == 7))
                    ins = None
                    for it in range(8):
                        ins = t.matmul(b_im[:, :],
                                       st_sb[it][:, ot * 128:(ot + 1) * 128],
                                       lnw[it][:, :], start=(it == 0),
                                       stop=(it == 7))
                    return ins
                wt = [(s_app, s_app.n)] if ot == 0 else []
                if si == 0 and ot == 0:
                    wt.append((s_sin, s_sin.n))
                if s_sqv.n > 0:
                    wt.append((s_sqv, s_sqv.n))
                P.op("tensor", b_fmm, wait=wt, inc=[(s_fmm, 1)])

                def b_sq2(s, b_re=b_re, b_im=b_im):
                    s.activation(re2[:, :], b_re[:, :], AF.Square)
                    return s.activation(im2[:, :], b_im[:, :], AF.Square)
                wt = [(s_fmm, s_fmm.n)]
                if s_sqv.n:
                    wt.append((s_mag, s_mag.n - 1))
                P.op("scalar", b_sq2, wait=wt, inc=[(s_sqv, 1)])

                def b_sum(v):
                    return v.tensor_add(mag[:, :], re2[:, :], im2[:, :])
                wt = [(s_sqv, s_sqv.n)]
                if s_fin.n:
                    wt.append((s_fin, s_fin.n))
                P.op("vector", b_sum, wait=wt, inc=[(s_mag, 1)])

                def b_sqrt(s):
                    s.activation(mag[:, :], mag[:, :], AF.Ln)
                    return s.activation(mag[:, :], mag[:, :], AF.Exp,
                                        scale=C.c(0.5))
                P.op("scalar", b_sqrt, wait=[(s_mag, s_mag.n)],
                     inc=[(s_mag, 1)])

                def b_out(v, ot=ot, si=si):
                    return v.tensor_add(out_st[ot][:, :], xo_st[ot][:, :],
                                        mag[:, :])
                wt = [(s_mag, s_mag.n)]
                if si > 0:
                    wt.append((s_sto, 16 * (8 * (si - 1) + ot + 1)))
                P.op("vector", b_out, wait=wt, inc=[(s_fin, 1)])

            wt_sto = (s_fin.h, s_fin.n)

            def b_sto(sy, soff=soff, wt=wt_sto):
                sy.wait_ge(*wt)
                for i in range(8):
                    sy.dma_start(out=ov[i][:, soff:soff + 512],
                                 in_=out_st[i][:, :]).then_inc(s_sto.h, 16)
            s_sto.bump(16 * 8)
            P.raw("sync", b_sto)

        P.op("sync", lambda e: None, wait=[(s_sto, s_sto.n)])
        return dict(s_sto=s_sto, s_fin=s_fin)



GRID_T = [(i * 512, 512) for i in range(8)]


def declare_A(nc, kind_out="ExternalOutput"):
    return dict(
        x_in=nc.dram_tensor("x", [D, T], F32, kind="ExternalInput"),
        q12w=nc.dram_tensor("q12w", [D, 1024], F32, kind="ExternalInput"),
        k12w=nc.dram_tensor("k12w", [D, 1024], F32, kind="ExternalInput"),
        vw=nc.dram_tensor("vw", [D, 1024], F32, kind="ExternalInput"),
        lamv=nc.dram_tensor("lamv", [1, 2048], F32, kind="ExternalInput"),
        tri=nc.dram_tensor("tri", [128, 128], BF16, kind="ExternalInput"),
        yn_out=nc.dram_tensor("ynorm", [1024, T], BF16, kind=kind_out),
    )


def build_A():
    nc = bass.Bass("TRN2", target_bir_lowering=False)
    io = declare_A(nc)
    with ExitStack() as ctx:
        P = Prog(nc, ctx)
        C = Common(nc, ctx, P)
        emit_A(nc, ctx, P, C, io)
        P.play()
    return nc


def emit_A(nc, ctx, P, C, io):
    x_in, q12w, k12w, vw = io["x_in"], io["q12w"], io["k12w"], io["vw"]
    lamv, tri, yn_out = io["lamv"], io["tri"], io["yn_out"]
    if True:
        sb = lambda nm, s, d: ctx.enter_context(nc.sbuf_tensor('A_' + nm, s, d))
        psum = lambda nm, s: ctx.enter_context(nc.psum_tensor('A_' + nm, s, F32))

        lnx = [sb(f"lnx{i}", [128, T], BF16) for i in range(8)]
        tri_sb = sb("trisb", [128, 128], BF16)
        lam_sb = sb("lamsb", [1, 2048], F32)
        nlam = sb("nlam", [1, 512], F32)
        lr1 = sb("lr1", [1, 512], F32)
        lr2 = sb("lr2", [1, 512], F32)
        s_ldc = P.sem("s_ldc")

        def b_ldc(g):
            g.dma_start(out=tri_sb[:, :], in_=tri[:, :]).then_inc(s_ldc.h, 16)
            g.dma_start(out=lam_sb[:, :], in_=lamv[:, :]).then_inc(s_ldc.h, 16)
        s_ldc.bump(32)
        P.raw("gpsimd", b_ldc)

        s_lam = P.sem("s_lam")

        def b_lam1(v):
            v.tensor_mul(lam_sb[0:1, 0:512], lam_sb[0:1, 0:512],
                         lam_sb[0:1, 512:1024])
            v.tensor_mul(lam_sb[0:1, 1024:1536], lam_sb[0:1, 1024:1536],
                         lam_sb[0:1, 1536:2048])
            p1 = lam_sb[0:1, 0:512].rearrange("p (h d) -> p h d", d=64)
            p2 = lam_sb[0:1, 1024:1536].rearrange("p (h d) -> p h d", d=64)
            v.reduce_sum(lr1[0:1, 0:8].rearrange("p (h d) -> p h d", d=1), p1,
                         axis=mybir.AxisListType.X)
            v.reduce_sum(lr2[0:1, 0:8].rearrange("p (h d) -> p h d", d=1),
                         p2, axis=mybir.AxisListType.X)
            return v.tensor_copy(lam_sb[0:1, 0:512], lam_sb[0:1, 0:512])
        P.op("vector", b_lam1, wait=[(s_ldc, 32)], inc=[(s_lam, 1)])

        def b_lam2(s):
            s.activation(lr1[:, 0:8], lr1[:, 0:8], AF.Exp)
            s.activation(lr2[:, 0:8], lr2[:, 0:8], AF.Exp)
            return s.activation(lam_sb[0:1, 0:512], lam_sb[0:1, 0:512],
                                AF.Identity)
        P.op("scalar", b_lam2, wait=[(s_lam, 1)], inc=[(s_lam, 1)])

        def b_lam3(v):
            v.scalar_tensor_tensor(nlam[:, 0:8], lr1[:, 0:8], -1.0,
                                   lr2[:, 0:8], ALU.mult, ALU.add)
            v.tensor_copy(lam_sb[0:1, 0:512], lam_sb[0:1, 0:512])
            v.tensor_scalar_add(nlam[:, 0:8], nlam[:, 0:8], -LAMBDA_INIT)
            return v.tensor_copy(lam_sb[0:1, 0:512], lam_sb[0:1, 0:512])
        P.op("vector", b_lam3, wait=[(s_lam, 2)], inc=[(s_lam, 1)])

        with ExitStack() as c1:
            sb1 = lambda nm, s, d: c1.enter_context(nc.sbuf_tensor('A_' + nm, s, d))
            ps1 = ExitStack()
            st_ps = ps1.enter_context(nc.psum_tensor("st_ps", [64, 512], F32))
            bc_ps = [ps1.enter_context(nc.psum_tensor(f"bc{i}", [128, 512], F32))
                     for i in range(2)]
            xch = [[sb1(f"xc{b}_{i}", [128, 512], F32) for i in range(8)]
                   for b in range(2)]
            sq = [sb1(f"sq{i}", [128, 512], BF16) for i in range(8)]
            rows = [sb1(f"row{i}", [1, 512], F32) for i in range(4)]
            s_lx = P.sem("s_lx")
            ln_sems = tuple(P.sem() for _ in range(6))
            xv = x_in.rearrange("(a p) t -> a p t", p=128)

            app_sems = []
            for k, (off, w) in enumerate(GRID_T):
                buf = xch[k % 2]

                def b_lx(sy, off=off, k=k, buf=buf, prev=app_sems[-2] if
                         len(app_sems) >= 2 else None):
                    if prev is not None:
                        sy.wait_ge(prev[0].h, prev[1])
                    for i in range(8):
                        sy.dma_start(out=buf[i][:, :],
                                     in_=xv[i][:, off:off + w]
                                     ).then_inc(s_lx.h, 16)
                s_lx.bump(16 * 8)
                P.raw("sync", b_lx)

                pre = [(s_lx, s_lx.n), (C.s_init, 1)]
                if app_sems:
                    pre.append(app_sems[-1])
                s_app, _ = emit_ln(P, C, buf, [OffView(lt, off) for lt in lnx],
                                   [(0, 512)],
                                   dict(sq=sq, st_ps=st_ps, bc_ps=bc_ps,
                                        rows=rows), pre, sems=ln_sems)
                app_sems.append((s_app, s_app.n))
            ln_done = app_sems[-1]
            lnall = [(s, v) for s, v in app_sems]
            P.op("gpsimd", lambda g: None,
                 wait=[(s, v) for s, v in app_sems])
            ps1.close()

        s12 = [psum("s12a", [128, 1024]), psum("s12b", [128, 1024])]
        y1_ps = psum("y1", [128, 512])
        y2_ps = psum("y2", [128, 512])
        sum_ps = psum("sums", [64, 512])
        bc2_ps = psum("bc2", [128, 512])

        qw_sb = [sb(f"qw{i}", [128, 1024], BF16) for i in range(2)]
        kw_sb = [sb(f"kw{i}", [128, 1024], BF16) for i in range(2)]
        vw_sb = [sb(f"vw{i}", [128, 1024], BF16) for i in range(2)]
        q12_sb = sb("q12", [128, T], BF16)
        k12_sb = sb("k12", [128, T], BF16)
        v_sb = sb("vsb", [128, T], BF16)
        e12 = [sb(f"e12_{i}", [128, 1024], BF16) for i in range(2)]
        r2_sb = sb("r2", [1, 512], F32)
        gneg = sb("gneg", [1, 512], F32)
        s1_sb = sb("s1r", [1, 512], F32)
        s1e = sb("s1e", [1, 512], F32)
        gb_sb = sb("gb", [128, 512], F32)
        t2_sb = sb("t2", [128, 512], F32)
        yc_bf = sb("ycb", [128, 512], BF16)
        yc2_bf = sb("yc2b", [128, 512], BF16)
        t3_sb = sb("t3", [128, 512], F32)
        yn_st = sb("ynst", [128, 512], BF16)
        mu_r = sb("amur", [1, 512], F32)
        m2_r = sb("m2r", [1, 512], F32)
        var_r = sb("varr", [1, 512], F32)
        rst_r = sb("rstr", [1, 512], F32)

        s_wld = P.sem("s_wld")
        s_pj = P.sem("s_pj")
        s_pe = P.sem("s_pe")
        s_qk = P.sem("s_qk")
        s_e = P.sem("s_e")
        s_tri = P.sem("s_tri")
        s_v = P.sem("s_v")
        s_ep = P.sem("s_ep")
        s_ev = P.sem("s_ev")
        s_er = P.sem("s_er")
        s_sto = P.sem("s_sto")

        ynv = yn_out.rearrange("(a p) t -> a p t", p=128)
        qv = q12w.rearrange("(a p) t -> a p t", p=128)
        kv = k12w.rearrange("(a p) t -> a p t", p=128)
        vv = vw.rearrange("(a p) t -> a p t", p=128)

        def load_head(h):
            hb = h % 2

            def b_lw(g, h=h, hb=hb, pjn=s_pj.n):
                if h >= 2:
                    g.wait_ge(s_pj.h, pjn)
                for i in range(8):
                    g.dma_start(out=qw_sb[hb][:, i * 128:(i + 1) * 128],
                                in_=qv[i][:, h * 128:(h + 1) * 128]
                                ).then_inc(s_wld.h, 16)
                for i in range(8):
                    g.dma_start(out=kw_sb[hb][:, i * 128:(i + 1) * 128],
                                in_=kv[i][:, h * 128:(h + 1) * 128]
                                ).then_inc(s_wld.h, 16)
                for i in range(8):
                    g.dma_start(out=vw_sb[hb][:, i * 128:(i + 1) * 128],
                                in_=vv[i][:, h * 128:(h + 1) * 128]
                                ).then_inc(s_wld.h, 16)
            s_wld.bump(16 * 24)
            P.raw("gpsimd", b_lw)

        load_head(0)
        load_head(1)

        pend_t = []
        pend_s = []
        tile_gates = []
        last_ep2 = (s_ev, 0)
        for h in range(8):
            hb = h % 2
            for which, (wsb, dst) in enumerate(((qw_sb[hb], q12_sb),
                                                (kw_sb[hb], k12_sb))):
                for k, (off, w) in enumerate(GRID_T):
                    bank = y1_ps if (k % 2 == 0) else y2_ps

                    def b_p(t, wsb=wsb, off=off, w=w, bank=bank):
                        ins = None
                        for ct in range(8):
                            ins = t.matmul(bank[:, :w],
                                           wsb[:, ct * 128:(ct + 1) * 128],
                                           lnx[ct][:, off:off + w],
                                           start=(ct == 0), stop=(ct == 7))
                        return ins
                    wt = []
                    if which == 0 and k == 0:
                        wt.append((s_wld, 16 * 24 * (h + 1)))
                        if h == 0:
                            wt += list(lnall) + [(s_lam, 3)]
                        else:
                            wt.append((s_v, s_v.n))
                            wt.append((s_ev, s_ev.n))
                    if s_pj.n >= 2:
                        wt.append((s_pe, s_pj.n - 1))
                    P.op("tensor", b_p, wait=wt, inc=[(s_pj, 1)])

                    def b_pe(v, dst=dst, off=off, w=w, bank=bank):
                        return v.tensor_copy(dst[:, off:off + w], bank[:, :w])
                    P.op("vector", b_pe, wait=[(s_pj, s_pj.n)],
                         inc=[(s_pe, 1)])
            for tc in range(32):
                bank = y1_ps if (tc % 2 == 0) else y2_ps

                def b_vp(t, tc=tc, hb=hb, bank=bank):
                    ins = None
                    for ct in range(8):
                        ins = t.matmul(bank[:, 0:128],
                                       lnx[ct][:, tc * 128:(tc + 1) * 128],
                                       vw_sb[hb][:, ct * 128:(ct + 1) * 128],
                                       start=(ct == 0), stop=(ct == 7))
                    return ins
                wt = []
                if s_pj.n >= 2:
                    wt.append((s_pe, s_pj.n - 1))
                P.op("tensor", b_vp, wait=wt, inc=[(s_pj, 1)])

                def b_vpe(v, tc=tc, bank=bank):
                    return v.tensor_copy(v_sb[:, tc * 128:(tc + 1) * 128],
                                         bank[:, 0:128])
                P.op("vector", b_vpe, wait=[(s_pj, s_pj.n)], inc=[(s_pe, 1)])

            proj_done = (s_pe, s_pe.n)
            load_head(h + 2) if h + 2 < 8 else None

            for qb in range(8):
                nt = 4 * (qb + 1)
                vs_recs = []

                def mk_vs(i, qb=qb, nt=nt):
                    b = i % 2

                    def b_vs(t, i=i, b=b, st=(i == 0), sp=(i == nt - 1)):
                        t.matmul(y1_ps[:, :],
                                 v_sb[:, i * 128:(i + 1) * 128],
                                 e12[b][:, 0:512], start=st, stop=sp)
                        t.matmul(y2_ps[:, :],
                                 v_sb[:, i * 128:(i + 1) * 128],
                                 e12[b][:, 512:1024], start=st, stop=sp)
                        t.matmul(sum_ps[0:1, :], C.ones128b[:, :],
                                 e12[b][:, 0:512], start=st, stop=sp,
                                 tile_position=(0, 0))
                        return t.matmul(sum_ps[32:33, :],
                                        C.ones128b[:, :],
                                        e12[b][:, 512:1024],
                                        start=st, stop=sp,
                                        tile_position=(0, 32))
                    return b_vs

                for i in range(nt):
                    b = i % 2
                    joff = max(0, (i - 4 * qb) * 128)

                    def b_qk(t, i=i, qb=qb, b=b, joff=joff):
                        q0 = qb * 512
                        t.matmul(s12[b][0:128, joff:512],
                                 k12_sb[0:64, i * 128:(i + 1) * 128],
                                 q12_sb[0:64, q0 + joff:q0 + 512],
                                 start=True, stop=True, tile_position=(0, 0))
                        return t.matmul(s12[b][0:128, 512 + joff:1024],
                                        k12_sb[64:128, i * 128:(i + 1) * 128],
                                        q12_sb[64:128, q0 + joff:q0 + 512],
                                        start=True, stop=True,
                                        tile_position=(64, 0))
                    wt = []
                    if i == 0 and qb == 0 and h == 0:
                        wt.append(proj_done)
                    if h > 0 and i == 0 and qb == 0:
                        wt.append((s_pe, s_pe.n))
                    if s_qk.n >= 2:
                        wt.append((s_e, s_qk.n - 1))
                    P.op("tensor", b_qk, wait=wt, inc=[(s_qk, 1)])
                    if pend_t:
                        P.emit_rec(pend_t.pop(0))

                    def b_exp(s, b=b):
                        return s.activation(e12[b][:, :], s12[b][:, :], AF.Exp,
                                            bias=C.c(ATT_BIAS),
                                            scale=C.c(0.125))
                    wt = [(s_qk, s_qk.n)]
                    if s_e.n >= 2:
                        wt.append((s_v, s_e.n - 1))
                        wt.append((s_tri, s_tri.n))
                    P.op("scalar", b_exp, wait=wt, inc=[(s_e, 1)])
                    if pend_s:
                        P.emit_rec(pend_s.pop(0))

                    if i >= 4 * qb:
                        def b_tri(v, b=b, joff=joff):
                            if joff > 0:
                                v.memset(e12[b][:, 0:joff], 0.0)
                                v.memset(e12[b][:, 512:512 + joff], 0.0)
                            v.tensor_mul(e12[b][:, joff:joff + 128],
                                         e12[b][:, joff:joff + 128],
                                         tri_sb[:, :])
                            return v.tensor_mul(
                                e12[b][:, 512 + joff:512 + joff + 128],
                                e12[b][:, 512 + joff:512 + joff + 128],
                                tri_sb[:, :])
                        P.op("vector", b_tri, wait=[(s_e, s_e.n)],
                             inc=[(s_tri, 1)])
                        tile_gates.append((s_tri, s_tri.n))
                    else:
                        tile_gates.append((s_e, s_e.n))

                    if i > 0:
                        wt = [tile_gates[-2]]
                        if i == 1:
                            wt.append(last_ep2)
                        P.op("tensor", mk_vs(i - 1), wait=wt, inc=[(s_v, 1)])
                P.op("tensor", mk_vs(nt - 1), wait=[tile_gates[-1]],
                     inc=[(s_v, 1)])
                tile_gates.clear()

                def b_ep1(v, h=h):
                    v.reciprocal(r2_sb[:, :], sum_ps[32:33, :])
                    v.tensor_copy(s1_sb[:, :], sum_ps[0:1, :])
                    v.scalar_tensor_tensor(gneg[:, :], sum_ps[0:1, :],
                                           nlam[0:1, h:h + 1], r2_sb[:, :],
                                           ALU.mult, ALU.mult)
                    return v.scalar_tensor_tensor(s1e[:, :], s1_sb[:, :],
                                                  C.c(EPS)[0:1, :], s1_sb[:, :],
                                                  ALU.mult, ALU.mult)
                P.op("vector", b_ep1, wait=[(s_v, s_v.n)], inc=[(s_ev, 1)])

                def b_gb(t):
                    return t.matmul(bc2_ps[:, :], C.ones1f[:, :], gneg[:, :],
                                    start=True, stop=True)
                pend_t.append(P.rec("tensor", b_gb, wait=[(s_ev, s_ev.n)],
                                    inc=[(s_ep, 1)]))

                def b_ep2(v):
                    v.tensor_copy(gb_sb[:, :], bc2_ps[:, :])
                    v.tensor_mul(t2_sb[:, :], gb_sb[:, :], y2_ps[:, :])
                    v.tensor_add(yc_bf[:, :], y1_ps[:, :], t2_sb[:, :])
                    return v.tensor_mul(yc2_bf[:, :], yc_bf[:, :],
                                        yc_bf[:, :])
                P.op("vector", b_ep2, wait=[(s_ep, s_ep.n)], inc=[(s_ev, 1)])
                last_ep2 = (s_ev, s_ev.n)

                def b_st2(t):
                    t.matmul(bc2_ps[0:1, :], C.ones128b[:, :], yc_bf[:, :],
                             start=True, stop=True, tile_position=(0, 0))
                    return t.matmul(bc2_ps[32:33, :], C.ones128b[:, :],
                                    yc2_bf[:, :], start=True, stop=True,
                                    tile_position=(0, 32))
                pend_t.append(P.rec("tensor", b_st2, wait=[(s_ev, s_ev.n)],
                                    inc=[(s_ep, 1)]))

                def b_ep3(v):
                    v.tensor_scalar_mul(mu_r[:, :], bc2_ps[0:1, :], 1.0 / 128)
                    v.tensor_mul(m2_r[:, :], mu_r[:, :], mu_r[:, :])
                    v.scalar_tensor_tensor(var_r[:, :], bc2_ps[32:33, :],
                                           1.0 / 128, m2_r[:, :], ALU.mult,
                                           ALU.subtract)
                    return v.tensor_add(var_r[:, :], var_r[:, :], s1e[:, :])
                P.op("vector", b_ep3, wait=[(s_ep, s_ep.n)], inc=[(s_ev, 1)])

                def b_ep4(s):
                    s.activation(rst_r[:, :], var_r[:, :], AF.Ln)
                    return s.activation(rst_r[:, :], rst_r[:, :], AF.Exp,
                                        scale=C.c(-0.5)[0:1, :],
                                        bias=C.c(math.log(1.0 - LAMBDA_INIT))[0:1, :])
                pend_s.append(P.rec("scalar", b_ep4, wait=[(s_ev, s_ev.n)],
                                    inc=[(s_er, 1)]))

                def b_mb(t):
                    return t.matmul(bc2_ps[:, :], C.ones1f[:, :], mu_r[:, :],
                                    start=True, stop=True)
                pend_t.append(P.rec("tensor", b_mb,
                                    wait=[(s_ev, s_ev.n), (s_er, s_er.n)],
                                    inc=[(s_ep, 1)]))

                def b_ep5(v):
                    return v.tensor_sub(t3_sb[:, :], yc_bf[:, :],
                                        bc2_ps[:, :])
                P.op("vector", b_ep5, wait=[(s_ep, s_ep.n)], inc=[(s_ev, 1)])

                def b_rb(t):
                    return t.matmul(bc2_ps[:, :], C.ones1f[:, :], rst_r[:, :],
                                    start=True, stop=True)
                pend_t.append(P.rec("tensor", b_rb, wait=[(s_ev, s_ev.n)],
                                    inc=[(s_ep, 1)]))

                def b_ep6(v):
                    return v.tensor_mul(yn_st[:, :], t3_sb[:, :],
                                        bc2_ps[:, :])
                wt = [(s_ep, s_ep.n)]
                if s_sto.n:
                    wt.append((s_sto, s_sto.n))
                P.op("vector", b_ep6, wait=wt, inc=[(s_ev, 1)])

                def b_yst(sy, h=h, qb=qb, evn=s_ev.n):
                    sy.wait_ge(s_ev.h, evn)
                    sy.dma_start(out=ynv[h][:, qb * 512:(qb + 1) * 512],
                                 in_=yn_st[:, :]).then_inc(s_sto.h, 16)
                s_sto.bump(16)
                P.raw("sync", b_yst)

                if qb == 7:
                    while pend_t:
                        P.emit_rec(pend_t.pop(0))
                    while pend_s:
                        P.emit_rec(pend_s.pop(0))

        P.op("sync", lambda e: None, wait=[(s_sto, s_sto.n)])
        return dict(s_sto=s_sto, s_v=s_v, s_ev=s_ev, s_ep=s_ep, s_er=s_er,
                    s_pe=s_pe, s_pj=s_pj, s_e=s_e, s_tri=s_tri, s_qk=s_qk,
                    lnx=lnx, q12_sb=q12_sb, k12_sb=k12_sb, v_sb=v_sb)



GRID_T = [(i * 512, 512) for i in range(8)]


class OffView:

    def __init__(self, t, off):
        self.t = t
        self.off = off

    def __getitem__(self, idx):
        p, f = idx
        assert isinstance(f, slice)
        return self.t[p, f.start + self.off:f.stop + self.off]



class _APView:

    def __init__(self, ap_fn):
        self._fn = ap_fn

    def __getitem__(self, idx):
        return self._fn(idx)


def build_fused():
    FDBG = bool(os.environ.get("FDBG"))
    KI = "ExternalOutput" if FDBG else "Internal"
    nc = bass.Bass("TRN2", target_bir_lowering=False)
    ioA = declare_A(nc, kind_out="Internal")
    cwm = nc.dram_tensor("cwm", [D, D], F32, kind="ExternalInput")
    xw_d = nc.dram_tensor("xw", [D, L], F32, kind="ExternalInput")
    um_d = nc.dram_tensor("umask", [128, 1], F32, kind="ExternalInput")
    wr = nc.dram_tensor("wr", [D, D], F32, kind="ExternalInput")
    wk = nc.dram_tensor("wk", [D, D], F32, kind="ExternalInput")
    wv = nc.dram_tensor("wv", [D, D], F32, kind="ExternalInput")
    mur = nc.dram_tensor("mur", [128, 8], F32, kind="ExternalInput")
    muk = nc.dram_tensor("muk", [128, 8], F32, kind="ExternalInput")
    cwp = nc.dram_tensor("cwp", [2 * D, OWN], F32, kind="Internal")
    rs_out = nc.dram_tensor("rsout", [D, OWN], F32, kind="Internal")
    agh_in = nc.dram_tensor("aghin", [D, HALO], F32, kind="Internal")
    agh_out = nc.dram_tensor("aghout", [2 * D, HALO], F32, kind="Internal")
    cy_out = nc.dram_tensor("cyout", [32, 1], F32, kind="Internal")
    xp_d = nc.dram_tensor("xp", [D, L], F32, kind=KI)
    xod = nc.dram_tensor("xod", [D, OWN], F32, kind="Internal")
    ioC = declare_C(nc, kind_out=KI, skip_x=True)
    ioC["x_in"] = xp_d
    if FDBG:
        cdb = ioC["carry_out"]
        ioC["carry_out"] = nc.dram_tensor("carryint", [16, 1], F32,
                                          kind="Internal")
    ioD = declare_D(nc, skip_io=True)
    ioD["y0_in"] = ioC["y0_out"]
    ioD["sz_in"] = ioC["sz_out"]
    ioD["x_in"] = xod
    ioD["carry_in"] = _APView(lambda idx: cy_out[0:16, 0:1])

    with ExitStack() as ctx:
        P = Prog(nc, ctx)
        C = Common(nc, ctx, P)
        sb = lambda nm, s, d: ctx.enter_context(nc.sbuf_tensor(nm, s, d))
        um_sb = sb("umsb", [128, 1], F32)
        s_um = P.sem("s_um")

        def b_um(sy):
            sy.dma_start(out=um_sb[:, :], in_=um_d[:, :]).then_inc(s_um.h, 16)
        s_um.bump(16)
        P.raw("sync", b_um)

        with ExitStack() as ctxA:
            gA = emit_A(nc, ctxA, P, C, ioA)
        gateA = [(gA[k], gA[k].n) for k in
                 ("s_sto", "s_v", "s_ev", "s_ep", "s_er", "s_pe")]

        s_rs1 = P.sem("s_rs1")
        s_agh = P.sem("s_agh")
        s_agc = P.sem("s_agc")
        with ExitStack() as ctxP:
            sbP = lambda nm, s, d: ctxP.enter_context(nc.sbuf_tensor(nm, s, d))
            psP = lambda nm, s: ctxP.enter_context(nc.psum_tensor(nm, s, F32))
            yn_sb = [sbP(f"pyn{i}", [128, T], BF16) for i in range(8)]
            cwm_sb = [sbP(f"pcw{i}", [128, D], BF16) for i in range(8)]
            o2_st = [sbP(f"po2{i}", [128, 512], F32) for i in range(4)]
            pp_ps = [psP(f"pp{i}", [128, 512]) for i in range(4)]
            s_pl = P.sem("s_pl")
            s_pm = P.sem("s_pm")
            s_pv = P.sem("s_pv")
            s_ps = P.sem("s_ps")
            dma_tiles(P, "gpsimd", s_pl, ioA["yn_out"], yn_sb, 8, wait=gateA)
            dma_tiles(P, "gpsimd", s_pl, cwm, cwm_sb, 8)
            cpv = cwp.rearrange("(a p) t -> a p t", p=128)
            nblk = 0
            for dm in range(8):
                for tc in range(8):
                    bank = pp_ps[nblk % 4]
                    stg = o2_st[nblk % 4]

                    def b_pm(t, dm=dm, tc=tc, bank=bank):
                        ins = None
                        for d2 in range(8):
                            ins = t.matmul(bank[:, :],
                                           cwm_sb[d2][:, dm * 128:(dm + 1) * 128],
                                           yn_sb[d2][:, tc * 512:(tc + 1) * 512],
                                           start=(d2 == 0), stop=(d2 == 7))
                        return ins
                    wt = [(s_pl, s_pl.n)] if nblk == 0 else []
                    if nblk >= 4:
                        wt.append((s_ps, 16 * (nblk - 3)))
                    P.op("tensor", b_pm, wait=wt, inc=[(s_pm, 1)])

                    def b_pv(v, bank=bank, stg=stg):
                        return v.tensor_copy(stg[:, :], bank[:, :])
                    P.op("vector", b_pv, wait=[(s_pm, s_pm.n)], inc=[(s_pv, 1)])

                    blk = tc // 4
                    col = (tc % 4) * 512
                    wtv = (s_pv.h, s_pv.n)

                    def b_pst(sy, dm=dm, blk=blk, col=col, stg=stg, wtv=wtv):
                        sy.wait_ge(*wtv)
                        sy.dma_start(out=cpv[8 * blk + dm][:, col:col + 512],
                                     in_=stg[:, :]).then_inc(s_ps.h, 16)
                    s_ps.bump(16)
                    P.raw("sync", b_pst)
                    nblk += 1

            def b_agh(g):
                g.wait_ge(s_ps.h, s_ps.n)
                g.dma_start(out=agh_in[:, :],
                            in_=cwp[0:D, OWN - HALO:OWN]).then_inc(s_pl.h, 16)
            s_pl.bump(16)
            P.raw("gpsimd", b_agh)

            def b_cc1(g):
                g.wait_ge(s_pl.h, s_pl.n)
                g.collective_compute(
                    "ReduceScatter", ALU.add, replica_groups=PAIRS,
                    ins=[cwp.ap().opt()], outs=[rs_out.ap().opt()],
                ).then_inc(s_rs1.h, 1)
                g.collective_compute(
                    "AllGather", ALU.bypass, replica_groups=PAIRS,
                    ins=[agh_in.ap().opt()], outs=[agh_out.ap().opt()],
                ).then_inc(s_agh.h, 1)
            s_rs1.bump(1)
            s_agh.bump(1)
            P.raw("gpsimd", b_cc1)

        scr = {}
        ctxBspan = ctx.enter_context(ExitStack())
        sbS = lambda nm, s, d: ctxBspan.enter_context(nc.sbuf_tensor(nm, s, d))
        psS = lambda nm, s: ctxBspan.enter_context(nc.psum_tensor(nm, s, F32))
        scr["x_sb"] = [sbS(f"bx{i}", [128, L], F32) for i in range(8)]
        scr["sq"] = [sbS(f"bsq{i}", [128, 512], BF16) for i in range(8)]
        scr["rows"] = [sbS(f"brow{i}", [1, 512], F32) for i in range(4)]
        scr["st_ps"] = psS("bst", [64, 512])
        scr["bc_ps"] = [psS("bbc0", [128, 512]), psS("bbc1", [128, 512])]
        scr["mm_ps"] = [psS(f"bmm{i}", [128, 512]) for i in range(4)]
        x_sb = scr["x_sb"]
        with ExitStack() as ctxR:
            sbR = lambda nm, s, d: ctxR.enter_context(nc.sbuf_tensor(nm, s, d))
            xw_sb = [sbR(f"rxw{i}", [128, L], F32) for i in range(8)]
            rs_st = [sbR(f"rrs{i}", [128, 512], F32) for i in range(8)]
            hsA = sbR("rhsA", [128, 512], F32)
            hsB = sbR("rhsB", [128, 512], F32)
            hsT = sbR("rhsT", [128, 512], F32)
            s_rl = P.sem("s_rl")
            s_rs = P.sem("s_rs")
            s_hz = P.sem("s_hz")
            s_res = P.sem("s_res")
            dma_tiles(P, "sync", s_rl, xw_d, xw_sb, 8,
                      wait=[(s_rs1, 1), (s_agh, 1)])
            ahv = agh_out.rearrange("(a p) t -> a p t", p=128)

            def b_hz(v):
                v.memset(hsA[:, :], 0.0)
                return v.memset(hsB[:, :], 0.0)
            P.op("vector", b_hz, inc=[(s_hz, 1)])
            for i in range(8):
                def b_h(sy, i=i, hzn=s_hz.n, resn=s_res.n):
                    if i == 0:
                        sy.wait_ge(s_hz.h, hzn)
                    else:
                        sy.wait_ge(s_res.h, resn)
                    sy.dma_start(out=hsA[:, 0:HALO],
                                 in_=ahv[i][:, :]).then_inc(s_rl.h, 16)
                    sy.dma_start(out=hsB[:, 0:HALO],
                                 in_=ahv[8 + i][:, :]).then_inc(s_rl.h, 16)
                s_rl.bump(32)
                P.raw("sync", b_h)

                def b_hf(v, i=i):
                    v.tensor_add(hsT[:, :], hsA[:, :], hsB[:, :])
                    v.tensor_add(hsT[:, :], hsT[:, :], xw_sb[i][:, 0:512])
                    return v.tensor_scalar(x_sb[i][:, 0:HALO], hsT[:, 0:HALO],
                                           um_sb[:, 0:1], None, ALU.mult)
                P.op("vector", b_hf, wait=[(s_rl, s_rl.n), (s_um, 16)],
                     inc=[(s_res, 1)])

            rsv = rs_out.rearrange("(a p) t -> a p t", p=128)
            for k in range(4):
                def b_rld(sy, k=k, resn=s_res.n):
                    if k > 0:
                        sy.wait_ge(s_res.h, resn)
                    for i in range(8):
                        sy.dma_start(out=rs_st[i][:, :],
                                     in_=rsv[i][:, k * 512:(k + 1) * 512]
                                     ).then_inc(s_rs.h, 16)
                s_rs.bump(16 * 8)
                P.raw("sync", b_rld)
                for i in range(8):
                    def b_ra(v, i=i, k=k):
                        return v.tensor_add(
                            x_sb[i][:, HALO + k * 512:HALO + (k + 1) * 512],
                            xw_sb[i][:, HALO + k * 512:HALO + (k + 1) * 512],
                            rs_st[i][:, :])
                    P.op("vector", b_ra, wait=[(s_rs, s_rs.n)],
                         inc=[(s_res, 1)])
            res_done = (s_res, s_res.n)
            P.op("gpsimd", lambda g: None, wait=[res_done])

        gB = emit_ln2_ffn0(nc, ctx, P, C, scr,
                           dict(wr=wr, wk=wk, wv=wv, mur=mur, muk=muk,
                                xp_out=xp_d), res_done)
        ctxBspan.close()
        gateB = [(gB[k], gB[k].n) for k in ("s_sto", "s_out", "s_mmf", "s_app",
                                            "s_r", "s_k", "s_ka")]

        s_gx = P.sem("s_gx")

        def b_gx(g):
            for h, v in [(s.h, n) for s, n in gateB]:
                g.wait_ge(h, v)
            g.dma_start(out=xod[:, :], in_=xp_d[0:D, HALO:L]).then_inc(s_gx.h, 16)
        s_gx.bump(16)
        P.raw("gpsimd", b_gx)

        with ExitStack() as ctxC:
            gC = emit_C(nc, ctxC, P, C, ioC, pre=gateB + [(s_gx, 16)])
        gateC = [(gC[k], gC[k].n) for k in ("s_sto", "s_yev", "s_ymm", "s_u",
                                            "s_xc", "s_scan", "s_cy")]

        def b_cc2(g):
            for h, v in [(s.h, n) for s, n in gateC]:
                g.wait_ge(h, v)
            if FDBG:
                g.dma_start(out=cdb[:, :],
                            in_=ioC["carry_out"][:, :]).then_inc(s_agc.h, 16)
            g.collective_compute(
                "AllGather", ALU.bypass, replica_groups=PAIRS,
                ins=[ioC["carry_out"].ap().opt()], outs=[cy_out.ap().opt()],
            ).then_inc(s_agc.h, 1)
        s_agc.bump(1)
        P.raw("gpsimd", b_cc2)

        def carry_fold(v, cb_sb):
            v.tensor_scalar(cb_sb[:, :], cb_sb[:, :], um_sb[0:16, 0:1], None,
                            ALU.mult)
            v.tensor_copy(scr_settle[:, :], scr_settle[:, :])

        scr_settle = sb("settle", [128, 512], F32)

        def b_dsgate(sy):
            for h, v in [(s.h, n) for s, n in gateC]:
                sy.wait_ge(h, v)
        P.raw("sync", b_dsgate)

        with ExitStack() as ctxD:
            gD = emit_D(nc, ctxD, P, C, ioD,
                        pre=gateC + [(s_agc, 1)], carry_fold=carry_fold)
        P.play()
    return nc



_CACHE = {}


def _built(name, fn):
    if name not in _CACHE:
        _CACHE[name] = fn()
    return _CACHE[name]


def _tri_const():
    return np.ascontiguousarray(
        np.tril(np.ones((128, 128), np.float32)).T).astype(BF)


def _decay_const():
    d = (0.9 ** (np.arange(OWN, dtype=np.float64) + 1)).astype(np.float32)
    return np.ascontiguousarray(np.broadcast_to(d, (128, OWN)))


def kernel(**inputs):
    if os.environ.get("K_UNFUSED"):
        return kernel_unfused(**inputs)
    return kernel_fused(**inputs)


def kernel_fused(L0_attn_x, L0_attn_q1_w, L0_attn_q2_w, L0_attn_k1_w,
                 L0_attn_k2_w, L0_attn_v_w, L0_attn_c_w, L0_attn_lambda_q1,
                 L0_attn_lambda_k1, L0_attn_lambda_q2, L0_attn_lambda_k2,
                 L0_ffn_W_r, L0_ffn_W_k, L0_ffn_W_v, L0_ffn_mu_r, L0_ffn_mu_k,
                 L1_attn_w_in, L1_attn_conv_w, L1_attn_conv_b, L1_attn_w_B,
                 L1_attn_w_C, L1_attn_w_out, L1_ffn_theta, L1_ffn_phi,
                 L1_ffn_theta_noise, L1_ffn_phi_noise):
    f32c = lambda a: np.ascontiguousarray(np.asarray(a), dtype=np.float32)
    xT = [f32c(np.asarray(L0_attn_x[b]).T) for b in range(4)]
    q1w, q2w = f32c(L0_attn_q1_w), f32c(L0_attn_q2_w)
    k1w, k2w = f32c(L0_attn_k1_w), f32c(L0_attn_k2_w)
    vw_full = f32c(L0_attn_v_w)
    lq1, lk1 = f32c(L0_attn_lambda_q1), f32c(L0_attn_lambda_k1)
    lq2, lk2 = f32c(L0_attn_lambda_q2), f32c(L0_attn_lambda_k2)
    cw_full = f32c(L0_attn_c_w)
    tri = _tri_const()
    wrT = f32c(np.asarray(L0_ffn_W_r).T)
    wkT = f32c(np.asarray(L0_ffn_W_k).T)
    wvT = f32c(np.asarray(L0_ffn_W_v).T)
    mur = f32c(np.asarray(L0_ffn_mu_r).reshape(8, 128).T)
    muk = f32c(np.asarray(L0_ffn_mu_k).reshape(8, 128).T)
    cwf = np.asarray(L1_attn_conv_w, np.float32).reshape(2048, 4)
    convw = np.zeros((128, 64), np.float32)
    convb = np.zeros((128, 16), np.float32)
    for pt in range(16):
        convw[:, pt * 4:pt * 4 + 4] = cwf[pt * 128:(pt + 1) * 128]
        convb[:, pt] = np.asarray(L1_attn_conv_b,
                                  np.float32)[pt * 128:(pt + 1) * 128]
    win = f32c(L1_attn_w_in)
    wB = f32c(L1_attn_w_B)
    wC = f32c(L1_attn_w_C)
    wout = f32c(L1_attn_w_out)
    decay = _decay_const()
    th, thn = f32c(L1_ffn_theta), f32c(L1_ffn_theta_noise)
    ph, phn = f32c(L1_ffn_phi), f32c(L1_ffn_phi_noise)
    zx = np.zeros((D, HALO), np.float32)

    in_maps = []
    for c in range(NCORES):
        b, u = c // 2, c % 2
        q12 = np.zeros((D, 1024), np.float32)
        k12 = np.zeros((D, 1024), np.float32)
        for hh in range(8):
            g = 8 * u + hh
            q12[:, hh * 128:hh * 128 + 64] = q1w[:, 64 * g:64 * g + 64]
            q12[:, hh * 128 + 64:hh * 128 + 128] = q2w[:, 64 * g:64 * g + 64]
            k12[:, hh * 128:hh * 128 + 64] = k1w[:, 64 * g:64 * g + 64]
            k12[:, hh * 128 + 64:hh * 128 + 128] = k2w[:, 64 * g:64 * g + 64]
        lamv = np.concatenate(
            [lq1[8 * u:8 * u + 8].reshape(-1), lk1[8 * u:8 * u + 8].reshape(-1),
             lq2[8 * u:8 * u + 8].reshape(-1), lk2[8 * u:8 * u + 8].reshape(-1)]
        )[None, :].astype(np.float32)
        if u == 0:
            xw = np.concatenate([zx, xT[b][:, 0:OWN]], axis=1)
        else:
            xw = xT[b][:, OWN - HALO:T]
        in_maps.append(dict(
            x=xT[b], q12w=q12, k12w=k12,
            vw=np.ascontiguousarray(vw_full[:, 1024 * u:1024 * (u + 1)]),
            lamv=lamv, tri=tri,
            cwm=np.ascontiguousarray(cw_full[1024 * u:1024 * (u + 1), :]),
            xw=np.ascontiguousarray(xw),
            umask=np.full((128, 1), float(u), np.float32),
            wr=wrT, wk=wkT, wv=wvT, mur=mur, muk=muk,
            win=win, convw=convw, convb=convb, wB=wB, wC=wC,
            wout=wout, wc2=wC, decay=decay, th=th, thn=thn, ph=ph, phn=phn))
    res = run_bass_kernel_spmd(_built("FUSED", build_fused), in_maps,
                               core_ids=list(range(NCORES)))
    out = np.empty((4, T, D), np.float32)
    for c in range(NCORES):
        b, u = c // 2, c % 2
        out[b, u * OWN:(u + 1) * OWN, :] = \
            np.asarray(res.results[c]["out"], np.float32).T
    return out


def kernel_unfused(L0_attn_x, L0_attn_q1_w, L0_attn_q2_w, L0_attn_k1_w, L0_attn_k2_w,
           L0_attn_v_w, L0_attn_c_w, L0_attn_lambda_q1, L0_attn_lambda_k1,
           L0_attn_lambda_q2, L0_attn_lambda_k2, L0_ffn_W_r, L0_ffn_W_k,
           L0_ffn_W_v, L0_ffn_mu_r, L0_ffn_mu_k, L1_attn_w_in, L1_attn_conv_w,
           L1_attn_conv_b, L1_attn_w_B, L1_attn_w_C, L1_attn_w_out,
           L1_ffn_theta, L1_ffn_phi, L1_ffn_theta_noise, L1_ffn_phi_noise):
    f32c = lambda a: np.ascontiguousarray(np.asarray(a), dtype=np.float32)
    xT = [f32c(np.asarray(L0_attn_x[b]).T) for b in range(4)]
    q1w, q2w = f32c(L0_attn_q1_w), f32c(L0_attn_q2_w)
    k1w, k2w = f32c(L0_attn_k1_w), f32c(L0_attn_k2_w)
    vw_full = f32c(L0_attn_v_w)
    lq1, lk1 = f32c(L0_attn_lambda_q1), f32c(L0_attn_lambda_k1)
    lq2, lk2 = f32c(L0_attn_lambda_q2), f32c(L0_attn_lambda_k2)
    tri = _tri_const()

    in_A = []
    for c in range(NCORES):
        b, u = c // 2, c % 2
        q12 = np.zeros((D, 1024), np.float32)
        k12 = np.zeros((D, 1024), np.float32)
        for hh in range(8):
            g = 8 * u + hh
            q12[:, hh * 128:hh * 128 + 64] = q1w[:, 64 * g:64 * g + 64]
            q12[:, hh * 128 + 64:hh * 128 + 128] = q2w[:, 64 * g:64 * g + 64]
            k12[:, hh * 128:hh * 128 + 64] = k1w[:, 64 * g:64 * g + 64]
            k12[:, hh * 128 + 64:hh * 128 + 128] = k2w[:, 64 * g:64 * g + 64]
        lamv = np.concatenate(
            [lq1[8 * u:8 * u + 8].reshape(-1), lk1[8 * u:8 * u + 8].reshape(-1),
             lq2[8 * u:8 * u + 8].reshape(-1), lk2[8 * u:8 * u + 8].reshape(-1)]
        )[None, :].astype(np.float32)
        in_A.append(dict(x=xT[b], q12w=q12, k12w=k12,
                         vw=np.ascontiguousarray(vw_full[:, 1024 * u:1024 * (u + 1)]),
                         lamv=lamv, tri=tri))
    res_A = run_bass_kernel_spmd(_built("A", build_A), in_A,
                                 core_ids=list(range(NCORES)))
    yn = [np.asarray(res_A.results[c]["ynorm"]) for c in range(NCORES)]

    wrT = f32c(np.asarray(L0_ffn_W_r).T)
    wkT = f32c(np.asarray(L0_ffn_W_k).T)
    wvT = f32c(np.asarray(L0_ffn_W_v).T)
    mur = f32c(np.asarray(L0_ffn_mu_r).reshape(8, 128).T)
    muk = f32c(np.asarray(L0_ffn_mu_k).reshape(8, 128).T)
    cw = f32c(L0_attn_c_w)
    zb = np.zeros((2048, HALO), BF)
    zx = np.zeros((D, HALO), np.float32)
    in_B = []
    for c in range(NCORES):
        b, u = c // 2, c % 2
        ynf = np.concatenate([yn[2 * b], yn[2 * b + 1]], axis=0)
        if u == 0:
            ynw = np.concatenate([zb, ynf[:, 0:OWN]], axis=1)
            xw = np.concatenate([zx, xT[b][:, 0:OWN]], axis=1)
        else:
            ynw = ynf[:, OWN - HALO:T]
            xw = xT[b][:, OWN - HALO:T]
        in_B.append(dict(ynf=np.ascontiguousarray(ynw),
                         x=np.ascontiguousarray(xw), cw=cw, wr=wrT, wk=wkT,
                         wv=wvT, mur=mur, muk=muk))
    res_B = run_bass_kernel_spmd(_built("B", build_B), in_B,
                                 core_ids=list(range(NCORES)))
    xp = [np.asarray(res_B.results[c]["xp"]) for c in range(NCORES)]

    cwf = np.asarray(L1_attn_conv_w, np.float32).reshape(2048, 4)
    convw = np.zeros((128, 64), np.float32)
    convb = np.zeros((128, 16), np.float32)
    for pt in range(16):
        convw[:, pt * 4:pt * 4 + 4] = cwf[pt * 128:(pt + 1) * 128]
        convb[:, pt] = np.asarray(L1_attn_conv_b, np.float32)[pt * 128:(pt + 1) * 128]
    win = f32c(L1_attn_w_in)
    wB = f32c(L1_attn_w_B)
    wC = f32c(L1_attn_w_C)
    in_C = [dict(x=xp[c], win=win, convw=convw, convb=convb, wB=wB, wC=wC)
            for c in range(NCORES)]
    res_C = run_bass_kernel_spmd(_built("C", build_C), in_C,
                                 core_ids=list(range(NCORES)))

    wout = f32c(L1_attn_w_out)
    decay = _decay_const()
    th, thn = f32c(L1_ffn_theta), f32c(L1_ffn_theta_noise)
    ph, phn = f32c(L1_ffn_phi), f32c(L1_ffn_phi_noise)
    zc = np.zeros((16, 1), np.float32)
    in_D = []
    for c in range(NCORES):
        b, u = c // 2, c % 2
        carry = np.asarray(res_C.results[2 * b]["carry"], np.float32) if u == 1 else zc
        in_D.append(dict(y0=np.asarray(res_C.results[c]["y0"]),
                         sz=np.asarray(res_C.results[c]["sz"]),
                         carry=carry,
                         x=np.ascontiguousarray(xp[c][:, HALO:]),
                         wout=wout, wc2=wC, decay=decay, th=th, thn=thn,
                         ph=ph, phn=phn))
    res_D = run_bass_kernel_spmd(_built("D", build_D), in_D,
                                 core_ids=list(range(NCORES)))

    out = np.empty((4, T, D), np.float32)
    for c in range(NCORES):
        b, u = c // 2, c % 2
        out[b, u * OWN:(u + 1) * OWN, :] = \
            np.asarray(res_D.results[c]["out"], np.float32).T
    return out


# version 07
# speedup vs baseline: 1.4565x; 1.0325x over previous
import math
import os
from contextlib import ExitStack

import numpy as np
import ml_dtypes

import concourse.bass as bass
import concourse.mybir as mybir
from concourse.bass_utils import run_bass_kernel_spmd

F32 = mybir.dt.float32
BF16 = mybir.dt.bfloat16
AF = mybir.ActivationFunctionType
ALU = mybir.AluOpType

NCORES = 8
D = 1024
T = 4096
HALO = 4
OWN = T // 2
L = HALO + OWN
NH = 8
HD = 64
LAMBDA_INIT = 0.8 - 0.6 * math.exp(-0.3 * (4 - 1))
EPS = 1e-5
ATT_BIAS = -12.0
PAIRS = [[0, 1], [2, 3], [4, 5], [6, 7]]
ENGS = ("sync", "gpsimd", "vector", "scalar", "tensor")
BF = ml_dtypes.bfloat16


def chunks_of(total, step=512):
    out, pos = [], 0
    r = total % step
    if r:
        out.append((0, r))
        pos = r
    while pos < total:
        out.append((pos, step))
        pos += step
    return out


GRID_L = [(0, 512), (512, 512), (1024, 512), (1536, 512), (1540, 512)]
GRID_OWN = [(0, 512), (512, 512), (1024, 512), (1536, 512)]


class Sem:
    def __init__(self, handle):
        self.h = handle
        self.n = 0

    def bump(self, k=1):
        self.n += k
        return self.n


class Prog:

    def __init__(self, nc, ctx):
        self.nc = nc
        self.ctx = ctx
        self.ops = {e: [] for e in ENGS}
        self._semid = 0

    def sem(self, name=None):
        self._semid += 1
        nm = f"{name}_{self._semid}" if name else f"s{self._semid}"
        h = self.ctx.enter_context(self.nc.semaphore(nm))
        return Sem(h)

    def raw(self, eng, fn):
        self.ops[eng].append(fn)

    def rec(self, eng, build, wait=(), inc=()):
        waits = [(s.h, v) for s, v in wait]
        incs = []
        for s, k in inc:
            s.bump(k)
            incs.append((s.h, k))

        def f(e):
            for h, v in waits:
                e.wait_ge(h, v)
            ins = build(e)
            if incs:
                assert ins is not None
                for h, k in incs:
                    ins.then_inc(h, k)
        return (eng, f)

    def emit_rec(self, rec):
        eng, f = rec
        self.ops[eng].append(f)

    def op(self, eng, build, wait=(), inc=()):
        waits = [(s.h, v) for s, v in wait]
        incs = []
        for s, k in inc:
            s.bump(k)
            incs.append((s.h, k))

        def f(e):
            for h, v in waits:
                e.wait_ge(h, v)
            ins = build(e)
            if incs:
                assert ins is not None
                for h, k in incs:
                    ins.then_inc(h, k)
        self.ops[eng].append(f)

    def play(self):
        with self.nc.Block() as block:
            @block.sync
            def _(e):
                for f in self.ops["sync"]:
                    f(e)

            @block.gpsimd
            def _(e):
                for f in self.ops["gpsimd"]:
                    f(e)

            @block.vector
            def _(e):
                for f in self.ops["vector"]:
                    f(e)

            @block.scalar
            def _(e):
                for f in self.ops["scalar"]:
                    f(e)

            @block.tensor
            def _(e):
                for f in self.ops["tensor"]:
                    f(e)


def dma_tiles(P, eng, sem, dram, tiles, n, wait=()):
    dv = dram.rearrange("(a p) t -> a p t", p=128)
    waits = [(s.h, v) for s, v in wait]

    def f(e):
        for h, v in waits:
            e.wait_ge(h, v)
        for i in range(n):
            e.dma_start(out=tiles[i][:, :], in_=dv[i]).then_inc(sem.h, 16)
    sem.bump(16 * n)
    P.raw(eng, f)


class Common:
    def __init__(self, nc, ctx, P):
        sb = lambda nm, s, d: ctx.enter_context(nc.sbuf_tensor(nm, s, d))
        self.ones128f = sb("ones128f", [128, 1], F32)
        self.ones128b = sb("ones128b", [128, 1], BF16)
        self.ones1f = sb("ones1f", [1, 128], F32)
        self.consts = sb("constsA", [128, 12], F32)
        self.s_init = P.sem("s_init")
        cvals = [EPS, -0.5, ATT_BIAS, 0.125, LAMBDA_INIT, 0.0, math.pi / 2, 0.5, math.log(1.0 - LAMBDA_INIT)]
        self.cidx = {v: i for i, v in enumerate(cvals)}

        def v_init(v):
            v.memset(self.ones128f[:, :], 1.0)
            v.memset(self.ones128b[:, :], 1.0)
            v.memset(self.ones1f[:, :], 1.0)
            ins = None
            for val, i in self.cidx.items():
                ins = v.memset(self.consts[:, i:i + 1], val)
            return ins
        P.op("vector", v_init, inc=[(self.s_init, 1)])

    def c(self, val):
        i = self.cidx[val]
        return self.consts[:, i:i + 1]


def emit_ln(P, C, x_tiles, out_tiles, chs, scr, pre_wait, nft=8, sems=None):
    if sems is None:
        sems = tuple(P.sem() for _ in range(6))
    s_sq, s_st, s_row, s_rs, s_bc, s_app = sems
    sq, st_ps, bc_ps = scr["sq"], scr["st_ps"], scr["bc_ps"]
    r_mu, r_mu2, r_var, r_rstd = scr["rows"]
    inv_d = 1.0 / (128 * nft)
    app_vals = []
    for k, (off, w) in enumerate(chs):
        pw = list(pre_wait) if k == 0 else []

        def b_sq(s, off=off, w=w):
            ins = None
            for i in range(nft):
                ins = s.activation(sq[i][:, :w], x_tiles[i][:, off:off + w],
                                   AF.Square)
            return ins
        gate = [(s_st, s_st.n), (s_app, s_app.n)]
        P.op("scalar", b_sq, wait=pw + gate, inc=[(s_sq, 1)])

        def b_st(t, off=off, w=w):
            ins = None
            for i in range(nft):
                ins = t.matmul(st_ps[0:1, :w], C.ones128f[:, :],
                               x_tiles[i][:, off:off + w], start=(i == 0),
                               stop=(i == nft - 1), tile_position=(0, 0))
            return ins
        P.op("tensor", b_st, wait=pw + [(s_row, s_row.n)])

        def b_st2(t, w=w):
            ins = None
            for i in range(nft):
                ins = t.matmul(st_ps[32:33, :w], C.ones128b[:, :], sq[i][:, :w],
                               start=(i == 0), stop=(i == nft - 1),
                               tile_position=(0, 32))
            return ins
        P.op("tensor", b_st2, wait=[(s_sq, s_sq.n)], inc=[(s_st, 1)])

        def b_row(v, w=w):
            v.tensor_scalar_mul(r_mu[:, :w], st_ps[0:1, :w], inv_d)
            v.tensor_mul(r_mu2[:, :w], r_mu[:, :w], r_mu[:, :w])
            return v.scalar_tensor_tensor(r_var[:, :w], st_ps[32:33, :w], inv_d,
                                          r_mu2[:, :w], ALU.mult, ALU.subtract)
        P.op("vector", b_row, wait=[(s_st, s_st.n), (s_bc, s_bc.n)],
             inc=[(s_row, 1)])

        def b_rs(s, w=w):
            s.activation(r_rstd[:, :w], r_var[:, :w], AF.Ln,
                         bias=C.c(EPS)[0:1, :])
            return s.activation(r_rstd[:, :w], r_rstd[:, :w], AF.Exp,
                                scale=C.c(-0.5)[0:1, :])
        P.op("scalar", b_rs, wait=[(s_row, s_row.n)], inc=[(s_rs, 1)])

        def b_bc(t, w=w):
            t.matmul(bc_ps[0][:, :w], C.ones1f[:, :], r_mu[:, :w], start=True,
                     stop=True)
            return t.matmul(bc_ps[1][:, :w], C.ones1f[:, :], r_rstd[:, :w],
                            start=True, stop=True)
        P.op("tensor", b_bc, wait=[(s_rs, s_rs.n), (s_app, s_app.n)],
             inc=[(s_bc, 1)])

        def b_app(v, off=off, w=w):
            ins = None
            for i in range(nft):
                v.tensor_sub(sq[i][:, :w], x_tiles[i][:, off:off + w],
                             bc_ps[0][:, :w])
                ins = v.tensor_mul(out_tiles[i][:, off:off + w], sq[i][:, :w],
                                   bc_ps[1][:, :w])
            return ins
        P.op("vector", b_app, wait=[(s_bc, s_bc.n)], inc=[(s_app, 1)])
        app_vals.append(s_app.n)
    return s_app, app_vals



def build_B():
    nc = bass.Bass("TRN2", target_bir_lowering=False)
    ynf = nc.dram_tensor("ynf", [2048, L], BF16, kind="ExternalInput")
    cw = nc.dram_tensor("cw", [2048, D], F32, kind="ExternalInput")
    x_in = nc.dram_tensor("x", [D, L], F32, kind="ExternalInput")
    wr = nc.dram_tensor("wr", [D, D], F32, kind="ExternalInput")
    wk = nc.dram_tensor("wk", [D, D], F32, kind="ExternalInput")
    wv = nc.dram_tensor("wv", [D, D], F32, kind="ExternalInput")
    mur = nc.dram_tensor("mur", [128, 8], F32, kind="ExternalInput")
    muk = nc.dram_tensor("muk", [128, 8], F32, kind="ExternalInput")
    xp_out = nc.dram_tensor("xp", [D, L], F32, kind="ExternalOutput")

    with ExitStack() as ctx:
        P = Prog(nc, ctx)
        C = Common(nc, ctx, P)
        sb = lambda nm, s, d: ctx.enter_context(nc.sbuf_tensor(nm, s, d))
        psum = lambda nm, s: ctx.enter_context(nc.psum_tensor(nm, s, F32))

        x_sb = [sb(f"x{i}", [128, L], F32) for i in range(8)]
        sq = [sb(f"sq{i}", [128, 512], BF16) for i in range(8)]
        rows = [sb(f"row{i}", [1, 512], F32) for i in range(4)]
        st_ps = psum("st_ps", [64, 512])
        bc_ps = [psum("bc0", [128, 512]), psum("bc1", [128, 512])]
        mm_ps = [psum(f"mm{i}", [128, 512]) for i in range(4)]

        with ExitStack() as c1:
            sb1 = lambda nm, s, d: c1.enter_context(nc.sbuf_tensor(nm, s, d))
            ynf_sb = [sb1(f"ynf{i}", [128, L], BF16) for i in range(16)]
            cw_sb = [sb1(f"cwt{i}", [128, D], BF16) for i in range(16)]
            xin_st = [sb1(f"xis{i}", [128, 512], F32) for i in range(8)]
            s_ld1 = P.sem("s_ld1")
            s_lx = P.sem("s_lx")
            s_cw = P.sem("s_cw")
            s_res = P.sem("s_res")
            dma_tiles(P, "gpsimd", s_ld1, ynf, ynf_sb, 16)
            dma_tiles(P, "gpsimd", s_ld1, cw, cw_sb, 16)

            xiv = x_in.rearrange("(a p) t -> a p t", p=128)
            nblk = 0
            for k, (off, w) in enumerate(GRID_L):
                def b_lx(sy, off=off, w=w, k=k, resn=s_res.n):
                    if k > 0:
                        sy.wait_ge(s_res.h, resn)
                    for i in range(8):
                        sy.dma_start(out=xin_st[i][:, :w],
                                     in_=xiv[i][:, off:off + w]
                                     ).then_inc(s_lx.h, 16)
                s_lx.bump(16 * 8)
                P.raw("sync", b_lx)
                for dm in range(8):
                    bank = mm_ps[nblk % 4]

                    def b_cw(t, dm=dm, off=off, w=w, bank=bank):
                        ins = None
                        for d2 in range(16):
                            ins = t.matmul(bank[:, :w],
                                           cw_sb[d2][:, dm * 128:(dm + 1) * 128],
                                           ynf_sb[d2][:, off:off + w],
                                           start=(d2 == 0), stop=(d2 == 15))
                        return ins
                    wt = [(s_ld1, s_ld1.n)] if nblk == 0 else []
                    if nblk >= 4:
                        wt.append((s_res, nblk - 3))
                    P.op("tensor", b_cw, wait=wt, inc=[(s_cw, 1)])

                    def b_res(v, dm=dm, off=off, w=w, bank=bank):
                        return v.tensor_add(x_sb[dm][:, off:off + w],
                                            xin_st[dm][:, :w],
                                            bank[:, :w])
                    P.op("vector", b_res, wait=[(s_cw, s_cw.n), (s_lx, s_lx.n)],
                         inc=[(s_res, 1)])
                    nblk += 1
            res_done = (s_res, s_res.n)
            cw_done = (s_cw, s_cw.n)
            P.op("gpsimd", lambda g: None, wait=[cw_done, res_done])

        emit_ln2_ffn0(nc, ctx, P, C, dict(x_sb=x_sb, sq=sq, rows=rows,
                                          st_ps=st_ps, bc_ps=bc_ps,
                                          mm_ps=mm_ps),
                      dict(wr=wr, wk=wk, wv=wv, mur=mur, muk=muk,
                           xp_out=xp_out), res_done)
        P.play()
    return nc


def emit_ln2_ffn0(nc, ctx, P, C, scr, io, res_done):
    x_sb, sq, rows = scr["x_sb"], scr["sq"], scr["rows"]
    st_ps, bc_ps, mm_ps = scr["st_ps"], scr["bc_ps"], scr["mm_ps"]
    wr, wk, wv, mur, muk = io["wr"], io["wk"], io["wv"], io["mur"], io["muk"]
    xp_out = io["xp_out"]
    if True:
        with ExitStack() as c2:
            sb2 = lambda nm, s, d: c2.enter_context(nc.sbuf_tensor('F_' + nm, s, d))
            wr_sb = [sb2(f"wrt{i}", [128, D], BF16) for i in range(8)]
            wk_sb = [sb2(f"wkt{i}", [128, D], BF16) for i in range(8)]
            wv_sb = [sb2(f"wvt{i}", [128, D], BF16) for i in range(8)]
            mu_sb = sb2("mu", [128, 16], F32)
            lny = [sb2(f"lny{i}", [128, L], BF16) for i in range(8)]
            xr_sb = [sb2(f"xm{i}", [128, 512], BF16) for i in range(8)]
            xk_sb = [sb2(f"xk{i}", [128, 512], BF16) for i in range(8)]
            r_sb = [sb2(f"r{i}", [128, 512], BF16) for i in range(8)]
            ka_sb = [sb2(f"ka{i}", [128, 512], BF16) for i in range(8)]
            xp_st = [sb2(f"xps{i}", [128, 512], F32) for i in range(4)]
            s_ld2 = P.sem("s_ld2")
            dma_tiles(P, "gpsimd", s_ld2, wr, wr_sb, 8)
            dma_tiles(P, "gpsimd", s_ld2, wk, wk_sb, 8)
            dma_tiles(P, "gpsimd", s_ld2, wv, wv_sb, 8)

            def b_mu(g):
                g.dma_start(out=mu_sb[:, 0:8], in_=mur[:, :]).then_inc(s_ld2.h, 16)
                g.dma_start(out=mu_sb[:, 8:16], in_=muk[:, :]).then_inc(s_ld2.h, 16)
            s_ld2.bump(32)
            P.raw("gpsimd", b_mu)

            pre = [res_done, (C.s_init, 1)]
            s_app, app_vals = emit_ln(P, C, x_sb, lny, GRID_L,
                                      dict(sq=sq, st_ps=st_ps, bc_ps=bc_ps,
                                           rows=rows), pre)

            s_mix = P.sem("s_mix")
            s_r = P.sem("s_r")
            s_k = P.sem("s_k")
            s_ka = P.sem("s_ka")
            s_mmf = P.sem("s_mmf")
            s_out = P.sem("s_out")
            s_sto = P.sem("s_sto")

            for k, (off, w) in enumerate(GRID_L):
                def mk_mix(cols, buf):
                    def b_mix(v, off=off, w=w, cols=cols, buf=buf):
                        ins = None
                        for i in range(8):
                            muc = mu_sb[:, cols + i:cols + i + 1]
                            if off == 0:
                                v.tensor_copy(buf[i][:, 0:1], lny[i][:, 0:1])
                                o, wo = 1, w - 1
                            else:
                                o, wo = off, w
                            cur = lny[i][:, o:o + wo]
                            prv = lny[i][:, o - 1:o - 1 + wo]
                            dm_ = buf[i][:, o - off:o - off + wo]
                            v.tensor_sub(dm_, cur, prv)
                            ins = v.scalar_tensor_tensor(dm_, dm_, muc, prv,
                                                         ALU.mult, ALU.add)
                        return ins
                    return b_mix
                P.op("vector", mk_mix(0, xr_sb),
                     wait=[(s_app, app_vals[k]), (s_ld2, s_ld2.n),
                           (s_mmf, s_mmf.n)],
                     inc=[(s_mix, 1)])
                P.op("vector", mk_mix(8, xk_sb), wait=[], inc=[(s_mix, 1)])

                for dt in range(8):
                    bank = mm_ps[dt % 2]

                    def b_rmm(t, dt=dt, w=w, bank=bank):
                        ins = None
                        for ct in range(8):
                            ins = t.matmul(bank[:, :w],
                                           wr_sb[ct][:, dt * 128:(dt + 1) * 128],
                                           xr_sb[ct][:, :w],
                                           start=(ct == 0), stop=(ct == 7))
                        return ins
                    wt = [(s_mix, s_mix.n)] if dt == 0 else []
                    if k == 0 and dt == 0:
                        wt.append(res_done)
                    if s_r.n >= 2:
                        wt.append((s_r, s_r.n - 1))
                    P.op("tensor", b_rmm, wait=wt, inc=[(s_mmf, 1)])

                    def b_rev(s, dt=dt, w=w, bank=bank):
                        return s.activation(r_sb[dt][:, :w], bank[:, :w],
                                            AF.Sigmoid)
                    wt = [(s_mmf, s_mmf.n)]
                    if k > 0:
                        wt.append((s_out, 8 * (k - 1) + dt + 1))
                    P.op("scalar", b_rev, wait=wt, inc=[(s_r, 1)])

                for dt in range(8):
                    bank = mm_ps[2 + dt % 2]

                    def b_kmm(t, dt=dt, w=w, bank=bank):
                        ins = None
                        for ct in range(8):
                            ins = t.matmul(bank[:, :w],
                                           wk_sb[ct][:, dt * 128:(dt + 1) * 128],
                                           xk_sb[ct][:, :w],
                                           start=(ct == 0), stop=(ct == 7))
                        return ins
                    wt = [(s_mix, s_mix.n)] if dt == 0 else []
                    if s_k.n >= 2:
                        wt.append((s_k, s_k.n - 1))
                    P.op("tensor", b_kmm, wait=wt, inc=[(s_mmf, 1)])

                    def b_kev(s, dt=dt, w=w, bank=bank):
                        return s.activation(ka_sb[dt][:, :w], bank[:, :w],
                                            AF.Relu)
                    P.op("scalar", b_kev, wait=[(s_mmf, s_mmf.n)],
                         inc=[(s_k, 1)])

                def b_kasq(v, w=w):
                    ins = None
                    for dt in range(8):
                        ins = v.tensor_mul(ka_sb[dt][:, :w], ka_sb[dt][:, :w],
                                           ka_sb[dt][:, :w])
                    return ins
                P.op("vector", b_kasq, wait=[(s_k, s_k.n)], inc=[(s_ka, 1)])

                xv = xp_out.rearrange("(a p) t -> a p t", p=128)
                for dt in range(8):
                    bank = mm_ps[dt % 2]
                    g = 8 * k + dt
                    stg = xp_st[g % 4]

                    def b_vmm(t, dt=dt, w=w, bank=bank):
                        ins = None
                        for ct in range(8):
                            ins = t.matmul(bank[:, :w],
                                           wv_sb[ct][:, dt * 128:(dt + 1) * 128],
                                           ka_sb[ct][:, :w],
                                           start=(ct == 0), stop=(ct == 7))
                        return ins
                    wt = [(s_ka, s_ka.n), (s_r, s_r.n)] if dt == 0 else []
                    if dt >= 2:
                        wt.append((s_out, s_out.n - 1))
                    P.op("tensor", b_vmm, wait=wt, inc=[(s_mmf, 1)])

                    def b_vout(v, dt=dt, off=off, w=w, bank=bank, stg=stg):
                        v.tensor_mul(stg[:, :w], r_sb[dt][:, :w], bank[:, :w])
                        return v.tensor_add(stg[:, :w],
                                            x_sb[dt][:, off:off + w],
                                            stg[:, :w])
                    wt = [(s_mmf, s_mmf.n)]
                    if g >= 4:
                        wt.append((s_sto, 16 * (g - 3)))
                    P.op("vector", b_vout, wait=wt, inc=[(s_out, 1)])

                    wt_sto = (s_out.h, s_out.n)

                    def b_sto(sy, dt=dt, off=off, w=w, wt=wt_sto, stg=stg):
                        sy.wait_ge(*wt)
                        sy.dma_start(out=xv[dt][:, off:off + w],
                                     in_=stg[:, :w]).then_inc(s_sto.h, 16)
                    s_sto.bump(16)
                    P.raw("sync", b_sto)

            P.op("sync", lambda e: None, wait=[(s_sto, s_sto.n)])
            return dict(s_sto=s_sto, s_out=s_out, s_mmf=s_mmf, s_app=s_app,
                        s_r=s_r, s_k=s_k, s_ka=s_ka, s_mix=s_mix)



def own_store_range(off, w):
    own_lo = max(0, off - HALO)
    src_lo = own_lo - (off - HALO)
    return src_lo, own_lo, w - src_lo


def declare_C(nc, kind_x="ExternalInput", kind_out="ExternalOutput",
              xname="xc", skip_x=False):
    return dict(
        x_in=None if skip_x else nc.dram_tensor(xname, [D, L], F32,
                                                kind=kind_x),
        win=nc.dram_tensor("win", [D, 4096], F32, kind="ExternalInput"),
        convw=nc.dram_tensor("convw", [128, 64], F32, kind="ExternalInput"),
        convb=nc.dram_tensor("convb", [128, 16], F32, kind="ExternalInput"),
        wB=nc.dram_tensor("wB", [2048, 16], F32, kind="ExternalInput"),
        wC=nc.dram_tensor("wC", [16, 2048], F32, kind="ExternalInput"),
        y0_out=nc.dram_tensor("y0", [2048, OWN], BF16, kind=kind_out),
        sz_out=nc.dram_tensor("sz", [2048, OWN], BF16, kind=kind_out),
        carry_out=nc.dram_tensor("carry", [16, 1], F32, kind=kind_out),
    )


def build_C():
    nc = bass.Bass("TRN2", target_bir_lowering=False)
    io = declare_C(nc, kind_x="ExternalInput", xname="x")
    with ExitStack() as ctx:
        P = Prog(nc, ctx)
        C = Common(nc, ctx, P)
        emit_C(nc, ctx, P, C, io, pre=())
        P.play()
    return nc


def emit_C(nc, ctx, P, C, io, pre=()):
    x_in, win, convw, convb = io["x_in"], io["win"], io["convw"], io["convb"]
    wB, wC = io["wB"], io["wC"]
    y0_out, sz_out, carry_out = io["y0_out"], io["sz_out"], io["carry_out"]
    if True:
        sb = lambda nm, s, d: ctx.enter_context(nc.sbuf_tensor('C_' + nm, s, d))
        psum = lambda nm, s: ctx.enter_context(nc.psum_tensor('C_' + nm, s, F32))

        lnz = [sb(f"lnz{i}", [128, L], BF16) for i in range(8)]
        xp_full = [sb(f"xpf{i}", [128, L], BF16) for i in range(16)]
        s_sto = P.sem("s_sto")

        st_ps = psum("st_ps", [64, 512])
        bc_ps = [psum("bc0", [128, 512]), psum("bc1", [128, 512])]
        with ExitStack() as c1:
            sb1 = lambda nm, s, d: c1.enter_context(nc.sbuf_tensor('C_' + nm, s, d))
            x_sb = [sb1(f"x{i}", [128, L], F32) for i in range(8)]
            sq = [sb1(f"sq{i}", [128, 512], BF16) for i in range(8)]
            rows = [sb1(f"row{i}", [1, 512], F32) for i in range(4)]
            s_ldx = P.sem("s_ldx")
            dma_tiles(P, "sync", s_ldx, x_in, x_sb, 8, wait=pre)
            pre = [(s_ldx, s_ldx.n), (C.s_init, 1)]
            s_app, app_vals = emit_ln(P, C, x_sb, lnz, GRID_L,
                                      dict(sq=sq, st_ps=st_ps, bc_ps=bc_ps,
                                           rows=rows), pre)
            ln_done = (s_app, s_app.n)
            ln_sem, ln_app_vals = s_app, app_vals
            P.op("gpsimd", lambda g: None, wait=[ln_done])

        mm_ps = [psum(f"mm{i}", [128, 512]) for i in range(4)]
        with ExitStack() as c2:
            sb2 = lambda nm, s, d: c2.enter_context(nc.sbuf_tensor('C_' + nm, s, d))
            wi_sb = [sb2(f"wi{i}", [128, 4096], BF16) for i in range(8)]
            sz_st = [sb2(f"szs{i}", [128, 512], BF16) for i in range(16)]
            s_ldw = P.sem("s_ldw")
            dma_tiles(P, "gpsimd", s_ldw, win, wi_sb, 8)

            s_wmm = P.sem("s_wmm")
            s_xp = P.sem("s_xp")
            nblk = 0
            for k, (off, w) in enumerate(GRID_L):
                for dt in range(32):
                    bank = mm_ps[nblk % 4]

                    def b_wmm(t, dt=dt, off=off, w=w, bank=bank):
                        ins = None
                        for ct in range(8):
                            ins = t.matmul(bank[:, :w],
                                           wi_sb[ct][:, dt * 128:(dt + 1) * 128],
                                           lnz[ct][:, off:off + w],
                                           start=(ct == 0), stop=(ct == 7))
                        return ins
                    wt = [(s_ldw, s_ldw.n)] if nblk == 0 else []
                    if dt == 0:
                        wt.append((ln_sem, ln_app_vals[k]))
                    if nblk >= 4:
                        wt.append((s_xp, nblk - 3))
                    P.op("tensor", b_wmm, wait=wt, inc=[(s_wmm, 1)])

                    if dt < 16:
                        def b_xev(v, dt=dt, off=off, w=w, bank=bank):
                            return v.tensor_copy(xp_full[dt][:, off:off + w],
                                                 bank[:, :w])
                        P.op("vector", b_xev, wait=[(s_wmm, s_wmm.n)],
                             inc=[(s_xp, 1)])
                    else:
                        def b_zev(s, dt=dt, w=w, bank=bank):
                            return s.activation(sz_st[dt - 16][:, :w],
                                                bank[:, :w], AF.Silu)
                        wt = [(s_wmm, s_wmm.n)]
                        if k > 0:
                            wt.append((s_sto, 16 * 16 * k))
                        P.op("scalar", b_zev, wait=wt, inc=[(s_xp, 1)])
                    nblk += 1

                src_lo, own_lo, ww = own_store_range(off, w)
                zv = sz_out.rearrange("(a p) t -> a p t", p=128)
                wt_sto = (s_xp.h, s_xp.n)

                def b_szst(sy, src_lo=src_lo, own_lo=own_lo, ww=ww, wt=wt_sto,
                           zv=zv):
                    sy.wait_ge(*wt)
                    for i in range(16):
                        sy.dma_start(out=zv[i][:, own_lo:own_lo + ww],
                                     in_=sz_st[i][:, src_lo:src_lo + ww]
                                     ).then_inc(s_sto.h, 16)
                s_sto.bump(16 * 16)
                P.raw("sync", b_szst)
            wmm_done = (s_wmm, s_wmm.n)
            evac_done = (s_xp, s_xp.n)
            szsto_done = (s_sto, s_sto.n)
            P.op("gpsimd", lambda g: None,
                 wait=[wmm_done, evac_done, szsto_done])

        with ExitStack() as c3:
            sb3 = lambda nm, s, d: c3.enter_context(nc.sbuf_tensor('C_' + nm, s, d))
            cw_sb = sb3("cwv", [128, 64], F32)
            cb_sb = sb3("cbv", [128, 16], F32)
            wb_sb = [sb3(f"wb{i}", [128, 16], BF16) for i in range(16)]
            wc_sb = sb3("wcv", [16, 2048], BF16)
            c09 = sb3("c09", [16, 512], F32)
            zini = sb3("zini", [16, 1], F32)
            xc_st = [sb3(f"xc{i}", [128, 512], BF16) for i in range(16)]
            acc = [sb3(f"acc{i}", [128, 512], F32) for i in range(4)]
            u01 = sb3("u01", [16, 512], F32)
            h_sec = [sb3(f"h{i}", [16, 512], F32) for i in range(4)]
            hb = sb3("hb", [16, 512], BF16)
            hpad = sb3("hpad", [16, 512], F32)
            y0_st = [sb3(f"y0s{i}", [128, 512], BF16) for i in range(16)]
            ups = psum("ups", [16, 512])
            s_ld3 = P.sem("s_ld3")
            dma_tiles(P, "gpsimd", s_ld3, wB, wb_sb, 16)

            def b_ld3(g):
                g.dma_start(out=cw_sb[:, :], in_=convw[:, :]).then_inc(s_ld3.h, 16)
                g.dma_start(out=cb_sb[:, :], in_=convb[:, :]).then_inc(s_ld3.h, 16)
                g.dma_start(out=wc_sb[:, :], in_=wC[:, :]).then_inc(s_ld3.h, 16)
            s_ld3.bump(48)
            P.raw("gpsimd", b_ld3)

            s_c09 = P.sem("s_c09")

            def b_c09(v):
                v.memset(c09[:, :], 0.9)
                return v.memset(zini[:, :], 0.0)
            P.op("vector", b_c09, wait=[evac_done, szsto_done],
                 inc=[(s_c09, 1)])

            s_conv = P.sem("s_conv")
            s_xc = P.sem("s_xc")
            s_u = P.sem("s_u")
            s_scan = P.sem("s_scan")
            s_ymm = P.sem("s_ymm")
            s_yev = P.sem("s_yev")

            for si, (soff, sw) in enumerate(GRID_OWN):
                for pt in range(16):
                    a = acc[pt % 4]

                    def b_conv(v, soff=soff, pt=pt, a=a):
                        v.tensor_scalar(a[:, :], xp_full[pt][:, soff + 1:soff + 513],
                                        cw_sb[:, pt * 4:pt * 4 + 1], None,
                                        ALU.mult)
                        ins = None
                        for j in (1, 2, 3):
                            srcv = xp_full[pt][:, soff + 1 + j:soff + 513 + j]
                            ins = v.scalar_tensor_tensor(
                                a[:, :], srcv, cw_sb[:, pt * 4 + j:pt * 4 + j + 1],
                                a[:, :], ALU.mult, ALU.add)
                        return ins
                    wt = []
                    if si == 0 and pt == 0:
                        wt += [(s_ld3, s_ld3.n), (s_c09, 1)]
                    if s_conv.n >= 4:
                        wt.append((s_xc, s_conv.n - 3))
                    if si > 0 and pt == 0:
                        wt.append((s_ymm, s_ymm.n))
                    P.op("vector", b_conv, wait=wt, inc=[(s_conv, 1)])

                    def b_silu(s, pt=pt, a=a):
                        return s.activation(xc_st[pt][:, :], a[:, :], AF.Silu,
                                            bias=cb_sb[:, pt:pt + 1])
                    wt = [(s_conv, s_conv.n)]
                    if si > 0 and pt == 0:
                        wt.append((s_u, si))
                    P.op("scalar", b_silu, wait=wt, inc=[(s_xc, 1)])

                def b_umm(t, si=si):
                    ins = None
                    for ct in range(16):
                        ins = t.matmul(ups[:, :], wb_sb[ct][:, :],
                                       xc_st[ct][:, :], start=(ct == 0),
                                       stop=(ct == 15))
                    return ins
                P.op("tensor", b_umm, wait=[(s_xc, s_xc.n), (s_scan, si)],
                     inc=[(s_u, 1)])

                def b_scan(v, si=si):
                    v.tensor_scalar_mul(u01[:, :], ups[:, :], 0.1)
                    init = zini[:, 0:1] if si == 0 else h_sec[si - 1][:, 511:512]
                    v.tensor_tensor_scan(h_sec[si][:, :], c09[:, :], u01[:, :],
                                         init, ALU.mult, ALU.add)
                    return v.tensor_copy(hb[:, :], h_sec[si][:, :])
                P.op("vector", b_scan, wait=[(s_u, s_u.n), (s_ymm, s_ymm.n)],
                     inc=[(s_scan, 1)])

                for dt in range(16):
                    bank = mm_ps[dt % 2]

                    def b_ymm(t, dt=dt, bank=bank):
                        return t.matmul(bank[:, :], wc_sb[:, dt * 128:(dt + 1) * 128],
                                        hb[:, :], start=True, stop=True)
                    wt = [(s_scan, s_scan.n)] if dt == 0 else []
                    if s_ymm.n >= 2:
                        wt.append((s_yev, s_ymm.n - 1))
                    P.op("tensor", b_ymm, wait=wt, inc=[(s_ymm, 1)])

                    def b_yev(v, dt=dt, bank=bank):
                        return v.tensor_copy(y0_st[dt][:, :], bank[:, :])
                    wt = [(s_ymm, s_ymm.n)]
                    if si > 0:
                        wt.append((s_sto, 16 * (16 * 5 + 16 * (si - 1) + dt + 1)))
                    P.op("vector", b_yev, wait=wt, inc=[(s_yev, 1)])

                yv = y0_out.rearrange("(a p) t -> a p t", p=128)
                wt_sto = (s_yev.h, s_yev.n)

                def b_yst(sy, soff=soff, wt=wt_sto, yv=yv):
                    sy.wait_ge(*wt)
                    for i in range(16):
                        sy.dma_start(out=yv[i][:, soff:soff + 512],
                                     in_=y0_st[i][:, :]).then_inc(s_sto.h, 16)
                s_sto.bump(16 * 16)
                P.raw("sync", b_yst)

            s_cy = P.sem("s_cy")

            def b_hpad(v):
                return v.tensor_copy(hpad[:, :], h_sec[3][:, :])
            P.op("vector", b_hpad, wait=[(s_scan, s_scan.n)], inc=[(s_cy, 1)])

            def b_cyst(sy):
                sy.wait_ge(s_cy.h, 1)
                sy.dma_start(out=carry_out[:, :], in_=hpad[:, 511:512]
                             ).then_inc(s_sto.h, 16)
            s_sto.bump(16)
            P.raw("sync", b_cyst)

            P.op("sync", lambda e: None, wait=[(s_sto, s_sto.n)])
            return dict(s_sto=s_sto, s_scan=s_scan, s_yev=s_yev, s_ymm=s_ymm,
                        s_u=s_u, s_xc=s_xc, s_conv=s_conv, s_cy=s_cy)



def declare_D(nc, kind_in="ExternalInput", kind_x="ExternalInput",
              xname="xd", carryname="carryin", skip_io=False):
    return dict(
        y0_in=None if skip_io else nc.dram_tensor("y0", [2048, OWN], BF16,
                                                  kind=kind_in),
        sz_in=None if skip_io else nc.dram_tensor("sz", [2048, OWN], BF16,
                                                  kind=kind_in),
        carry_in=None if skip_io else nc.dram_tensor(carryname, [16, 1], F32,
                                                     kind=kind_in),
        x_in=None if skip_io else nc.dram_tensor(xname, [D, OWN], F32,
                                                 kind=kind_x),
        wout=nc.dram_tensor("wout", [2048, D], F32, kind="ExternalInput"),
        wc2=nc.dram_tensor("wc2", [16, 2048], F32, kind="ExternalInput"),
        decay=nc.dram_tensor("decay", [128, OWN], F32, kind="ExternalInput"),
        th=nc.dram_tensor("th", [D, D], F32, kind="ExternalInput"),
        thn=nc.dram_tensor("thn", [D, D], F32, kind="ExternalInput"),
        ph=nc.dram_tensor("ph", [D, D], F32, kind="ExternalInput"),
        phn=nc.dram_tensor("phn", [D, D], F32, kind="ExternalInput"),
        out_d=nc.dram_tensor("out", [D, OWN], F32, kind="ExternalOutput"),
    )


def build_D():
    nc = bass.Bass("TRN2", target_bir_lowering=False)
    io = declare_D(nc, xname="x", carryname="carry")
    with ExitStack() as ctx:
        P = Prog(nc, ctx)
        C = Common(nc, ctx, P)
        emit_D(nc, ctx, P, C, io, pre=())
        P.play()
    return nc


def emit_D(nc, ctx, P, C, io, pre=(), carry_fold=None, carry_wait=()):
    y0_in, sz_in, carry_in, x_in = (io["y0_in"], io["sz_in"], io["carry_in"],
                                    io["x_in"])
    x_off = io.get("x_off", 0)
    wout, wc2, decay = io["wout"], io["wc2"], io["decay"]
    th, thn, ph, phn, out_d = io["th"], io["thn"], io["ph"], io["phn"], io["out_d"]
    if True:
        sb = lambda nm, s, d: ctx.enter_context(nc.sbuf_tensor('E_' + nm, s, d))
        psum = lambda nm, s: ctx.enter_context(nc.psum_tensor('E_' + nm, s, F32))

        xsec = [sb(f"x{i}", [128, 512], F32) for i in range(8)]
        out_st = [sb(f"os{i}", [128, 512], F32) for i in range(4)]
        dec_sb = sb("dec", [128, OWN], BF16)
        ct_sb = [sb(f"ct{i}", [128, D], BF16) for i in range(8)]
        st_sb = [sb(f"st{i}", [128, D], BF16) for i in range(8)]
        wo_sb = [sb(f"wo{i}", [128, D], BF16) for i in range(16)]
        wc2_sb = sb("wc2v", [16, 2048], BF16)
        cb_sb = sb("cb", [16, 1], BF16)
        cpad = sb("cpad", [16, 128], BF16)
        cwc_sb = sb("cwc", [128, 16], F32)
        y0_st2 = [[sb(f"y0s{b}_{i}", [128, 512], BF16) for i in range(16)]
                  for b in range(2)]
        sz_st = [sb(f"szs{i}", [128, 512], BF16) for i in range(16)]
        xo_st = [sb(f"xo{i}", [128, 512], F32) for i in range(8)]
        lnw = [sb(f"lnw{i}", [128, 512], BF16) for i in range(8)]
        sq = [sb(f"sq{i}", [128, 512], BF16) for i in range(8)]
        rows = [sb(f"row{i}", [1, 512], F32) for i in range(4)]
        re2 = [sb(f"re2_{b}", [128, 512], F32) for b in range(2)]
        im2 = [sb(f"im2_{b}", [128, 512], F32) for b in range(2)]
        mag = [sb(f"mag_{b}", [128, 512], F32) for b in range(2)]
        st_ps = psum("st_ps", [64, 512])
        bc_ps = [psum("bc0", [128, 512]), psum("bc1", [128, 512])]
        mm_ps = [psum(f"mm{i}", [128, 512]) for i in range(4)]
        cwc_ps = psum("cwc_ps", [128, 144])

        s_ldx = P.sem("s_ldx")
        s_ldw = P.sem("s_ldw")
        ln_sems_D = tuple(P.sem() for _ in range(6))
        s_trg = P.sem("s_trg")
        s_sin = P.sem("s_sin")
        s_cwm = P.sem("s_cwm")
        s_cwc = P.sem("s_cwc")
        s_ld = P.sem("s_ld")
        s_cor = P.sem("s_cor")
        s_wmm = P.sem("s_wmm")
        s_xo = P.sem("s_xo")
        s_fmm = P.sem("s_fmm")
        s_sqv = P.sem("s_sqv")
        s_mag = P.sem("s_mag")
        s_fin = P.sem("s_fin")
        s_sto = P.sem("s_sto")

        dma_tiles(P, "gpsimd", s_ldw, wout, wo_sb, 16, wait=pre)
        xv_in = x_in.rearrange("(a p) t -> a p t", p=128)
        xo_ = x_off

        def b_ldm(g):
            g.dma_start(out=dec_sb[:, :], in_=decay[:, :]).then_inc(s_ldw.h, 16)
            g.dma_start(out=wc2_sb[:, :], in_=wc2[:, :]).then_inc(s_ldw.h, 16)
        s_ldw.bump(32)
        P.raw("gpsimd", b_ldm)

        tctx = ExitStack()
        trg = [tctx.enter_context(nc.sbuf_tensor(f"E_trg{i}", [128, D], F32))
               for i in range(2)]
        thv = th.rearrange("(a p) t -> a p t", p=128)
        thnv = thn.rearrange("(a p) t -> a p t", p=128)
        phv = ph.rearrange("(a p) t -> a p t", p=128)
        phnv = phn.rearrange("(a p) t -> a p t", p=128)
        for pt in range(8):
            for which, (av, bv, dst, bias) in enumerate(
                    ((thv, thnv, ct_sb, True), (phv, phnv, st_sb, False))):
                u = 2 * pt + which
                tA, tB = trg[0], trg[1]
                tS = tA
                prev2 = s_sin.n

                def b_ldA(g, pt=pt, av=av, tA=tA, prev2=prev2):
                    if prev2:
                        g.wait_ge(s_sin.h, prev2)
                    g.dma_start(out=tA[:, :], in_=av[pt]).then_inc(s_trg.h, 16)
                s_trg.bump(16)
                P.raw("gpsimd", b_ldA)

                def b_ldB(g, pt=pt, bv=bv, tB=tB, prev2=prev2):
                    if prev2:
                        g.wait_ge(s_sin.h, prev2)
                    g.dma_start(out=tB[:, :], in_=bv[pt]).then_inc(s_trg.h, 16)
                s_trg.bump(16)
                P.raw("sync", b_ldB)

                def b_add(v, tA=tA, tB=tB, tS=tS):
                    return v.tensor_add(tS[:, :], tA[:, :], tB[:, :])
                P.op("vector", b_add, wait=[(s_trg, s_trg.n),
                                            (s_sin, prev2)],
                     inc=[(s_trg, 1)])

                if bias:
                    def b_sin(s, tS=tS):
                        return s.activation(tS[:, :], tS[:, :], AF.Sin,
                                            bias=C.c(math.pi / 2))
                    P.op("scalar", b_sin, wait=[(s_trg, s_trg.n), (C.s_init, 1)],
                         inc=[(s_trg, 1)])

                    def b_ctd(v, pt=pt, dst=dst, tS=tS):
                        return v.tensor_scalar_add(dst[pt][:, :], tS[:, :],
                                                   -1.0)
                    P.op("vector", b_ctd, wait=[(s_trg, s_trg.n)],
                         inc=[(s_sin, 1)])
                else:
                    def b_sin(s, pt=pt, dst=dst, tS=tS):
                        return s.activation(dst[pt][:, :], tS[:, :], AF.Sin)
                    P.op("scalar", b_sin, wait=[(s_trg, s_trg.n), (C.s_init, 1)],
                         inc=[(s_sin, 1)])

        tctx.close()
        trig_done = [(s_sin, s_sin.n), (s_trg, s_trg.n)]

        def b_ldc(g, cw=list(carry_wait)):
            for s, v in cw:
                g.wait_ge(s.h if isinstance(s, Sem) else s, v)
            g.dma_start(out=cb_sb[:, :], in_=carry_in[:, :]).then_inc(s_ldw.h, 16)
        s_ldw.bump(16)
        P.raw("gpsimd", b_ldc)

        def b_cini(v):
            if carry_fold is not None:
                carry_fold(v, cb_sb)
            return v.memset(cpad[:, :], 0.0)
        P.op("vector", b_cini, wait=[(s_ldw, s_ldw.n)], inc=[(s_cwm, 1)])

        def b_cwm(t):
            for dt in range(16):
                t.matmul(cwc_ps[:, dt:dt + 1], wc2_sb[:, dt * 128:(dt + 1) * 128],
                         cb_sb[:, :], start=True, stop=True)
            return t.matmul(cwc_ps[:, 16:144], wc2_sb[:, 0:128], cpad[:, :],
                            start=True, stop=True)
        P.op("tensor", b_cwm, wait=[(s_cwm, 1)], inc=[(s_cwm, 1)])

        def b_cwe(v):
            return v.tensor_copy(cwc_sb[:, :], cwc_ps[:, 0:16])
        P.op("vector", b_cwe, wait=[(s_cwm, 2)], inc=[(s_cwc, 1)])

        y0v = y0_in.rearrange("(a p) t -> a p t", p=128)
        szv = sz_in.rearrange("(a p) t -> a p t", p=128)
        ov = out_d.rearrange("(a p) t -> a p t", p=128)

        for si, (soff, sw) in enumerate(GRID_OWN):
            y0_st = y0_st2[si % 2]
            yb_st = y0_st

            def b_ldy(sy, soff=soff, si=si, wmmn=max(0, s_wmm.n - 8),
                      y0_st=y0_st, td=list(trig_done)):
                if si == 0:
                    for h, v in td:
                        sy.wait_ge(h.h if isinstance(h, Sem) else h, v)
                if si > 1:
                    sy.wait_ge(s_wmm.h, wmmn)
                for i in range(16):
                    sy.dma_start(out=y0_st[i][:, :],
                                 in_=y0v[i][:, soff:soff + 512]).then_inc(s_ld.h, 16)
            s_ld.bump(16 * 16)
            P.raw("sync", b_ldy)

            def b_ldsz(sy, soff=soff, si=si, corn=s_cor.n):
                if si > 0:
                    sy.wait_ge(s_cor.h, corn)
                for i in range(16):
                    sy.dma_start(out=sz_st[i][:, :],
                                 in_=szv[i][:, soff:soff + 512]).then_inc(s_ld.h, 16)
            s_ld.bump(16 * 16)
            P.raw("sync", b_ldsz)

            def b_ldx(sy, soff=soff, si=si, xon=s_xo.n):
                if si > 0:
                    sy.wait_ge(s_xo.h, xon)
                for i in range(8):
                    sy.dma_start(out=xsec[i][:, :],
                                 in_=xv_in[i][:, xo_ + soff:xo_ + soff + 512]
                                 ).then_inc(s_ld.h, 16)
            s_ld.bump(16 * 8)
            P.raw("sync", b_ldx)

            def b_cor(v, soff=soff, si=si, y0_st=y0_st, yb_st=yb_st):
                ins = None
                for pt in range(16):
                    v.scalar_tensor_tensor(yb_st[pt][:, :],
                                           dec_sb[:, soff:soff + 512],
                                           cwc_sb[:, pt:pt + 1],
                                           y0_st[pt][:, :], ALU.mult, ALU.add)
                    ins = v.tensor_mul(yb_st[pt][:, :], yb_st[pt][:, :],
                                       sz_st[pt][:, :])
                return ins
            wt = [(s_ld, s_ld.n), (s_cwc, 1)]
            if si > 1:
                wt.append((s_wmm, s_wmm.n - 8))
            P.op("vector", b_cor, wait=wt, inc=[(s_cor, 1)])

            for dm in range(8):
                bank = mm_ps[dm % 2]

                def b_wmm(t, dm=dm, bank=bank, yb_st=yb_st):
                    ins = None
                    for ct in range(16):
                        ins = t.matmul(bank[:, :],
                                       wo_sb[ct][:, dm * 128:(dm + 1) * 128],
                                       yb_st[ct][:, :],
                                       start=(ct == 0), stop=(ct == 15))
                    return ins
                wt = [(s_cor, s_cor.n)] if dm == 0 else []
                if dm == 0 and si == 0:
                    wt.append((s_ldw, s_ldw.n))
                if dm == 0 and si > 0:
                    wt.append((s_sqv, s_sqv.n))
                if s_wmm.n >= 2:
                    wt.append((s_xo, s_wmm.n - 1))
                P.op("tensor", b_wmm, wait=wt, inc=[(s_wmm, 1)])

                def b_xo(v, dm=dm, bank=bank):
                    return v.tensor_add(xo_st[dm][:, :], xsec[dm][:, :],
                                        bank[:, :])
                P.op("vector", b_xo, wait=[(s_wmm, s_wmm.n)], inc=[(s_xo, 1)])

            pre = [(s_xo, s_xo.n), (C.s_init, 1)]
            s_app, _ = emit_ln(P, C, xo_st, lnw, [(0, 512)],
                               dict(sq=sq, st_ps=st_ps, bc_ps=bc_ps,
                                    rows=rows), pre, sems=ln_sems_D)

            for ot in range(8):
                g = 8 * si + ot
                b_re = mm_ps[2] if ot % 2 == 0 else mm_ps[0]
                b_im = mm_ps[3] if ot % 2 == 0 else mm_ps[1]
                rr, ii, mm = re2[g % 2], im2[g % 2], mag[g % 2]

                def b_fmm(t, ot=ot, b_re=b_re, b_im=b_im):
                    for it in range(8):
                        t.matmul(b_re[:, :], ct_sb[it][:, ot * 128:(ot + 1) * 128],
                                 lnw[it][:, :], start=(it == 0), stop=(it == 7))
                    ins = None
                    for it in range(8):
                        ins = t.matmul(b_im[:, :],
                                       st_sb[it][:, ot * 128:(ot + 1) * 128],
                                       lnw[it][:, :], start=(it == 0),
                                       stop=(it == 7))
                    return ins
                wt = [(s_app, s_app.n)] if ot == 0 else []
                if si == 0 and ot == 0:
                    wt.append((s_sin, s_sin.n))
                if s_sqv.n >= 2:
                    wt.append((s_sqv, s_sqv.n - 1))
                P.op("tensor", b_fmm, wait=wt, inc=[(s_fmm, 1)])

                def b_sq2(s, b_re=b_re, b_im=b_im, rr=rr, ii=ii):
                    s.activation(rr[:, :], b_re[:, :], AF.Square)
                    return s.activation(ii[:, :], b_im[:, :], AF.Square)
                wt = [(s_fmm, s_fmm.n)]
                if s_sqv.n >= 2:
                    wt.append((s_mag, s_mag.n - 3))
                P.op("scalar", b_sq2, wait=wt, inc=[(s_sqv, 1)])

                def b_sum(v, rr=rr, ii=ii, mm=mm):
                    return v.tensor_add(mm[:, :], rr[:, :], ii[:, :])
                wt = [(s_sqv, s_sqv.n)]
                if s_fin.n >= 2:
                    wt.append((s_fin, s_fin.n - 1))
                P.op("vector", b_sum, wait=wt, inc=[(s_mag, 1)])

                def b_sqrt(s, mm=mm):
                    s.activation(mm[:, :], mm[:, :], AF.Ln)
                    return s.activation(mm[:, :], mm[:, :], AF.Exp,
                                        scale=C.c(0.5))
                P.op("scalar", b_sqrt, wait=[(s_mag, s_mag.n)],
                     inc=[(s_mag, 1)])

                def b_out(v, ot=ot, si=si, mm=mm, og=out_st[g % 4]):
                    return v.tensor_add(og[:, :], xo_st[ot][:, :],
                                        mm[:, :])
                wt = [(s_mag, s_mag.n)]
                if g >= 4:
                    wt.append((s_sto, 16 * (g - 3)))
                P.op("vector", b_out, wait=wt, inc=[(s_fin, 1)])

                wt_sto = (s_fin.h, s_fin.n)

                def b_sto(sy, ot=ot, soff=soff, wt=wt_sto, og=out_st[g % 4]):
                    sy.wait_ge(*wt)
                    sy.dma_start(out=ov[ot][:, soff:soff + 512],
                                 in_=og[:, :]).then_inc(s_sto.h, 16)
                s_sto.bump(16)
                P.raw("sync", b_sto)

        P.op("sync", lambda e: None, wait=[(s_sto, s_sto.n)])
        return dict(s_sto=s_sto, s_fin=s_fin)



GRID_T = [(i * 512, 512) for i in range(8)]


def declare_A(nc, kind_out="ExternalOutput"):
    return dict(
        x_in=nc.dram_tensor("x", [D, T], F32, kind="ExternalInput"),
        q12w=nc.dram_tensor("q12w", [D, 1024], F32, kind="ExternalInput"),
        k12w=nc.dram_tensor("k12w", [D, 1024], F32, kind="ExternalInput"),
        vw=nc.dram_tensor("vw", [D, 1024], F32, kind="ExternalInput"),
        lamv=nc.dram_tensor("lamv", [1, 2048], F32, kind="ExternalInput"),
        tri=nc.dram_tensor("tri", [128, 128], BF16, kind="ExternalInput"),
        yn_out=nc.dram_tensor("ynorm", [1024, T], BF16, kind=kind_out),
    )


def build_A():
    nc = bass.Bass("TRN2", target_bir_lowering=False)
    io = declare_A(nc)
    with ExitStack() as ctx:
        P = Prog(nc, ctx)
        C = Common(nc, ctx, P)
        emit_A(nc, ctx, P, C, io)
        P.play()
    return nc


def emit_A(nc, ctx, P, C, io):
    x_in, q12w, k12w, vw = io["x_in"], io["q12w"], io["k12w"], io["vw"]
    lamv, tri, yn_out = io["lamv"], io["tri"], io["yn_out"]
    if True:
        sb = lambda nm, s, d: ctx.enter_context(nc.sbuf_tensor('A_' + nm, s, d))
        psum = lambda nm, s: ctx.enter_context(nc.psum_tensor('A_' + nm, s, F32))

        lnx = [sb(f"lnx{i}", [128, T], BF16) for i in range(8)]
        tri_sb = sb("trisb", [128, 128], BF16)
        lam_sb = sb("lamsb", [1, 2048], F32)
        nlam = sb("nlam", [1, 512], F32)
        lr1 = sb("lr1", [1, 512], F32)
        lr2 = sb("lr2", [1, 512], F32)
        s_ldc = P.sem("s_ldc")

        def b_ldc(g):
            g.dma_start(out=tri_sb[:, :], in_=tri[:, :]).then_inc(s_ldc.h, 16)
            g.dma_start(out=lam_sb[:, :], in_=lamv[:, :]).then_inc(s_ldc.h, 16)
        s_ldc.bump(32)
        P.raw("gpsimd", b_ldc)

        s_lam = P.sem("s_lam")

        def b_lam1(v):
            v.tensor_mul(lam_sb[0:1, 0:512], lam_sb[0:1, 0:512],
                         lam_sb[0:1, 512:1024])
            v.tensor_mul(lam_sb[0:1, 1024:1536], lam_sb[0:1, 1024:1536],
                         lam_sb[0:1, 1536:2048])
            p1 = lam_sb[0:1, 0:512].rearrange("p (h d) -> p h d", d=64)
            p2 = lam_sb[0:1, 1024:1536].rearrange("p (h d) -> p h d", d=64)
            v.reduce_sum(lr1[0:1, 0:8].rearrange("p (h d) -> p h d", d=1), p1,
                         axis=mybir.AxisListType.X)
            v.reduce_sum(lr2[0:1, 0:8].rearrange("p (h d) -> p h d", d=1),
                         p2, axis=mybir.AxisListType.X)
            return v.tensor_copy(lam_sb[0:1, 0:512], lam_sb[0:1, 0:512])
        P.op("vector", b_lam1, wait=[(s_ldc, 32)], inc=[(s_lam, 1)])

        def b_lam2(s):
            s.activation(lr1[:, 0:8], lr1[:, 0:8], AF.Exp)
            s.activation(lr2[:, 0:8], lr2[:, 0:8], AF.Exp)
            return s.activation(lam_sb[0:1, 0:512], lam_sb[0:1, 0:512],
                                AF.Identity)
        P.op("scalar", b_lam2, wait=[(s_lam, 1)], inc=[(s_lam, 1)])

        def b_lam3(v):
            v.scalar_tensor_tensor(nlam[:, 0:8], lr1[:, 0:8], -1.0,
                                   lr2[:, 0:8], ALU.mult, ALU.add)
            v.tensor_copy(lam_sb[0:1, 0:512], lam_sb[0:1, 0:512])
            v.tensor_scalar_add(nlam[:, 0:8], nlam[:, 0:8], -LAMBDA_INIT)
            return v.tensor_copy(lam_sb[0:1, 0:512], lam_sb[0:1, 0:512])
        P.op("vector", b_lam3, wait=[(s_lam, 2)], inc=[(s_lam, 1)])

        with ExitStack() as c1:
            sb1 = lambda nm, s, d: c1.enter_context(nc.sbuf_tensor('A_' + nm, s, d))
            ps1 = ExitStack()
            st_ps = ps1.enter_context(nc.psum_tensor("st_ps", [64, 512], F32))
            bc_ps = [ps1.enter_context(nc.psum_tensor(f"bc{i}", [128, 512], F32))
                     for i in range(2)]
            xch = [[sb1(f"xc{b}_{i}", [128, 512], F32) for i in range(8)]
                   for b in range(2)]
            sq = [sb1(f"sq{i}", [128, 512], BF16) for i in range(8)]
            rows = [sb1(f"row{i}", [1, 512], F32) for i in range(4)]
            s_lx = P.sem("s_lx")
            ln_sems = tuple(P.sem() for _ in range(6))
            xv = x_in.rearrange("(a p) t -> a p t", p=128)

            app_sems = []
            for k, (off, w) in enumerate(GRID_T):
                buf = xch[k % 2]

                def b_lx(sy, off=off, k=k, buf=buf, prev=app_sems[-2] if
                         len(app_sems) >= 2 else None):
                    if prev is not None:
                        sy.wait_ge(prev[0].h, prev[1])
                    for i in range(8):
                        sy.dma_start(out=buf[i][:, :],
                                     in_=xv[i][:, off:off + w]
                                     ).then_inc(s_lx.h, 16)
                s_lx.bump(16 * 8)
                P.raw("sync", b_lx)

                pre = [(s_lx, s_lx.n), (C.s_init, 1)]
                if app_sems:
                    pre.append(app_sems[-1])
                s_app, _ = emit_ln(P, C, buf, [OffView(lt, off) for lt in lnx],
                                   [(0, 512)],
                                   dict(sq=sq, st_ps=st_ps, bc_ps=bc_ps,
                                        rows=rows), pre, sems=ln_sems)
                app_sems.append((s_app, s_app.n))
            ln_done = app_sems[-1]
            lnall = [(s, v) for s, v in app_sems]
            P.op("gpsimd", lambda g: None,
                 wait=[(s, v) for s, v in app_sems])
            ps1.close()

        s12 = [psum("s12a", [128, 1024]), psum("s12b", [128, 1024])]
        y1_ps = psum("y1", [128, 512])
        y2_ps = psum("y2", [128, 512])
        sum_ps = psum("sums", [64, 512])
        bc2_ps = psum("bc2", [128, 512])

        qw_sb = [sb(f"qw{i}", [128, 1024], BF16) for i in range(2)]
        kw_sb = [sb(f"kw{i}", [128, 1024], BF16) for i in range(2)]
        vw_sb = [sb(f"vw{i}", [128, 1024], BF16) for i in range(2)]
        q12_sb = sb("q12", [128, T], BF16)
        k12_sb = sb("k12", [128, T], BF16)
        v_sb = sb("vsb", [128, T], BF16)
        e12 = [sb(f"e12_{i}", [128, 1024], BF16) for i in range(2)]
        r2_sb = sb("r2", [1, 512], F32)
        gneg = sb("gneg", [1, 512], F32)
        s1_sb = sb("s1r", [1, 512], F32)
        s1e = sb("s1e", [1, 512], F32)
        gb_sb = sb("gb", [128, 512], F32)
        t2_sb = sb("t2", [128, 512], F32)
        yc_bf = sb("ycb", [128, 512], BF16)
        yc2_bf = sb("yc2b", [128, 512], BF16)
        t3_sb = sb("t3", [128, 512], F32)
        yn_st = sb("ynst", [128, 512], BF16)
        mu_r = sb("amur", [1, 512], F32)
        m2_r = sb("m2r", [1, 512], F32)
        var_r = sb("varr", [1, 512], F32)
        rst_r = sb("rstr", [1, 512], F32)

        s_wld = P.sem("s_wld")
        s_pj = P.sem("s_pj")
        s_pe = P.sem("s_pe")
        s_qk = P.sem("s_qk")
        s_e = P.sem("s_e")
        s_tri = P.sem("s_tri")
        s_v = P.sem("s_v")
        s_ep = P.sem("s_ep")
        s_ev = P.sem("s_ev")
        s_er = P.sem("s_er")
        s_sto = P.sem("s_sto")

        ynv = yn_out.rearrange("(a p) t -> a p t", p=128)
        qv = q12w.rearrange("(a p) t -> a p t", p=128)
        kv = k12w.rearrange("(a p) t -> a p t", p=128)
        vv = vw.rearrange("(a p) t -> a p t", p=128)

        def load_head(h):
            hb = h % 2

            def b_lw(g, h=h, hb=hb, pjn=s_pj.n):
                if h >= 2:
                    g.wait_ge(s_pj.h, pjn)
                for i in range(8):
                    g.dma_start(out=qw_sb[hb][:, i * 128:(i + 1) * 128],
                                in_=qv[i][:, h * 128:(h + 1) * 128]
                                ).then_inc(s_wld.h, 16)
                for i in range(8):
                    g.dma_start(out=kw_sb[hb][:, i * 128:(i + 1) * 128],
                                in_=kv[i][:, h * 128:(h + 1) * 128]
                                ).then_inc(s_wld.h, 16)
                for i in range(8):
                    g.dma_start(out=vw_sb[hb][:, i * 128:(i + 1) * 128],
                                in_=vv[i][:, h * 128:(h + 1) * 128]
                                ).then_inc(s_wld.h, 16)
            s_wld.bump(16 * 24)
            P.raw("gpsimd", b_lw)

        load_head(0)
        load_head(1)

        pend_t = []
        pend_s = []
        tile_gates = []
        last_ep2 = (s_ev, 0)
        for h in range(8):
            hb = h % 2
            pbanks = [y1_ps[:, 0:512], y2_ps[:, 0:512],
                      s12[0][:, 0:512], s12[0][:, 512:1024]]
            for which, (wsb, dst) in enumerate(((qw_sb[hb], q12_sb),
                                                (kw_sb[hb], k12_sb))):
                for k, (off, w) in enumerate(GRID_T):
                    bank = pbanks[k % 4]

                    def b_p(t, wsb=wsb, off=off, w=w, bank=bank):
                        ins = None
                        for ct in range(8):
                            ins = t.matmul(bank[:, :w],
                                           wsb[:, ct * 128:(ct + 1) * 128],
                                           lnx[ct][:, off:off + w],
                                           start=(ct == 0), stop=(ct == 7))
                        return ins
                    wt = []
                    if which == 0 and k == 0:
                        wt.append((s_wld, 16 * 24 * (h + 1)))
                        if h == 0:
                            wt += list(lnall) + [(s_lam, 3)]
                        else:
                            wt.append((s_v, s_v.n))
                            wt.append((s_ev, s_ev.n))
                            wt.append((s_e, s_e.n))
                    if s_pj.n >= 4:
                        wt.append((s_pe, s_pj.n - 3))
                    P.op("tensor", b_p, wait=wt, inc=[(s_pj, 1)])

                    def b_pe(v, dst=dst, off=off, w=w, bank=bank):
                        return v.tensor_copy(dst[:, off:off + w], bank[:, :w])
                    P.op("vector", b_pe, wait=[(s_pj, s_pj.n)],
                         inc=[(s_pe, 1)])
            for tc in range(32):
                bank = pbanks[tc % 4]

                def b_vp(t, tc=tc, hb=hb, bank=bank):
                    ins = None
                    for ct in range(8):
                        ins = t.matmul(bank[:, 0:128],
                                       lnx[ct][:, tc * 128:(tc + 1) * 128],
                                       vw_sb[hb][:, ct * 128:(ct + 1) * 128],
                                       start=(ct == 0), stop=(ct == 7))
                    return ins
                wt = []
                if s_pj.n >= 4:
                    wt.append((s_pe, s_pj.n - 3))
                P.op("tensor", b_vp, wait=wt, inc=[(s_pj, 1)])

                def b_vpe(v, tc=tc, bank=bank):
                    return v.tensor_copy(v_sb[:, tc * 128:(tc + 1) * 128],
                                         bank[:, 0:128])
                P.op("vector", b_vpe, wait=[(s_pj, s_pj.n)], inc=[(s_pe, 1)])

            proj_done = (s_pe, s_pe.n)
            load_head(h + 2) if h + 2 < 8 else None

            for qb in range(8):
                nt = 4 * (qb + 1)
                vs_recs = []

                def mk_vs(i, qb=qb, nt=nt):
                    b = i % 2

                    def b_vs(t, i=i, b=b, st=(i == 0), sp=(i == nt - 1)):
                        t.matmul(y1_ps[:, :],
                                 v_sb[:, i * 128:(i + 1) * 128],
                                 e12[b][:, 0:512], start=st, stop=sp)
                        t.matmul(y2_ps[:, :],
                                 v_sb[:, i * 128:(i + 1) * 128],
                                 e12[b][:, 512:1024], start=st, stop=sp)
                        t.matmul(sum_ps[0:1, :], C.ones128b[:, :],
                                 e12[b][:, 0:512], start=st, stop=sp,
                                 tile_position=(0, 0))
                        return t.matmul(sum_ps[32:33, :],
                                        C.ones128b[:, :],
                                        e12[b][:, 512:1024],
                                        start=st, stop=sp,
                                        tile_position=(0, 32))
                    return b_vs

                for i in range(nt):
                    b = i % 2
                    joff = max(0, (i - 4 * qb) * 128)

                    def b_qk(t, i=i, qb=qb, b=b, joff=joff):
                        q0 = qb * 512
                        t.matmul(s12[b][0:128, joff:512],
                                 k12_sb[0:64, i * 128:(i + 1) * 128],
                                 q12_sb[0:64, q0 + joff:q0 + 512],
                                 start=True, stop=True, tile_position=(0, 0))
                        return t.matmul(s12[b][0:128, 512 + joff:1024],
                                        k12_sb[64:128, i * 128:(i + 1) * 128],
                                        q12_sb[64:128, q0 + joff:q0 + 512],
                                        start=True, stop=True,
                                        tile_position=(64, 0))
                    wt = []
                    if i == 0 and qb == 0 and h == 0:
                        wt.append(proj_done)
                    if h > 0 and i == 0 and qb == 0:
                        wt.append((s_pe, s_pe.n))
                    if s_qk.n >= 2:
                        wt.append((s_e, s_qk.n - 1))
                    P.op("tensor", b_qk, wait=wt, inc=[(s_qk, 1)])
                    if pend_t:
                        P.emit_rec(pend_t.pop(0))

                    def b_exp(s, b=b):
                        return s.activation(e12[b][:, :], s12[b][:, :], AF.Exp,
                                            bias=C.c(ATT_BIAS),
                                            scale=C.c(0.125))
                    wt = [(s_qk, s_qk.n)]
                    if s_e.n >= 2:
                        wt.append((s_v, s_e.n - 1))
                        wt.append((s_tri, s_tri.n))
                    P.op("scalar", b_exp, wait=wt, inc=[(s_e, 1)])
                    if pend_s:
                        P.emit_rec(pend_s.pop(0))

                    if i >= 4 * qb:
                        def b_tri(v, b=b, joff=joff):
                            if joff > 0:
                                v.memset(e12[b][:, 0:joff], 0.0)
                                v.memset(e12[b][:, 512:512 + joff], 0.0)
                            v.tensor_mul(e12[b][:, joff:joff + 128],
                                         e12[b][:, joff:joff + 128],
                                         tri_sb[:, :])
                            return v.tensor_mul(
                                e12[b][:, 512 + joff:512 + joff + 128],
                                e12[b][:, 512 + joff:512 + joff + 128],
                                tri_sb[:, :])
                        P.op("vector", b_tri, wait=[(s_e, s_e.n)],
                             inc=[(s_tri, 1)])
                        tile_gates.append((s_tri, s_tri.n))
                    else:
                        tile_gates.append((s_e, s_e.n))

                    if i > 0:
                        wt = [tile_gates[-2]]
                        if i == 1:
                            wt.append(last_ep2)
                        P.op("tensor", mk_vs(i - 1), wait=wt, inc=[(s_v, 1)])
                P.op("tensor", mk_vs(nt - 1), wait=[tile_gates[-1]],
                     inc=[(s_v, 1)])
                tile_gates.clear()

                def b_ep1(v, h=h):
                    v.reciprocal(r2_sb[:, :], sum_ps[32:33, :])
                    v.tensor_copy(s1_sb[:, :], sum_ps[0:1, :])
                    v.scalar_tensor_tensor(gneg[:, :], sum_ps[0:1, :],
                                           nlam[0:1, h:h + 1], r2_sb[:, :],
                                           ALU.mult, ALU.mult)
                    return v.scalar_tensor_tensor(s1e[:, :], s1_sb[:, :],
                                                  C.c(EPS)[0:1, :], s1_sb[:, :],
                                                  ALU.mult, ALU.mult)
                P.op("vector", b_ep1, wait=[(s_v, s_v.n)], inc=[(s_ev, 1)])

                def b_gb(t):
                    return t.matmul(bc2_ps[:, :], C.ones1f[:, :], gneg[:, :],
                                    start=True, stop=True)
                pend_t.append(P.rec("tensor", b_gb, wait=[(s_ev, s_ev.n)],
                                    inc=[(s_ep, 1)]))

                def b_ep2(v):
                    v.tensor_copy(gb_sb[:, :], bc2_ps[:, :])
                    v.tensor_mul(t2_sb[:, :], gb_sb[:, :], y2_ps[:, :])
                    v.tensor_add(yc_bf[:, :], y1_ps[:, :], t2_sb[:, :])
                    return v.tensor_mul(yc2_bf[:, :], yc_bf[:, :],
                                        yc_bf[:, :])
                P.op("vector", b_ep2, wait=[(s_ep, s_ep.n)], inc=[(s_ev, 1)])
                last_ep2 = (s_ev, s_ev.n)

                def b_st2(t):
                    t.matmul(bc2_ps[0:1, :], C.ones128b[:, :], yc_bf[:, :],
                             start=True, stop=True, tile_position=(0, 0))
                    return t.matmul(bc2_ps[32:33, :], C.ones128b[:, :],
                                    yc2_bf[:, :], start=True, stop=True,
                                    tile_position=(0, 32))
                pend_t.append(P.rec("tensor", b_st2, wait=[(s_ev, s_ev.n)],
                                    inc=[(s_ep, 1)]))

                def b_ep3(v):
                    v.tensor_scalar_mul(mu_r[:, :], bc2_ps[0:1, :], 1.0 / 128)
                    v.tensor_mul(m2_r[:, :], mu_r[:, :], mu_r[:, :])
                    v.scalar_tensor_tensor(var_r[:, :], bc2_ps[32:33, :],
                                           1.0 / 128, m2_r[:, :], ALU.mult,
                                           ALU.subtract)
                    return v.tensor_add(var_r[:, :], var_r[:, :], s1e[:, :])
                P.op("vector", b_ep3, wait=[(s_ep, s_ep.n)], inc=[(s_ev, 1)])

                def b_ep4(s):
                    s.activation(rst_r[:, :], var_r[:, :], AF.Ln)
                    return s.activation(rst_r[:, :], rst_r[:, :], AF.Exp,
                                        scale=C.c(-0.5)[0:1, :],
                                        bias=C.c(math.log(1.0 - LAMBDA_INIT))[0:1, :])
                pend_s.append(P.rec("scalar", b_ep4, wait=[(s_ev, s_ev.n)],
                                    inc=[(s_er, 1)]))

                def b_mb(t):
                    return t.matmul(bc2_ps[:, :], C.ones1f[:, :], mu_r[:, :],
                                    start=True, stop=True)
                pend_t.append(P.rec("tensor", b_mb,
                                    wait=[(s_ev, s_ev.n), (s_er, s_er.n)],
                                    inc=[(s_ep, 1)]))

                def b_ep5(v):
                    return v.tensor_sub(t3_sb[:, :], yc_bf[:, :],
                                        bc2_ps[:, :])
                P.op("vector", b_ep5, wait=[(s_ep, s_ep.n)], inc=[(s_ev, 1)])

                def b_rb(t):
                    return t.matmul(bc2_ps[:, :], C.ones1f[:, :], rst_r[:, :],
                                    start=True, stop=True)
                pend_t.append(P.rec("tensor", b_rb, wait=[(s_ev, s_ev.n)],
                                    inc=[(s_ep, 1)]))

                def b_ep6(v):
                    return v.tensor_mul(yn_st[:, :], t3_sb[:, :],
                                        bc2_ps[:, :])
                wt = [(s_ep, s_ep.n)]
                if s_sto.n:
                    wt.append((s_sto, s_sto.n))
                P.op("vector", b_ep6, wait=wt, inc=[(s_ev, 1)])

                def b_yst(sy, h=h, qb=qb, evn=s_ev.n):
                    sy.wait_ge(s_ev.h, evn)
                    sy.dma_start(out=ynv[h][:, qb * 512:(qb + 1) * 512],
                                 in_=yn_st[:, :]).then_inc(s_sto.h, 16)
                s_sto.bump(16)
                P.raw("sync", b_yst)

                if qb == 7:
                    while pend_t:
                        P.emit_rec(pend_t.pop(0))
                    while pend_s:
                        P.emit_rec(pend_s.pop(0))

        P.op("sync", lambda e: None, wait=[(s_sto, s_sto.n)])
        return dict(s_sto=s_sto, s_v=s_v, s_ev=s_ev, s_ep=s_ep, s_er=s_er,
                    s_pe=s_pe, s_pj=s_pj, s_e=s_e, s_tri=s_tri, s_qk=s_qk,
                    lnx=lnx, q12_sb=q12_sb, k12_sb=k12_sb, v_sb=v_sb)



GRID_T = [(i * 512, 512) for i in range(8)]


class OffView:

    def __init__(self, t, off):
        self.t = t
        self.off = off

    def __getitem__(self, idx):
        p, f = idx
        assert isinstance(f, slice)
        return self.t[p, f.start + self.off:f.stop + self.off]



class _APView:

    def __init__(self, ap_fn):
        self._fn = ap_fn

    def __getitem__(self, idx):
        return self._fn(idx)


def build_fused():
    FDBG = bool(os.environ.get("FDBG"))
    KI = "ExternalOutput" if FDBG else "Internal"
    nc = bass.Bass("TRN2", target_bir_lowering=False)
    ioA = declare_A(nc, kind_out="Internal")
    cwm = nc.dram_tensor("cwm", [D, D], F32, kind="ExternalInput")
    xw_d = nc.dram_tensor("xw", [D, L], F32, kind="ExternalInput")
    um_d = nc.dram_tensor("umask", [128, 1], F32, kind="ExternalInput")
    wr = nc.dram_tensor("wr", [D, D], F32, kind="ExternalInput")
    wk = nc.dram_tensor("wk", [D, D], F32, kind="ExternalInput")
    wv = nc.dram_tensor("wv", [D, D], F32, kind="ExternalInput")
    mur = nc.dram_tensor("mur", [128, 8], F32, kind="ExternalInput")
    muk = nc.dram_tensor("muk", [128, 8], F32, kind="ExternalInput")
    cwp = nc.dram_tensor("cwp", [2 * D, OWN], F32, kind="Internal")
    rs_out = nc.dram_tensor("rsout", [D, OWN], F32, kind="Internal")
    agh_in = nc.dram_tensor("aghin", [D, HALO], F32, kind="Internal")
    agh_out = nc.dram_tensor("aghout", [2 * D, HALO], F32, kind="Internal")
    cy_out = nc.dram_tensor("cyout", [32, 1], F32, kind="Internal")
    xp_d = nc.dram_tensor("xp", [D, L], F32, kind=KI)
    ioC = declare_C(nc, kind_out=KI, skip_x=True)
    ioC["x_in"] = xp_d
    if FDBG:
        cdb = ioC["carry_out"]
        ioC["carry_out"] = nc.dram_tensor("carryint", [16, 1], F32,
                                          kind="Internal")
    ioD = declare_D(nc, skip_io=True)
    ioD["y0_in"] = ioC["y0_out"]
    ioD["sz_in"] = ioC["sz_out"]
    ioD["x_in"] = xp_d
    ioD["x_off"] = HALO
    ioD["carry_in"] = _APView(lambda idx: cy_out[0:16, 0:1])

    with ExitStack() as ctx:
        P = Prog(nc, ctx)
        C = Common(nc, ctx, P)
        sb = lambda nm, s, d: ctx.enter_context(nc.sbuf_tensor(nm, s, d))
        um_sb = sb("umsb", [128, 1], F32)
        s_um = P.sem("s_um")

        def b_um(sy):
            sy.dma_start(out=um_sb[:, :], in_=um_d[:, :]).then_inc(s_um.h, 16)
        s_um.bump(16)
        P.raw("sync", b_um)

        with ExitStack() as ctxA:
            gA = emit_A(nc, ctxA, P, C, ioA)
        gateA = [(gA[k], gA[k].n) for k in
                 ("s_sto", "s_v", "s_ev", "s_ep", "s_er", "s_pe")]

        s_rs1 = P.sem("s_rs1")
        s_agh = P.sem("s_agh")
        s_agc = P.sem("s_agc")
        with ExitStack() as ctxP:
            sbP = lambda nm, s, d: ctxP.enter_context(nc.sbuf_tensor(nm, s, d))
            psP = lambda nm, s: ctxP.enter_context(nc.psum_tensor(nm, s, F32))
            yn_sb = [sbP(f"pyn{i}", [128, T], BF16) for i in range(8)]
            cwm_sb = [sbP(f"pcw{i}", [128, D], BF16) for i in range(8)]
            o2_st = [sbP(f"po2{i}", [128, 512], F32) for i in range(4)]
            pp_ps = [psP(f"pp{i}", [128, 512]) for i in range(4)]
            s_pl = P.sem("s_pl")
            s_pm = P.sem("s_pm")
            s_pv = P.sem("s_pv")
            s_ps = P.sem("s_ps")
            dma_tiles(P, "gpsimd", s_pl, ioA["yn_out"], yn_sb, 8, wait=gateA)
            dma_tiles(P, "gpsimd", s_pl, cwm, cwm_sb, 8)
            cpv = cwp.rearrange("(a p) t -> a p t", p=128)
            nblk = 0
            for dm in range(8):
                for tc in range(8):
                    bank = pp_ps[nblk % 4]
                    stg = o2_st[nblk % 4]

                    def b_pm(t, dm=dm, tc=tc, bank=bank):
                        ins = None
                        for d2 in range(8):
                            ins = t.matmul(bank[:, :],
                                           cwm_sb[d2][:, dm * 128:(dm + 1) * 128],
                                           yn_sb[d2][:, tc * 512:(tc + 1) * 512],
                                           start=(d2 == 0), stop=(d2 == 7))
                        return ins
                    wt = [(s_pl, s_pl.n)] if nblk == 0 else []
                    if nblk >= 4:
                        wt.append((s_ps, 16 * (nblk - 3)))
                    P.op("tensor", b_pm, wait=wt, inc=[(s_pm, 1)])

                    def b_pv(v, bank=bank, stg=stg):
                        return v.tensor_copy(stg[:, :], bank[:, :])
                    P.op("vector", b_pv, wait=[(s_pm, s_pm.n)], inc=[(s_pv, 1)])

                    blk = tc // 4
                    col = (tc % 4) * 512
                    wtv = (s_pv.h, s_pv.n)

                    def b_pst(sy, dm=dm, blk=blk, col=col, stg=stg, wtv=wtv):
                        sy.wait_ge(*wtv)
                        sy.dma_start(out=cpv[8 * blk + dm][:, col:col + 512],
                                     in_=stg[:, :]).then_inc(s_ps.h, 16)
                    s_ps.bump(16)
                    P.raw("sync", b_pst)
                    nblk += 1

            def b_agh(g):
                g.wait_ge(s_ps.h, s_ps.n)
                g.dma_start(out=agh_in[:, :],
                            in_=cwp[0:D, OWN - HALO:OWN]).then_inc(s_pl.h, 16)
            s_pl.bump(16)
            P.raw("gpsimd", b_agh)

            def b_cc1(g):
                g.wait_ge(s_pl.h, s_pl.n)
                g.collective_compute(
                    "ReduceScatter", ALU.add, replica_groups=PAIRS,
                    ins=[cwp.ap().opt()], outs=[rs_out.ap().opt()],
                ).then_inc(s_rs1.h, 1)
                g.collective_compute(
                    "AllGather", ALU.bypass, replica_groups=PAIRS,
                    ins=[agh_in.ap().opt()], outs=[agh_out.ap().opt()],
                ).then_inc(s_agh.h, 1)
            s_rs1.bump(1)
            s_agh.bump(1)
            P.raw("gpsimd", b_cc1)

        scr = {}
        ctxBspan = ctx.enter_context(ExitStack())
        sbS = lambda nm, s, d: ctxBspan.enter_context(nc.sbuf_tensor(nm, s, d))
        psS = lambda nm, s: ctxBspan.enter_context(nc.psum_tensor(nm, s, F32))
        scr["x_sb"] = [sbS(f"bx{i}", [128, L], F32) for i in range(8)]
        scr["sq"] = [sbS(f"bsq{i}", [128, 512], BF16) for i in range(8)]
        scr["rows"] = [sbS(f"brow{i}", [1, 512], F32) for i in range(4)]
        scr["st_ps"] = psS("bst", [64, 512])
        scr["bc_ps"] = [psS("bbc0", [128, 512]), psS("bbc1", [128, 512])]
        scr["mm_ps"] = [psS(f"bmm{i}", [128, 512]) for i in range(4)]
        x_sb = scr["x_sb"]
        with ExitStack() as ctxR:
            sbR = lambda nm, s, d: ctxR.enter_context(nc.sbuf_tensor(nm, s, d))
            xw_sb = [sbR(f"rxw{i}", [128, L], F32) for i in range(8)]
            rs_st = [sbR(f"rrs{i}", [128, 512], F32) for i in range(8)]
            hsA = sbR("rhsA", [128, 512], F32)
            hsB = sbR("rhsB", [128, 512], F32)
            hsT = sbR("rhsT", [128, 512], F32)
            s_rl = P.sem("s_rl")
            s_rs = P.sem("s_rs")
            s_hz = P.sem("s_hz")
            s_res = P.sem("s_res")
            dma_tiles(P, "sync", s_rl, xw_d, xw_sb, 8,
                      wait=[(s_rs1, 1), (s_agh, 1)])
            ahv = agh_out.rearrange("(a p) t -> a p t", p=128)

            def b_hz(v):
                v.memset(hsA[:, :], 0.0)
                return v.memset(hsB[:, :], 0.0)
            P.op("vector", b_hz, inc=[(s_hz, 1)])
            for i in range(8):
                def b_h(sy, i=i, hzn=s_hz.n, resn=s_res.n):
                    if i == 0:
                        sy.wait_ge(s_hz.h, hzn)
                    else:
                        sy.wait_ge(s_res.h, resn)
                    sy.dma_start(out=hsA[:, 0:HALO],
                                 in_=ahv[i][:, :]).then_inc(s_rl.h, 16)
                    sy.dma_start(out=hsB[:, 0:HALO],
                                 in_=ahv[8 + i][:, :]).then_inc(s_rl.h, 16)
                s_rl.bump(32)
                P.raw("sync", b_h)

                def b_hf(v, i=i):
                    v.tensor_add(hsT[:, :], hsA[:, :], hsB[:, :])
                    v.tensor_add(hsT[:, :], hsT[:, :], xw_sb[i][:, 0:512])
                    return v.tensor_scalar(x_sb[i][:, 0:HALO], hsT[:, 0:HALO],
                                           um_sb[:, 0:1], None, ALU.mult)
                P.op("vector", b_hf, wait=[(s_rl, s_rl.n), (s_um, 16)],
                     inc=[(s_res, 1)])

            rsv = rs_out.rearrange("(a p) t -> a p t", p=128)
            for k in range(4):
                def b_rld(sy, k=k, resn=s_res.n):
                    if k > 0:
                        sy.wait_ge(s_res.h, resn)
                    for i in range(8):
                        sy.dma_start(out=rs_st[i][:, :],
                                     in_=rsv[i][:, k * 512:(k + 1) * 512]
                                     ).then_inc(s_rs.h, 16)
                s_rs.bump(16 * 8)
                P.raw("sync", b_rld)
                for i in range(8):
                    def b_ra(v, i=i, k=k):
                        return v.tensor_add(
                            x_sb[i][:, HALO + k * 512:HALO + (k + 1) * 512],
                            xw_sb[i][:, HALO + k * 512:HALO + (k + 1) * 512],
                            rs_st[i][:, :])
                    P.op("vector", b_ra, wait=[(s_rs, s_rs.n)],
                         inc=[(s_res, 1)])
            res_done = (s_res, s_res.n)
            P.op("gpsimd", lambda g: None, wait=[res_done])

        gB = emit_ln2_ffn0(nc, ctx, P, C, scr,
                           dict(wr=wr, wk=wk, wv=wv, mur=mur, muk=muk,
                                xp_out=xp_d), res_done)
        ctxBspan.close()
        gateB = [(gB[k], gB[k].n) for k in ("s_sto", "s_out", "s_mmf", "s_app",
                                            "s_r", "s_k", "s_ka")]

        with ExitStack() as ctxC:
            gC = emit_C(nc, ctxC, P, C, ioC, pre=gateB)
        gateC = [(gC[k], gC[k].n) for k in ("s_sto", "s_yev", "s_ymm", "s_u",
                                            "s_xc", "s_scan", "s_cy")]

        def b_cc2(g):
            for h, v in [(s.h, n) for s, n in gateC]:
                g.wait_ge(h, v)
            if FDBG:
                g.dma_start(out=cdb[:, :],
                            in_=ioC["carry_out"][:, :]).then_inc(s_agc.h, 16)
            g.collective_compute(
                "AllGather", ALU.bypass, replica_groups=PAIRS,
                ins=[ioC["carry_out"].ap().opt()], outs=[cy_out.ap().opt()],
            ).then_inc(s_agc.h, 1)
        s_agc.bump(1)
        P.raw("gpsimd", b_cc2)

        def carry_fold(v, cb_sb):
            v.tensor_scalar(cb_sb[:, :], cb_sb[:, :], um_sb[0:16, 0:1], None,
                            ALU.mult)
            v.tensor_copy(scr_settle[:, :], scr_settle[:, :])

        scr_settle = sb("settle", [128, 512], F32)

        def b_dsgate(sy):
            for h, v in [(s.h, n) for s, n in gateC]:
                sy.wait_ge(h, v)
        P.raw("sync", b_dsgate)

        with ExitStack() as ctxD:
            gD = emit_D(nc, ctxD, P, C, ioD, pre=gateC,
                        carry_fold=carry_fold, carry_wait=[(s_agc, 1)])
        P.play()
    return nc



_CACHE = {}


def _built(name, fn):
    if name not in _CACHE:
        _CACHE[name] = fn()
    return _CACHE[name]


def _tri_const():
    return np.ascontiguousarray(
        np.tril(np.ones((128, 128), np.float32)).T).astype(BF)


def _decay_const():
    d = (0.9 ** (np.arange(OWN, dtype=np.float64) + 1)).astype(np.float32)
    return np.ascontiguousarray(np.broadcast_to(d, (128, OWN)))


def kernel(**inputs):
    if os.environ.get("K_UNFUSED"):
        return kernel_unfused(**inputs)
    return kernel_fused(**inputs)


def kernel_fused(L0_attn_x, L0_attn_q1_w, L0_attn_q2_w, L0_attn_k1_w,
                 L0_attn_k2_w, L0_attn_v_w, L0_attn_c_w, L0_attn_lambda_q1,
                 L0_attn_lambda_k1, L0_attn_lambda_q2, L0_attn_lambda_k2,
                 L0_ffn_W_r, L0_ffn_W_k, L0_ffn_W_v, L0_ffn_mu_r, L0_ffn_mu_k,
                 L1_attn_w_in, L1_attn_conv_w, L1_attn_conv_b, L1_attn_w_B,
                 L1_attn_w_C, L1_attn_w_out, L1_ffn_theta, L1_ffn_phi,
                 L1_ffn_theta_noise, L1_ffn_phi_noise):
    f32c = lambda a: np.ascontiguousarray(np.asarray(a), dtype=np.float32)
    xT = [f32c(np.asarray(L0_attn_x[b]).T) for b in range(4)]
    q1w, q2w = f32c(L0_attn_q1_w), f32c(L0_attn_q2_w)
    k1w, k2w = f32c(L0_attn_k1_w), f32c(L0_attn_k2_w)
    vw_full = f32c(L0_attn_v_w)
    lq1, lk1 = f32c(L0_attn_lambda_q1), f32c(L0_attn_lambda_k1)
    lq2, lk2 = f32c(L0_attn_lambda_q2), f32c(L0_attn_lambda_k2)
    cw_full = f32c(L0_attn_c_w)
    tri = _tri_const()
    wrT = f32c(np.asarray(L0_ffn_W_r).T)
    wkT = f32c(np.asarray(L0_ffn_W_k).T)
    wvT = f32c(np.asarray(L0_ffn_W_v).T)
    mur = f32c(np.asarray(L0_ffn_mu_r).reshape(8, 128).T)
    muk = f32c(np.asarray(L0_ffn_mu_k).reshape(8, 128).T)
    cwf = np.asarray(L1_attn_conv_w, np.float32).reshape(2048, 4)
    convw = np.zeros((128, 64), np.float32)
    convb = np.zeros((128, 16), np.float32)
    for pt in range(16):
        convw[:, pt * 4:pt * 4 + 4] = cwf[pt * 128:(pt + 1) * 128]
        convb[:, pt] = np.asarray(L1_attn_conv_b,
                                  np.float32)[pt * 128:(pt + 1) * 128]
    win = f32c(L1_attn_w_in)
    wB = f32c(L1_attn_w_B)
    wC = f32c(L1_attn_w_C)
    wout = f32c(L1_attn_w_out)
    decay = _decay_const()
    th, thn = f32c(L1_ffn_theta), f32c(L1_ffn_theta_noise)
    ph, phn = f32c(L1_ffn_phi), f32c(L1_ffn_phi_noise)
    zx = np.zeros((D, HALO), np.float32)

    in_maps = []
    for c in range(NCORES):
        b, u = c // 2, c % 2
        q12 = np.zeros((D, 1024), np.float32)
        k12 = np.zeros((D, 1024), np.float32)
        for hh in range(8):
            g = 8 * u + hh
            q12[:, hh * 128:hh * 128 + 64] = q1w[:, 64 * g:64 * g + 64]
            q12[:, hh * 128 + 64:hh * 128 + 128] = q2w[:, 64 * g:64 * g + 64]
            k12[:, hh * 128:hh * 128 + 64] = k1w[:, 64 * g:64 * g + 64]
            k12[:, hh * 128 + 64:hh * 128 + 128] = k2w[:, 64 * g:64 * g + 64]
        lamv = np.concatenate(
            [lq1[8 * u:8 * u + 8].reshape(-1), lk1[8 * u:8 * u + 8].reshape(-1),
             lq2[8 * u:8 * u + 8].reshape(-1), lk2[8 * u:8 * u + 8].reshape(-1)]
        )[None, :].astype(np.float32)
        if u == 0:
            xw = np.concatenate([zx, xT[b][:, 0:OWN]], axis=1)
        else:
            xw = xT[b][:, OWN - HALO:T]
        in_maps.append(dict(
            x=xT[b], q12w=q12, k12w=k12,
            vw=np.ascontiguousarray(vw_full[:, 1024 * u:1024 * (u + 1)]),
            lamv=lamv, tri=tri,
            cwm=np.ascontiguousarray(cw_full[1024 * u:1024 * (u + 1), :]),
            xw=np.ascontiguousarray(xw),
            umask=np.full((128, 1), float(u), np.float32),
            wr=wrT, wk=wkT, wv=wvT, mur=mur, muk=muk,
            win=win, convw=convw, convb=convb, wB=wB, wC=wC,
            wout=wout, wc2=wC, decay=decay, th=th, thn=thn, ph=ph, phn=phn))
    res = run_bass_kernel_spmd(_built("FUSED", build_fused), in_maps,
                               core_ids=list(range(NCORES)))
    out = np.empty((4, T, D), np.float32)
    for c in range(NCORES):
        b, u = c // 2, c % 2
        out[b, u * OWN:(u + 1) * OWN, :] = \
            np.asarray(res.results[c]["out"], np.float32).T
    return out


def kernel_unfused(L0_attn_x, L0_attn_q1_w, L0_attn_q2_w, L0_attn_k1_w, L0_attn_k2_w,
           L0_attn_v_w, L0_attn_c_w, L0_attn_lambda_q1, L0_attn_lambda_k1,
           L0_attn_lambda_q2, L0_attn_lambda_k2, L0_ffn_W_r, L0_ffn_W_k,
           L0_ffn_W_v, L0_ffn_mu_r, L0_ffn_mu_k, L1_attn_w_in, L1_attn_conv_w,
           L1_attn_conv_b, L1_attn_w_B, L1_attn_w_C, L1_attn_w_out,
           L1_ffn_theta, L1_ffn_phi, L1_ffn_theta_noise, L1_ffn_phi_noise):
    f32c = lambda a: np.ascontiguousarray(np.asarray(a), dtype=np.float32)
    xT = [f32c(np.asarray(L0_attn_x[b]).T) for b in range(4)]
    q1w, q2w = f32c(L0_attn_q1_w), f32c(L0_attn_q2_w)
    k1w, k2w = f32c(L0_attn_k1_w), f32c(L0_attn_k2_w)
    vw_full = f32c(L0_attn_v_w)
    lq1, lk1 = f32c(L0_attn_lambda_q1), f32c(L0_attn_lambda_k1)
    lq2, lk2 = f32c(L0_attn_lambda_q2), f32c(L0_attn_lambda_k2)
    tri = _tri_const()

    in_A = []
    for c in range(NCORES):
        b, u = c // 2, c % 2
        q12 = np.zeros((D, 1024), np.float32)
        k12 = np.zeros((D, 1024), np.float32)
        for hh in range(8):
            g = 8 * u + hh
            q12[:, hh * 128:hh * 128 + 64] = q1w[:, 64 * g:64 * g + 64]
            q12[:, hh * 128 + 64:hh * 128 + 128] = q2w[:, 64 * g:64 * g + 64]
            k12[:, hh * 128:hh * 128 + 64] = k1w[:, 64 * g:64 * g + 64]
            k12[:, hh * 128 + 64:hh * 128 + 128] = k2w[:, 64 * g:64 * g + 64]
        lamv = np.concatenate(
            [lq1[8 * u:8 * u + 8].reshape(-1), lk1[8 * u:8 * u + 8].reshape(-1),
             lq2[8 * u:8 * u + 8].reshape(-1), lk2[8 * u:8 * u + 8].reshape(-1)]
        )[None, :].astype(np.float32)
        in_A.append(dict(x=xT[b], q12w=q12, k12w=k12,
                         vw=np.ascontiguousarray(vw_full[:, 1024 * u:1024 * (u + 1)]),
                         lamv=lamv, tri=tri))
    res_A = run_bass_kernel_spmd(_built("A", build_A), in_A,
                                 core_ids=list(range(NCORES)))
    yn = [np.asarray(res_A.results[c]["ynorm"]) for c in range(NCORES)]

    wrT = f32c(np.asarray(L0_ffn_W_r).T)
    wkT = f32c(np.asarray(L0_ffn_W_k).T)
    wvT = f32c(np.asarray(L0_ffn_W_v).T)
    mur = f32c(np.asarray(L0_ffn_mu_r).reshape(8, 128).T)
    muk = f32c(np.asarray(L0_ffn_mu_k).reshape(8, 128).T)
    cw = f32c(L0_attn_c_w)
    zb = np.zeros((2048, HALO), BF)
    zx = np.zeros((D, HALO), np.float32)
    in_B = []
    for c in range(NCORES):
        b, u = c // 2, c % 2
        ynf = np.concatenate([yn[2 * b], yn[2 * b + 1]], axis=0)
        if u == 0:
            ynw = np.concatenate([zb, ynf[:, 0:OWN]], axis=1)
            xw = np.concatenate([zx, xT[b][:, 0:OWN]], axis=1)
        else:
            ynw = ynf[:, OWN - HALO:T]
            xw = xT[b][:, OWN - HALO:T]
        in_B.append(dict(ynf=np.ascontiguousarray(ynw),
                         x=np.ascontiguousarray(xw), cw=cw, wr=wrT, wk=wkT,
                         wv=wvT, mur=mur, muk=muk))
    res_B = run_bass_kernel_spmd(_built("B", build_B), in_B,
                                 core_ids=list(range(NCORES)))
    xp = [np.asarray(res_B.results[c]["xp"]) for c in range(NCORES)]

    cwf = np.asarray(L1_attn_conv_w, np.float32).reshape(2048, 4)
    convw = np.zeros((128, 64), np.float32)
    convb = np.zeros((128, 16), np.float32)
    for pt in range(16):
        convw[:, pt * 4:pt * 4 + 4] = cwf[pt * 128:(pt + 1) * 128]
        convb[:, pt] = np.asarray(L1_attn_conv_b, np.float32)[pt * 128:(pt + 1) * 128]
    win = f32c(L1_attn_w_in)
    wB = f32c(L1_attn_w_B)
    wC = f32c(L1_attn_w_C)
    in_C = [dict(x=xp[c], win=win, convw=convw, convb=convb, wB=wB, wC=wC)
            for c in range(NCORES)]
    res_C = run_bass_kernel_spmd(_built("C", build_C), in_C,
                                 core_ids=list(range(NCORES)))

    wout = f32c(L1_attn_w_out)
    decay = _decay_const()
    th, thn = f32c(L1_ffn_theta), f32c(L1_ffn_theta_noise)
    ph, phn = f32c(L1_ffn_phi), f32c(L1_ffn_phi_noise)
    zc = np.zeros((16, 1), np.float32)
    in_D = []
    for c in range(NCORES):
        b, u = c // 2, c % 2
        carry = np.asarray(res_C.results[2 * b]["carry"], np.float32) if u == 1 else zc
        in_D.append(dict(y0=np.asarray(res_C.results[c]["y0"]),
                         sz=np.asarray(res_C.results[c]["sz"]),
                         carry=carry,
                         x=np.ascontiguousarray(xp[c][:, HALO:]),
                         wout=wout, wc2=wC, decay=decay, th=th, thn=thn,
                         ph=ph, phn=phn))
    res_D = run_bass_kernel_spmd(_built("D", build_D), in_D,
                                 core_ids=list(range(NCORES)))

    out = np.empty((4, T, D), np.float32)
    for c in range(NCORES):
        b, u = c // 2, c % 2
        out[b, u * OWN:(u + 1) * OWN, :] = \
            np.asarray(res_D.results[c]["out"], np.float32).T
    return out
